# version 02
# baseline (speedup 1.0000x reference)
import contextlib
import numpy as np
import ml_dtypes
import concourse.bass as bass
import concourse.mybir as mybir
from concourse.bass_utils import run_bass_kernel_spmd

F32 = mybir.dt.float32
BF16 = mybir.dt.bfloat16
I32 = mybir.dt.int32
AF = mybir.ActivationFunctionType
ALU = mybir.AluOpType

B, S, D = 2, 4096, 2048
HQ, HKV, HD, WIN = 16, 4, 128, 512
NCORES = 8
T = 1024
HALO = 512
TK = T + HALO
KC = D // 128
OC = D // 128
IC = 8192 // 128
NQT = T // 128
NKT = TK // 128
EPS = 1e-5
NEG = -1.0e30


def build_nc(debug=False, upto=99, dumps=None):
    if dumps is None:
        dumps = {"qkv", "attn", "y1", "h", "r2"} if debug else set()
    nc = bass.Bass("TRN2", target_bir_lowering=False)

    x_own = nc.dram_tensor("x_own", [128, KC * T], F32, kind="ExternalInput")
    x_all = nc.dram_tensor("x_all", [128, KC * TK], F32, kind="ExternalInput")
    masks = nc.dram_tensor("masks", [128, NQT * 640], BF16, kind="ExternalInput")
    wq_os = nc.dram_tensor("wq_os", [OC, 128, 2048], F32, kind="ExternalInput")
    wk_os = nc.dram_tensor("wk_os", [HKV, 128, 2048], F32, kind="ExternalInput")
    wv_ks = nc.dram_tensor("wv_ks", [KC, 128, 512], F32, kind="ExternalInput")
    wo_os = nc.dram_tensor("wo_os", [OC, 128, 2048], F32, kind="ExternalInput")
    wg_is = nc.dram_tensor("wg_is", [IC, 128, 2048], I32, kind="ExternalInput")
    wu_is = nc.dram_tensor("wu_is", [IC, 128, 2048], I32, kind="ExternalInput")
    wd_os = nc.dram_tensor("wd_os", [2, OC, 128, 4096], I32, kind="ExternalInput")
    scales = nc.dram_tensor("scales", [128, 144], F32, kind="ExternalInput")
    d01_h = nc.dram_tensor("d01", [128, 128], BF16, kind="ExternalInput")
    ones_c = nc.dram_tensor("ones_c", [128, 1], F32, kind="ExternalInput")
    ones_r = nc.dram_tensor("ones_r", [1, 128], F32, kind="ExternalInput")
    y_out = nc.dram_tensor("y", [128, KC * T], F32, kind="ExternalOutput")

    dbg = {}
    if debug:
        dbg["q"] = nc.dram_tensor("dbg_q", [128, HQ * T], F32, kind="ExternalOutput")
        dbg["k"] = nc.dram_tensor("dbg_k", [128, HKV * TK], F32, kind="ExternalOutput")
        dbg["vT"] = nc.dram_tensor("dbg_vT", [128, NKT * 512], F32, kind="ExternalOutput")
        dbg["attn"] = nc.dram_tensor("dbg_attn", [128, KC * T], F32, kind="ExternalOutput")
        dbg["y1"] = nc.dram_tensor("dbg_y1", [128, KC * T], F32, kind="ExternalOutput")
        dbg["h"] = nc.dram_tensor("dbg_h", [128, 32 * T], F32, kind="ExternalOutput")
        dbg["r2"] = nc.dram_tensor("dbg_r2", [128, KC * T], F32, kind="ExternalOutput")

    with contextlib.ExitStack() as ctx:
        S_sb = ctx.enter_context(nc.sbuf_tensor("S_sb", [128, KC * T], F32))
        ab = ctx.enter_context(nc.sbuf_tensor("ab", [128, 65536], BF16))
        sq_sb = ctx.enter_context(nc.sbuf_tensor("sq_sb", [128, 1024], F32))
        st_sb = ctx.enter_context(nc.sbuf_tensor("st_sb", [1, 1536], F32))
        scl_sb = ctx.enter_context(nc.sbuf_tensor("scl_sb", [128, 144], F32))
        d01_sb = ctx.enter_context(nc.sbuf_tensor("d01_sb", [128, 128], BF16))
        onc_sb = ctx.enter_context(nc.sbuf_tensor("onc_sb", [128, 1], F32))
        onr_sb = ctx.enter_context(nc.sbuf_tensor("onr_sb", [1, 128], F32))
        sums_sb = ctx.enter_context(nc.sbuf_tensor("sums_sb", [128, 4], F32))
        Dm_sb = ctx.enter_context(nc.sbuf_tensor("Dm_sb", [128, 256], BF16))

        def abv(off, n):
            return ab[:, off:off + n]

        x_bf_o, q_bf_o, k_bf_o, vT_o, wvres_o, wbuf_o = 0, 24576, 40960, 47104, 53248, 61440
        attn_o, probs_o, pT_o, mask_o = 0, 16384, 17664, 18944
        h_o, y1_o, wgu_o, wd_o = 0, 32768, 49152, 57344

        sg = ctx.enter_context(nc.semaphore("sg"))
        sq = ctx.enter_context(nc.semaphore("sq"))
        tm = ctx.enter_context(nc.semaphore("tm"))
        vv = ctx.enter_context(nc.semaphore("vv"))
        ss = ctx.enter_context(nc.semaphore("ss"))
        so = ctx.enter_context(nc.semaphore("so"))
        we = ctx.enter_context(nc.semaphore("we"))
        wo = ctx.enter_context(nc.semaphore("wo"))
        sems = {"sg": sg, "sq": sq, "tm": tm, "vv": vv, "ss": ss, "so": so,
                "we": we, "wo": wo}
        cnt = {k: 0 for k in sems}

        def inc(instr, name, step=1):
            cnt[name] += step
            instr.then_inc(sems[name], step)
            return cnt[name]

        block = ctx.enter_context(nc.Block())

        _bar_prev = [{k: 0 for k in sems}]

        def barrier():
            snap = {k: cnt[k] for k in sems}
            prev = _bar_prev[0]
            own_of = {"tm": block.tensor, "vv": block.vector, "ss": block.scalar}
            for own, engfn in (("tm", block.tensor), ("vv", block.vector),
                               ("ss", block.scalar), (None, block.gpsimd),
                               (None, block.sync)):
                def _f(e, snap=snap, own=own, prev=dict(prev)):
                    for k, v in snap.items():
                        if v > 0 and k != own and v > prev[k]:
                            e.wait_ge(sems[k], v)
                engfn(_f)
            _bar_prev[0] = snap


        def finish_prefix():
            @block.sync
            def _(sy):
                sy.wait_ge(so, cnt["so"])
        @block.sync
        def _(sy):
            inc(sy.dma_start(S_sb[:, :], x_own[:, :]), "sq", 16)
            inc(sy.dma_start(scl_sb[:, :], scales[:, :]), "sq", 16)
            inc(sy.dma_start(d01_sb[:, :], d01_h[:, :]), "sq", 16)
            inc(sy.dma_start(onc_sb[:, :], ones_c[:, :]), "sq", 16)
            inc(sy.dma_start(onr_sb[:1, :], ones_r[:1, :]), "sq", 16)
        p0_sq = cnt["sq"]

        @block.gpsimd
        def _(g):
            inc(g.dma_start(abv(x_bf_o, KC * TK), x_all[:, :]), "sg", 16)
            for k in range(KC):
                inc(g.dma_start(abv(wvres_o + 512 * k, 512), wv_ks[k, :, :]), "sg", 16)
        wv_ld = cnt["sg"]

        with nc.psum_tensor("psA", [128, 512], F32) as psA, \
             nc.psum_tensor("psB", [128, 512], F32) as psB, \
             nc.psum_tensor("psQ0", [128, 1024], F32) as psQ0, \
             nc.psum_tensor("psQ1", [128, 1024], F32) as psQ1:
            psV = [psA, psB]
            psQ = [psQ0, psQ1]

            vt_mm = [None] * NKT
            vt_ev = [None] * NKT
            for j in range(NKT):
                @block.tensor
                def _(t, j=j):
                    if j == 0:
                        t.wait_ge(sg, wv_ld)
                    if j >= 2:
                        t.wait_ge(vv, vt_ev[j - 2])
                    ps = psV[j % 2]
                    last = None
                    for k in range(KC):
                        last = t.matmul(ps[:, :], abv(x_bf_o + k * TK + j * 128, 128),
                                        abv(wvres_o + 512 * k, 512),
                                        start=(k == 0), stop=(k == KC - 1))
                    vt_mm[j] = inc(last, "tm")

                @block.vector
                def _(v, j=j):
                    v.wait_ge(tm, vt_mm[j])
                    vt_ev[j] = inc(v.tensor_copy(abv(vT_o + 512 * j, 512), psV[j % 2][:, :]), "vv")

            def proj(noc, wsrc, rhs_off, rhs_stride, rhs_base, ntt, out_kind, out_off):
                mm = [None] * noc
                ev = [None] * noc
                tm0 = cnt["tm"]
                for o in range(noc):
                    @block.gpsimd
                    def _(g, o=o, tm0=tm0):
                        if o >= 2:
                            g.wait_ge(tm, mm[o - 2])
                        else:
                            g.wait_ge(tm, tm0)
                        inc(g.dma_start(abv(wbuf_o + 2048 * (o % 2), 2048), wsrc[o, :, :]),
                            "we" if o % 2 == 0 else "wo", 16)
                    ld = cnt["we" if o % 2 == 0 else "wo"]

                    @block.tensor
                    def _(t, o=o, ld=ld):
                        t.wait_ge(we if o % 2 == 0 else wo, ld)
                        if o >= 2:
                            t.wait_ge(vv, ev[o - 2])
                        last = None
                        if ntt == 2:
                            ps = psQ[o % 2]
                            for tt in range(2):
                                for k in range(KC):
                                    last = t.matmul(ps[:, 512 * tt:512 * tt + 512],
                                                    abv(wbuf_o + 2048 * (o % 2) + 128 * k, 128),
                                                    abv(rhs_base + k * rhs_stride + rhs_off + 512 * tt, 512),
                                                    start=(k == 0), stop=(k == KC - 1))
                        else:
                            for tt in range(3):
                                ps = psQ[o % 2] if tt < 2 else psV[o % 2]
                                sl = ps[:, 512 * tt:512 * tt + 512] if tt < 2 else ps[:, :]
                                for k in range(KC):
                                    last = t.matmul(sl,
                                                    abv(wbuf_o + 2048 * (o % 2) + 128 * k, 128),
                                                    abv(rhs_base + k * rhs_stride + rhs_off + 512 * tt, 512),
                                                    start=(k == 0), stop=(k == KC - 1))
                        mm[o] = inc(last, "tm")

                    @block.vector
                    def _(v, o=o):
                        v.wait_ge(tm, mm[o])
                        if out_kind == "copy":
                            if ntt == 2:
                                ev[o] = inc(v.tensor_copy(abv(out_off + 1024 * o, 1024),
                                                          psQ[o % 2][:, :]), "vv")
                            else:
                                v.tensor_copy(abv(out_off + 1536 * o, 1024), psQ[o % 2][:, :])
                                ev[o] = inc(v.tensor_copy(abv(out_off + 1536 * o + 1024, 512),
                                                          psV[o % 2][:, :]), "vv")
                        else:
                            ev[o] = inc(v.tensor_tensor(S_sb[:, 1024 * o:1024 * o + 1024],
                                                        S_sb[:, 1024 * o:1024 * o + 1024],
                                                        psQ[o % 2][:, :], ALU.add), "vv")
                return ev

            proj(OC, wq_os, HALO, TK, x_bf_o, 2, "copy", q_bf_o)
            vt_guard = cnt["vv"]

            @block.tensor
            def _(t):
                t.wait_ge(vv, vt_guard)
            proj(HKV, wk_os, 0, TK, x_bf_o, 3, "copy", k_bf_o)

            if debug and "qkv" in dumps:
                @block.gpsimd
                def _(g):
                    g.wait_ge(vv, cnt["vv"])
                    inc(g.dma_start(dbg["q"][:, :], abv(q_bf_o, HQ * 1024)), "so", 16)
                    inc(g.dma_start(dbg["k"][:, :], abv(k_bf_o, HKV * 1536)), "so", 16)
                    inc(g.dma_start(dbg["vT"][:, :], abv(vT_o, NKT * 512)), "so", 16)

            barrier()

        if upto <= 1:
            finish_prefix()
            return nc

        @block.sync
        def _(sy):
            inc(sy.dma_start(abv(mask_o, 8 * 640), masks[:, :]), "sq", 16)
        mask_ld = cnt["sq"]

        with nc.psum_tensor("sc0", [128, 640], F32) as sc0, \
             nc.psum_tensor("sc1", [128, 640], F32) as sc1, \
             nc.psum_tensor("pT", [128, 640], F32) as pTp, \
             nc.psum_tensor("pv0", [128, 128], F32) as pv0, \
             nc.psum_tensor("pv1", [128, 128], F32) as pv1:
            scs = [sc0, sc1]
            pvs = [pv0, pv1]
            items = [(h, qt) for h in range(HQ) for qt in range(NQT)]
            NIT = len(items)
            c_mm = [None] * NIT
            c_msk = [None] * NIT
            c_exp = [None] * NIT
            c_rec = [None] * NIT
            c_D = [None] * NIT
            c_tD = [None] * NIT
            c_pTe = [None] * NIT
            c_pv = [None] * NIT
            c_ae = [None] * NIT

            for n in range(NIT + 2):
                @block.tensor
                def _(t, n=n):
                    if n < NIT:
                        h, qt = items[n]
                        if n >= 2:
                            t.wait_ge(ss, c_exp[n - 2])
                        qs = q_bf_o + 1024 * h + 128 * qt
                        ks = k_bf_o + 1536 * (h // 4) + 128 * qt
                        sc = scs[n % 2]
                        t.matmul(sc[:, 0:512], abv(qs, 128), abv(ks, 512), start=True, stop=True)
                        c_mm[n] = inc(t.matmul(sc[:, 512:640], abv(qs, 128), abv(ks + 512, 128),
                                               start=True, stop=True), "tm")
                    if 0 <= n - 1 < NIT:
                        m = n - 1
                        t.wait_ge(ss, c_D[m])
                        if m >= 1:
                            t.wait_ge(ss, c_pTe[m - 1])
                        last = None
                        for b in range(5):
                            last = t.matmul(pTp[:, 128 * b:128 * b + 128],
                                            abv(probs_o + 640 * (m % 2) + 128 * b, 128),
                                            Dm_sb[:, 128 * (m % 2):128 * (m % 2) + 128],
                                            start=True, stop=True)
                        c_tD[m] = inc(last, "tm")
                    if 0 <= n - 2 < NIT:
                        m = n - 2
                        t.wait_ge(ss, c_pTe[m])
                        if m >= 2:
                            t.wait_ge(vv, c_ae[m - 2])
                        mh, mqt = items[m]
                        last = None
                        for b in range(5):
                            last = t.matmul(pvs[m % 2][:, :],
                                            abv(vT_o + 512 * (mqt + b) + 128 * (mh // 4), 128),
                                            abv(pT_o + 640 * (m % 2) + 128 * b, 128),
                                            start=(b == 0), stop=(b == 4))
                        c_pv[m] = inc(last, "tm")

                @block.vector
                def _(v, n=n):
                    if n < NIT:
                        h, qt = items[n]
                        if n == 0:
                            v.wait_ge(sq, mask_ld)
                        v.wait_ge(tm, c_mm[n])
                        c_msk[n] = inc(v.tensor_tensor(scs[n % 2][:, :], scs[n % 2][:, :],
                                                       abv(mask_o + 640 * qt, 640), ALU.add), "vv")
                    if 0 <= n - 2 < NIT:
                        m = n - 2
                        mh, mqt = items[m]
                        v.wait_ge(tm, c_pv[m])
                        c_ae[m] = inc(v.tensor_copy(abv(attn_o + 1024 * mh + 128 * mqt, 128),
                                                    pvs[m % 2][:, :]), "vv")

                @block.scalar
                def _(s, n=n):
                    if n < NIT:
                        if n == 0:
                            s.wait_ge(sq, p0_sq)
                        s.wait_ge(vv, c_msk[n])
                        if n >= 2:
                            s.wait_ge(tm, c_tD[n - 2])
                        c_exp[n] = inc(
                            s.activation(abv(probs_o + 640 * (n % 2), 640), scs[n % 2][:, :], AF.Exp,
                                         bias=0.0, scale=1.0 / 128.0,
                                         accum_out=sums_sb[:, 2 * (n % 2):2 * (n % 2) + 1]), "ss")
                    if 0 <= n - 1 < NIT:
                        m = n - 1
                        s.wait_ge(tm, c_tD[m])
                        c_pTe[m] = inc(s.copy(abv(pT_o + 640 * (m % 2), 640), pTp[:, :]), "ss")

                if n < NIT:
                    @block.vector
                    def _(v, n=n):
                        v.wait_ge(ss, c_exp[n])
                        c_rec[n] = inc(v.reciprocal(sums_sb[:, 2 * (n % 2) + 1:2 * (n % 2) + 2],
                                                    sums_sb[:, 2 * (n % 2):2 * (n % 2) + 1]), "vv")

                    @block.scalar
                    def _(s, n=n):
                        s.wait_ge(vv, c_rec[n])
                        c_D[n] = inc(s.mul(Dm_sb[:, 128 * (n % 2):128 * (n % 2) + 128], d01_sb[:, :],
                                           sums_sb[:, 2 * (n % 2) + 1:2 * (n % 2) + 2]), "ss")

            if debug and "attn" in dumps:
                @block.gpsimd
                def _(g):
                    g.wait_ge(vv, cnt["vv"])
                    inc(g.dma_start(dbg["attn"][:, :], abv(attn_o, KC * T)), "so", 16)

            barrier()

        if upto <= 2:
            finish_prefix()
            return nc

        with nc.psum_tensor("psW0", [128, 1024], F32) as psW0, \
             nc.psum_tensor("psW1", [128, 1024], F32) as psW1:
            psQ = [psW0, psW1]

            def proj_wo():
                mm = [None] * OC
                ev = [None] * OC
                tm0 = cnt["tm"]
                for o in range(OC):
                    @block.gpsimd
                    def _(g, o=o, tm0=tm0):
                        if o >= 2:
                            g.wait_ge(tm, mm[o - 2])
                        else:
                            g.wait_ge(tm, tm0)
                        inc(g.dma_start(abv(wbuf_o + 2048 * (o % 2), 2048), wo_os[o, :, :]),
                            "we" if o % 2 == 0 else "wo", 16)
                    ld = cnt["we" if o % 2 == 0 else "wo"]

                    @block.tensor
                    def _(t, o=o, ld=ld):
                        t.wait_ge(we if o % 2 == 0 else wo, ld)
                        if o >= 2:
                            t.wait_ge(vv, ev[o - 2])
                        last = None
                        for tt in range(2):
                            for k in range(KC):
                                last = t.matmul(psQ[o % 2][:, 512 * tt:512 * tt + 512],
                                                abv(wbuf_o + 2048 * (o % 2) + 128 * k, 128),
                                                abv(attn_o + 1024 * k + 512 * tt, 512),
                                                start=(k == 0), stop=(k == KC - 1))
                        mm[o] = inc(last, "tm")

                    @block.vector
                    def _(v, o=o):
                        v.wait_ge(tm, mm[o])
                        ev[o] = inc(v.tensor_tensor(S_sb[:, 1024 * o:1024 * o + 1024],
                                                    S_sb[:, 1024 * o:1024 * o + 1024],
                                                    psQ[o % 2][:, :], ALU.add), "vv")
            proj_wo()
            barrier()

        if upto <= 3:
            finish_prefix()
            return nc

        _ln_id = [0]

        def emit_ln(write_bf16_to=None):
            _ln_id[0] += 1
            li = _ln_id[0]
            with nc.psum_tensor(f"psS1_{li}", [1, 512], F32) as psS1, \
                 nc.psum_tensor(f"psS2_{li}", [1, 512], F32) as psS2, \
                 nc.psum_tensor(f"psB1_{li}", [128, 512], F32) as psB1, \
                 nc.psum_tensor(f"psB2_{li}", [128, 512], F32) as psB2:
                for tt in range(2):
                    @block.tensor
                    def _(t, tt=tt):
                        t.wait_ge(sq, p0_sq)
                        last = None
                        for k in range(KC):
                            last = t.matmul(psS1[:1, :], onc_sb[:, :1],
                                            S_sb[:, 1024 * k + 512 * tt:1024 * k + 512 * tt + 512],
                                            start=(k == 0), stop=(k == KC - 1))
                        inc(last, "tm")
                    c_sq = [None] * KC
                    c_smm = [None] * KC
                    for k in range(KC):
                        @block.scalar
                        def _(s, k=k, tt=tt):
                            if k >= 2:
                                s.wait_ge(tm, c_smm[k - 2])
                            c_sq[k] = inc(
                                s.activation(sq_sb[:, 512 * (k % 2):512 * (k % 2) + 512],
                                             S_sb[:, 1024 * k + 512 * tt:1024 * k + 512 * tt + 512],
                                             AF.Square), "ss")

                        @block.tensor
                        def _(t, k=k):
                            t.wait_ge(ss, c_sq[k])
                            c_smm[k] = inc(
                                t.matmul(psS2[:1, :], onc_sb[:, :1],
                                         sq_sb[:, 512 * (k % 2):512 * (k % 2) + 512],
                                         start=(k == 0), stop=(k == KC - 1)), "tm")
                    tm_sums = cnt["tm"]

                    @block.vector
                    def _(v):
                        v.wait_ge(tm, tm_sums)
                        inc(v.tensor_scalar(st_sb[:1, 0:512], psS1[:1, :], 1.0 / D, None, ALU.mult), "vv")
                    c1 = cnt["vv"]

                    @block.scalar
                    def _(s):
                        s.wait_ge(vv, c1)
                        inc(s.activation(st_sb[:1, 1024:1536], st_sb[:1, 0:512], AF.Square), "ss")
                    c2 = cnt["ss"]

                    @block.vector
                    def _(v):
                        v.wait_ge(ss, c2)
                        inc(v.scalar_tensor_tensor(st_sb[:1, 512:1024], psS2[:1, :], 1.0 / D,
                                                   st_sb[:1, 1024:1536], ALU.mult, ALU.subtract), "vv")
                    c3 = cnt["vv"]

                    @block.scalar
                    def _(s):
                        s.wait_ge(vv, c3)
                        inc(s.activation(st_sb[:1, 512:1024], st_sb[:1, 512:1024],
                                         AF.Copy, bias=EPS), "ss")
                    c4 = cnt["ss"]

                    @block.vector
                    def _(v):
                        v.wait_ge(ss, c4)
                        inc(v.reciprocal(st_sb[:1, 512:1024], st_sb[:1, 512:1024]), "vv")
                    vv_rec = cnt["vv"]

                    @block.scalar
                    def _(s):
                        s.wait_ge(vv, vv_rec)
                        inc(s.activation(st_sb[:1, 512:1024], st_sb[:1, 512:1024], AF.Sqrt), "ss")
                    ss_rstd = cnt["ss"]

                    @block.vector
                    def _(v):
                        v.wait_ge(ss, ss_rstd)
                        inc(v.tensor_tensor(st_sb[:1, 1024:1536], st_sb[:1, 0:512],
                                            st_sb[:1, 512:1024], ALU.mult), "vv")
                    vv_mr = cnt["vv"]

                    @block.tensor
                    def _(t):
                        t.wait_ge(vv, vv_mr)
                        t.matmul(psB1[:, :], onr_sb[:1, :], st_sb[:1, 512:1024], start=True, stop=True)
                        inc(t.matmul(psB2[:, :], onr_sb[:1, :], st_sb[:1, 1024:1536], start=True, stop=True), "tm")
                    tm_bc = cnt["tm"]

                    ln_done = [None] * KC

                    @block.vector
                    def _(v, tt=tt, ln_done=ln_done):
                        v.wait_ge(tm, tm_bc)
                        for k in range(KC):
                            sl = S_sb[:, 1024 * k + 512 * tt:1024 * k + 512 * tt + 512]
                            inc(v.tensor_tensor(sl, sl, psB1[:, :], ALU.mult), "vv")
                        mid = cnt["vv"]
                        v.wait_ge(vv, mid)
                        for k in range(KC):
                            sl = S_sb[:, 1024 * k + 512 * tt:1024 * k + 512 * tt + 512]
                            ln_done[k] = inc(v.tensor_tensor(sl, sl, psB2[:, :], ALU.subtract), "vv")

                    if write_bf16_to is not None:
                        @block.scalar
                        def _(s, tt=tt, ln_done=ln_done):
                            for k in range(KC):
                                s.wait_ge(vv, ln_done[k])
                                inc(s.copy(abv(write_bf16_to + 1024 * k + 512 * tt, 512),
                                           S_sb[:, 1024 * k + 512 * tt:1024 * k + 512 * tt + 512]), "ss")

                    barrier()

        emit_ln(write_bf16_to=y1_o)

        if debug and "y1" in dumps:
            @block.sync
            def _(sy):
                sy.wait_ge(ss, cnt["ss"])
                inc(sy.dma_start(dbg["y1"][:, :], S_sb[:, :]), "so", 16)
            barrier()

        if upto <= 4:
            finish_prefix()
            return nc

        for half in range(2):
            with nc.psum_tensor(f"psG0_{half}", [128, 1024], F32) as psG0, \
                 nc.psum_tensor(f"psU0_{half}", [128, 1024], F32) as psU0, \
                 nc.psum_tensor(f"psG1_{half}", [128, 1024], F32) as psG1, \
                 nc.psum_tensor(f"psU1_{half}", [128, 1024], F32) as psU1:
                psG = [psG0, psG1]
                psU = [psU0, psU1]
                c_ms = [None] * 32
                c_g = [[None, None] for _ in range(32)]
                c_u = [None] * 32
                c_sil = [[None, None] for _ in range(32)]
                c_h = [[None, None] for _ in range(32)]

                for i in range(32):
                    ii = 32 * half + i

                    @block.vector
                    def _(v, i=i):
                        if i >= 2:
                            v.wait_ge(tm, c_u[i - 2])
                        c_ms[i] = inc(v.memset(abv(wgu_o + 4096 * (i % 2), 4096), -128.0), "vv")

                    @block.gpsimd
                    def _(g, i=i, ii=ii):
                        g.wait_ge(vv, c_ms[i])
                        pn = "we" if i % 2 == 0 else "wo"
                        inc(g.dma_start(abv(wgu_o + 4096 * (i % 2), 2048),
                                        wg_is[ii, :, :], accum_op=ALU.add), pn, 16)
                        inc(g.dma_start(abv(wgu_o + 4096 * (i % 2) + 2048, 2048),
                                        wu_is[ii, :, :], accum_op=ALU.add), pn, 16)
                    ld = cnt["we" if i % 2 == 0 else "wo"]

                    @block.tensor
                    def _(t, i=i, ld=ld):
                        t.wait_ge(we if i % 2 == 0 else wo, ld)
                        if i >= 2:
                            t.wait_ge(ss, c_sil[i - 2][1])
                            t.wait_ge(vv, c_h[i - 2][1])
                        for tt in range(2):
                            last = None
                            for k in range(KC):
                                last = t.matmul(psG[i % 2][:, 512 * tt:512 * tt + 512],
                                                abv(wgu_o + 4096 * (i % 2) + 128 * k, 128),
                                                abv(y1_o + 1024 * k + 512 * tt, 512),
                                                start=(k == 0), stop=(k == KC - 1))
                            c_g[i][tt] = inc(last, "tm")
                        last = None
                        for tt in range(2):
                            for k in range(KC):
                                last = t.matmul(psU[i % 2][:, 512 * tt:512 * tt + 512],
                                                abv(wgu_o + 4096 * (i % 2) + 2048 + 128 * k, 128),
                                                abv(y1_o + 1024 * k + 512 * tt, 512),
                                                start=(k == 0), stop=(k == KC - 1))
                        c_u[i] = inc(last, "tm")

                    for tt in range(2):
                        @block.scalar
                        def _(s, i=i, ii=ii, tt=tt):
                            s.wait_ge(tm, c_g[i][tt])
                            if i >= 1:
                                s.wait_ge(vv, c_h[i - 1][tt])
                            c_sil[i][tt] = inc(
                                s.activation(sq_sb[:, 512 * tt:512 * tt + 512],
                                             psG[i % 2][:, 512 * tt:512 * tt + 512], AF.Silu,
                                             bias=0.0, scale=scl_sb[:, ii:ii + 1]), "ss")

                        @block.vector
                        def _(v, i=i, ii=ii, tt=tt):
                            v.wait_ge(ss, c_sil[i][tt])
                            if tt == 1:
                                v.wait_ge(tm, c_u[i])
                            else:
                                v.wait_ge(tm, c_u[i])
                            c_h[i][tt] = inc(
                                v.scalar_tensor_tensor(abv(h_o + 1024 * i + 512 * tt, 512),
                                                       psU[i % 2][:, 512 * tt:512 * tt + 512],
                                                       scl_sb[:, 64 + ii:64 + ii + 1],
                                                       sq_sb[:, 512 * tt:512 * tt + 512],
                                                       ALU.mult, ALU.mult), "vv")

                barrier()

            if debug and "h" in dumps and half == 0:
                @block.gpsimd
                def _(g):
                    g.wait_ge(vv, cnt["vv"])
                    inc(g.dma_start(dbg["h"][:, :], abv(h_o, 32 * 1024)), "so", 16)
                barrier()

            if upto <= 5 and half == 0:
                break

            with nc.psum_tensor(f"psD0_{half}", [128, 1024], F32) as psD0, \
                 nc.psum_tensor(f"psD1_{half}", [128, 1024], F32) as psD1:
                psD = [psD0, psD1]
                c_wms = [None] * OC
                d_mm = [None] * OC
                d_ev = [None] * OC

                for o in range(OC):
                    @block.vector
                    def _(v, o=o):
                        if o >= 2:
                            v.wait_ge(tm, d_mm[o - 2])
                        c_wms[o] = inc(v.memset(abv(wd_o + 4096 * (o % 2), 4096), -128.0), "vv")

                    @block.gpsimd
                    def _(g, o=o, half=half):
                        g.wait_ge(vv, c_wms[o])
                        pn = "we" if o % 2 == 0 else "wo"
                        inc(g.dma_start(abv(wd_o + 4096 * (o % 2), 2048),
                                        wd_os[half, o, :, 0:2048], accum_op=ALU.add), pn, 16)
                        inc(g.dma_start(abv(wd_o + 4096 * (o % 2) + 2048, 2048),
                                        wd_os[half, o, :, 2048:4096], accum_op=ALU.add), pn, 16)
                    ld = cnt["we" if o % 2 == 0 else "wo"]

                    @block.tensor
                    def _(t, o=o, ld=ld):
                        t.wait_ge(we if o % 2 == 0 else wo, ld)
                        if o >= 2:
                            t.wait_ge(vv, d_ev[o - 2])
                        last = None
                        for tt in range(2):
                            for i in range(32):
                                last = t.matmul(psD[o % 2][:, 512 * tt:512 * tt + 512],
                                                abv(wd_o + 4096 * (o % 2) + 128 * i, 128),
                                                abv(h_o + 1024 * i + 512 * tt, 512),
                                                start=(i == 0), stop=(i == 31))
                        d_mm[o] = inc(last, "tm")

                    @block.vector
                    def _(v, o=o):
                        v.wait_ge(tm, d_mm[o])
                        d_ev[o] = inc(
                            v.scalar_tensor_tensor(S_sb[:, 1024 * o:1024 * o + 1024],
                                                   psD[o % 2][:, :],
                                                   scl_sb[:, 128 + o:128 + o + 1],
                                                   S_sb[:, 1024 * o:1024 * o + 1024],
                                                   ALU.mult, ALU.add), "vv")

                barrier()
            if upto <= 6 and half == 0:
                break

        if upto <= 6:
            finish_prefix()
            return nc

        if debug and "r2" in dumps:
            @block.sync
            def _(sy):
                sy.wait_ge(vv, cnt["vv"])
                inc(sy.dma_start(dbg["r2"][:, :], S_sb[:, :]), "so", 16)
            barrier()

        emit_ln(write_bf16_to=None)

        @block.sync
        def _(sy):
            sy.wait_ge(vv, cnt["vv"])
            inc(sy.dma_start(y_out[:, :], S_sb[:, :]), "so", 16)
            sy.wait_ge(so, cnt["so"])

    return nc


def _prep_shared(inp):
    Wq = np.asarray(inp["L0_attn_Wq"])
    Wk = np.asarray(inp["L0_attn_Wk"])
    Wv = np.asarray(inp["L0_attn_Wv"])
    Wo = np.asarray(inp["L0_attn_Wo"])
    Wg = np.asarray(inp["L0_ffn_gate_w"])
    Wu = np.asarray(inp["L0_ffn_up_w"])
    Wd = np.asarray(inp["L0_ffn_down_w"])
    gs = np.asarray(inp["L0_ffn_gate_scale"])
    us = np.asarray(inp["L0_ffn_up_scale"])
    ds = np.asarray(inp["L0_ffn_down_scale"])

    def oslab(W, oc):
        return np.ascontiguousarray(
            W.reshape(KC, 128, oc, 128).transpose(2, 1, 0, 3).reshape(oc, 128, KC * 128))

    wq_os = oslab(Wq, OC).astype(np.float32)
    wk_os = oslab(Wk, HKV).astype(np.float32)
    wo_os = oslab(Wo, OC).astype(np.float32)
    wv_ks = np.ascontiguousarray(Wv.reshape(KC, 128, 512)).astype(np.float32)
    wg_is = np.ascontiguousarray(
        Wg.reshape(IC, 128, KC, 128).transpose(0, 3, 2, 1).reshape(IC, 128, KC * 128)).astype(np.int32)
    wu_is = np.ascontiguousarray(
        Wu.reshape(IC, 128, KC, 128).transpose(0, 3, 2, 1).reshape(IC, 128, KC * 128)).astype(np.int32)
    wd_os = np.ascontiguousarray(
        Wd.reshape(OC, 128, 2, 32, 128).transpose(2, 0, 4, 3, 1).reshape(2, OC, 128, 32 * 128)).astype(np.int32)
    scales = np.concatenate([gs.reshape(IC, 128).T, us.reshape(IC, 128).T,
                             ds.reshape(OC, 128).T], axis=1).astype(np.float32)
    d01 = np.eye(128).astype(ml_dtypes.bfloat16)
    ones_c = np.ones((128, 1), np.float32)
    ones_r = np.ones((1, 128), np.float32)
    return dict(wq_os=wq_os, wk_os=wk_os, wv_ks=wv_ks, wo_os=wo_os,
                wg_is=wg_is, wu_is=wu_is, wd_os=wd_os, scales=scales,
                d01=d01, ones_c=ones_c, ones_r=ones_r)


def _mk_masks(part):
    r = np.arange(128)[:, None]
    c = np.arange(640)[None, :]
    band = (c > r) & (c <= r + 512)
    m = np.empty((NQT, 128, 640), np.float32)
    for qt in range(NQT):
        valid = band.copy()
        if part == 0:
            valid = valid & (c >= 512 - 128 * qt)
        m[qt] = np.where(valid, 0.0, NEG)
    return np.ascontiguousarray(m.transpose(1, 0, 2).reshape(128, NQT * 640)).astype(ml_dtypes.bfloat16)


def _prep_core(x, c):
    b, part = c // 4, c % 4
    tok0 = T * part
    xt = np.ascontiguousarray(np.asarray(x)[b].T)
    x_own = np.ascontiguousarray(
        xt[:, tok0:tok0 + T].reshape(KC, 128, T).transpose(1, 0, 2).reshape(128, KC * T)).astype(np.float32)
    win = np.zeros((D, TK), np.float32)
    lo = max(0, tok0 - HALO)
    win[:, HALO - (tok0 - lo):HALO] = xt[:, lo:tok0]
    win[:, HALO:] = xt[:, tok0:tok0 + T]
    x_all = np.ascontiguousarray(
        win.reshape(KC, 128, TK).transpose(1, 0, 2).reshape(128, KC * TK)).astype(np.float32)
    return dict(x_own=x_own, x_all=x_all, masks=_mk_masks(part))


def kernel(**inputs):
    shared = _prep_shared(inputs)
    x = inputs["L0_attn_hidden_states"]
    in_maps = []
    for c in range(NCORES):
        m = dict(shared)
        m.update(_prep_core(x, c))
        in_maps.append(m)
    nc = build_nc(debug=False)
    res = run_bass_kernel_spmd(nc, in_maps, core_ids=list(range(NCORES)))
    out = np.empty((B, S, D), np.float32)
    for c in range(NCORES):
        b, part = c // 4, c % 4
        y = np.asarray(res.results[c]["y"]).reshape(128, KC, T).transpose(1, 0, 2).reshape(D, T)
        out[b, T * part:T * part + T, :] = y.T
    return out


# version 03
# speedup vs baseline: 1.5430x; 1.5430x over previous
import contextlib
import numpy as np
import ml_dtypes
import concourse.bass as bass
import concourse.mybir as mybir
from concourse.bass_utils import run_bass_kernel_spmd

F32 = mybir.dt.float32
BF16 = mybir.dt.bfloat16
I32 = mybir.dt.int32
AF = mybir.ActivationFunctionType
ALU = mybir.AluOpType

B, S, D = 2, 4096, 2048
HQ, HKV, HD, WIN = 16, 4, 128, 512
NCORES = 8
T = 1024
HALO = 512
TK = T + HALO
KC = D // 128
OC = D // 128
IC = 8192 // 128
NQT = T // 128
NKT = TK // 128
EPS = 1e-5
NEG = -1.0e30


def build_nc(debug=False, upto=99, dumps=None):
    if dumps is None:
        dumps = {"qkv", "attn", "y1", "h", "r2"} if debug else set()
    nc = bass.Bass("TRN2", target_bir_lowering=False)

    x_own = nc.dram_tensor("x_own", [128, KC * T], F32, kind="ExternalInput")
    x_all = nc.dram_tensor("x_all", [128, KC * TK], F32, kind="ExternalInput")
    masks = nc.dram_tensor("masks", [128, NQT * 640], BF16, kind="ExternalInput")
    wq_os = nc.dram_tensor("wq_os", [OC, 128, 2048], F32, kind="ExternalInput")
    wk_os = nc.dram_tensor("wk_os", [HKV, 128, 2048], F32, kind="ExternalInput")
    wv_ks = nc.dram_tensor("wv_ks", [KC, 128, 512], F32, kind="ExternalInput")
    wo_os = nc.dram_tensor("wo_os", [OC, 128, 2048], F32, kind="ExternalInput")
    wg_is = nc.dram_tensor("wg_is", [IC, 128, 2048], I32, kind="ExternalInput")
    wu_is = nc.dram_tensor("wu_is", [IC, 128, 2048], I32, kind="ExternalInput")
    wd_os = nc.dram_tensor("wd_os", [2, OC, 128, 4096], I32, kind="ExternalInput")
    scales = nc.dram_tensor("scales", [128, 144], F32, kind="ExternalInput")
    d01_h = nc.dram_tensor("d01", [128, 128], BF16, kind="ExternalInput")
    ones_c = nc.dram_tensor("ones_c", [128, 1], F32, kind="ExternalInput")
    ones_r = nc.dram_tensor("ones_r", [1, 128], F32, kind="ExternalInput")
    negc = nc.dram_tensor("negc", [128, 4096], BF16, kind="ExternalInput")
    y_out = nc.dram_tensor("y", [128, KC * T], F32, kind="ExternalOutput")

    dbg = {}
    if debug:
        dbg["q"] = nc.dram_tensor("dbg_q", [128, HQ * T], F32, kind="ExternalOutput")
        dbg["k"] = nc.dram_tensor("dbg_k", [128, HKV * TK], F32, kind="ExternalOutput")
        dbg["vT"] = nc.dram_tensor("dbg_vT", [128, NKT * 512], F32, kind="ExternalOutput")
        dbg["attn"] = nc.dram_tensor("dbg_attn", [128, KC * T], F32, kind="ExternalOutput")
        dbg["y1"] = nc.dram_tensor("dbg_y1", [128, KC * T], F32, kind="ExternalOutput")
        dbg["h"] = nc.dram_tensor("dbg_h", [128, 32 * T], F32, kind="ExternalOutput")
        dbg["r2"] = nc.dram_tensor("dbg_r2", [128, KC * T], F32, kind="ExternalOutput")

    with contextlib.ExitStack() as ctx:
        S_sb = ctx.enter_context(nc.sbuf_tensor("S_sb", [128, KC * T], F32))
        ab = ctx.enter_context(nc.sbuf_tensor("ab", [128, 65536], BF16))
        sq_sb = ctx.enter_context(nc.sbuf_tensor("sq_sb", [128, 1024], F32))
        st_sb = ctx.enter_context(nc.sbuf_tensor("st_sb", [1, 1536], F32))
        scl_sb = ctx.enter_context(nc.sbuf_tensor("scl_sb", [128, 144], F32))
        d01_sb = ctx.enter_context(nc.sbuf_tensor("d01_sb", [128, 128], BF16))
        onc_sb = ctx.enter_context(nc.sbuf_tensor("onc_sb", [128, 1], F32))
        onr_sb = ctx.enter_context(nc.sbuf_tensor("onr_sb", [1, 128], F32))
        sums_sb = ctx.enter_context(nc.sbuf_tensor("sums_sb", [128, 4], F32))
        Dm_sb = ctx.enter_context(nc.sbuf_tensor("Dm_sb", [128, 256], BF16))

        def abv(off, n):
            return ab[:, off:off + n]

        x_bf_o, q_bf_o, k_bf_o, vT_o, wvres_o, wbuf_o = 0, 24576, 40960, 47104, 53248, 61440
        attn_o, probs_o, pT_o, mask_o = 0, 16384, 17664, 18944
        h_o, y1_o, wgu_o, wd_o = 0, 32768, 49152, 57344

        sg = ctx.enter_context(nc.semaphore("sg"))
        sq = ctx.enter_context(nc.semaphore("sq"))
        tm = ctx.enter_context(nc.semaphore("tm"))
        vv = ctx.enter_context(nc.semaphore("vv"))
        ss = ctx.enter_context(nc.semaphore("ss"))
        so = ctx.enter_context(nc.semaphore("so"))
        we = ctx.enter_context(nc.semaphore("we"))
        wo = ctx.enter_context(nc.semaphore("wo"))
        sems = {"sg": sg, "sq": sq, "tm": tm, "vv": vv, "ss": ss, "so": so,
                "we": we, "wo": wo}
        cnt = {k: 0 for k in sems}

        def inc(instr, name, step=1):
            cnt[name] += step
            instr.then_inc(sems[name], step)
            return cnt[name]

        block = ctx.enter_context(nc.Block())

        _bar_prev = [{k: 0 for k in sems}]

        def barrier():
            snap = {k: cnt[k] for k in sems}
            prev = _bar_prev[0]
            own_of = {"tm": block.tensor, "vv": block.vector, "ss": block.scalar}
            for own, engfn in (("tm", block.tensor), ("vv", block.vector),
                               ("ss", block.scalar), (None, block.gpsimd),
                               (None, block.sync)):
                def _f(e, snap=snap, own=own, prev=dict(prev)):
                    for k, v in snap.items():
                        if v > 0 and k != own and v > prev[k]:
                            e.wait_ge(sems[k], v)
                engfn(_f)
            _bar_prev[0] = snap


        def finish_prefix():
            @block.sync
            def _(sy):
                sy.wait_ge(so, cnt["so"])
        @block.sync
        def _(sy):
            inc(sy.dma_start(S_sb[:, :], x_own[:, :]), "sq", 16)
            inc(sy.dma_start(scl_sb[:, :], scales[:, :]), "sq", 16)
            inc(sy.dma_start(d01_sb[:, :], d01_h[:, :]), "sq", 16)
            inc(sy.dma_start(onc_sb[:, :], ones_c[:, :]), "sq", 16)
            inc(sy.dma_start(onr_sb[:1, :], ones_r[:1, :]), "sq", 16)
        p0_sq = cnt["sq"]

        @block.gpsimd
        def _(g):
            inc(g.dma_start(abv(x_bf_o, KC * TK), x_all[:, :]), "sg", 16)
            for k in range(KC):
                inc(g.dma_start(abv(wvres_o + 512 * k, 512), wv_ks[k, :, :]), "sg", 16)
        wv_ld = cnt["sg"]

        with nc.psum_tensor("psA", [128, 512], F32) as psA, \
             nc.psum_tensor("psB", [128, 512], F32) as psB, \
             nc.psum_tensor("psQ0", [128, 1024], F32) as psQ0, \
             nc.psum_tensor("psQ1", [128, 1024], F32) as psQ1:
            psV = [psA, psB]
            psQ = [psQ0, psQ1]

            vt_mm = [None] * NKT
            vt_ev = [None] * NKT
            for j in range(NKT):
                @block.tensor
                def _(t, j=j):
                    if j == 0:
                        t.wait_ge(sg, wv_ld)
                    if j >= 2:
                        t.wait_ge(vv, vt_ev[j - 2])
                    ps = psV[j % 2]
                    last = None
                    for k in range(KC):
                        last = t.matmul(ps[:, :], abv(x_bf_o + k * TK + j * 128, 128),
                                        abv(wvres_o + 512 * k, 512),
                                        start=(k == 0), stop=(k == KC - 1))
                    vt_mm[j] = inc(last, "tm")

                @block.vector
                def _(v, j=j):
                    v.wait_ge(tm, vt_mm[j])
                    vt_ev[j] = inc(v.tensor_copy(abv(vT_o + 512 * j, 512), psV[j % 2][:, :]), "vv")

            def proj(noc, wsrc, rhs_off, rhs_stride, rhs_base, ntt, out_kind, out_off):
                mm = [None] * noc
                ev = [None] * noc
                tm0 = cnt["tm"]
                for o in range(noc):
                    @block.gpsimd
                    def _(g, o=o, tm0=tm0):
                        if o >= 2:
                            g.wait_ge(tm, mm[o - 2])
                        else:
                            g.wait_ge(tm, tm0)
                        inc(g.dma_start(abv(wbuf_o + 2048 * (o % 2), 2048), wsrc[o, :, :]),
                            "we" if o % 2 == 0 else "wo", 16)
                    ld = cnt["we" if o % 2 == 0 else "wo"]

                    @block.tensor
                    def _(t, o=o, ld=ld):
                        t.wait_ge(we if o % 2 == 0 else wo, ld)
                        if o >= 2:
                            t.wait_ge(vv, ev[o - 2])
                        last = None
                        if ntt == 2:
                            ps = psQ[o % 2]
                            for tt in range(2):
                                for k in range(KC):
                                    last = t.matmul(ps[:, 512 * tt:512 * tt + 512],
                                                    abv(wbuf_o + 2048 * (o % 2) + 128 * k, 128),
                                                    abv(rhs_base + k * rhs_stride + rhs_off + 512 * tt, 512),
                                                    start=(k == 0), stop=(k == KC - 1))
                        else:
                            for tt in range(3):
                                ps = psQ[o % 2] if tt < 2 else psV[o % 2]
                                sl = ps[:, 512 * tt:512 * tt + 512] if tt < 2 else ps[:, :]
                                for k in range(KC):
                                    last = t.matmul(sl,
                                                    abv(wbuf_o + 2048 * (o % 2) + 128 * k, 128),
                                                    abv(rhs_base + k * rhs_stride + rhs_off + 512 * tt, 512),
                                                    start=(k == 0), stop=(k == KC - 1))
                        mm[o] = inc(last, "tm")

                    @block.vector
                    def _(v, o=o):
                        v.wait_ge(tm, mm[o])
                        if out_kind == "copy":
                            if ntt == 2:
                                ev[o] = inc(v.tensor_copy(abv(out_off + 1024 * o, 1024),
                                                          psQ[o % 2][:, :]), "vv")
                            else:
                                v.tensor_copy(abv(out_off + 1536 * o, 1024), psQ[o % 2][:, :])
                                ev[o] = inc(v.tensor_copy(abv(out_off + 1536 * o + 1024, 512),
                                                          psV[o % 2][:, :]), "vv")
                        else:
                            ev[o] = inc(v.tensor_tensor(S_sb[:, 1024 * o:1024 * o + 1024],
                                                        S_sb[:, 1024 * o:1024 * o + 1024],
                                                        psQ[o % 2][:, :], ALU.add), "vv")
                return ev

            proj(OC, wq_os, HALO, TK, x_bf_o, 2, "copy", q_bf_o)
            vt_guard = cnt["vv"]

            @block.tensor
            def _(t):
                t.wait_ge(vv, vt_guard)
            proj(HKV, wk_os, 0, TK, x_bf_o, 3, "copy", k_bf_o)

            if debug and "qkv" in dumps:
                @block.gpsimd
                def _(g):
                    g.wait_ge(vv, cnt["vv"])
                    inc(g.dma_start(dbg["q"][:, :], abv(q_bf_o, HQ * 1024)), "so", 16)
                    inc(g.dma_start(dbg["k"][:, :], abv(k_bf_o, HKV * 1536)), "so", 16)
                    inc(g.dma_start(dbg["vT"][:, :], abv(vT_o, NKT * 512)), "so", 16)

            barrier()

        if upto <= 1:
            finish_prefix()
            return nc

        @block.sync
        def _(sy):
            inc(sy.dma_start(abv(mask_o, 8 * 640), masks[:, :]), "sq", 16)
        mask_ld = cnt["sq"]

        with nc.psum_tensor("sc0", [128, 640], F32) as sc0, \
             nc.psum_tensor("sc1", [128, 640], F32) as sc1, \
             nc.psum_tensor("pT", [128, 640], F32) as pTp, \
             nc.psum_tensor("pv0", [128, 128], F32) as pv0, \
             nc.psum_tensor("pv1", [128, 128], F32) as pv1:
            scs = [sc0, sc1]
            pvs = [pv0, pv1]
            items = [(h, qt) for h in range(HQ) for qt in range(NQT)]
            NIT = len(items)
            c_mm = [None] * NIT
            c_msk = [None] * NIT
            c_exp = [None] * NIT
            c_rec = [None] * NIT
            c_D = [None] * NIT
            c_tD = [None] * NIT
            c_pTe = [None] * NIT
            c_pv = [None] * NIT
            c_ae = [None] * NIT

            for n in range(NIT + 2):
                @block.tensor
                def _(t, n=n):
                    if n < NIT:
                        h, qt = items[n]
                        if n >= 2:
                            t.wait_ge(ss, c_exp[n - 2])
                        qs = q_bf_o + 1024 * h + 128 * qt
                        ks = k_bf_o + 1536 * (h // 4) + 128 * qt
                        sc = scs[n % 2]
                        t.matmul(sc[:, 0:512], abv(qs, 128), abv(ks, 512), start=True, stop=True)
                        c_mm[n] = inc(t.matmul(sc[:, 512:640], abv(qs, 128), abv(ks + 512, 128),
                                               start=True, stop=True), "tm")
                    if 0 <= n - 1 < NIT:
                        m = n - 1
                        t.wait_ge(vv, c_D[m])
                        if m >= 1:
                            t.wait_ge(vv, c_pTe[m - 1])
                        last = None
                        for b in range(5):
                            last = t.matmul(pTp[:, 128 * b:128 * b + 128],
                                            abv(probs_o + 640 * (m % 2) + 128 * b, 128),
                                            Dm_sb[:, 128 * (m % 2):128 * (m % 2) + 128],
                                            start=True, stop=True)
                        c_tD[m] = inc(last, "tm")
                    if 0 <= n - 2 < NIT:
                        m = n - 2
                        t.wait_ge(vv, c_pTe[m])
                        if m >= 2:
                            t.wait_ge(vv, c_ae[m - 2])
                        mh, mqt = items[m]
                        last = None
                        for b in range(5):
                            last = t.matmul(pvs[m % 2][:, :],
                                            abv(vT_o + 512 * (mqt + b) + 128 * (mh // 4), 128),
                                            abv(pT_o + 640 * (m % 2) + 128 * b, 128),
                                            start=(b == 0), stop=(b == 4))
                        c_pv[m] = inc(last, "tm")

                @block.vector
                def _(v, n=n):
                    if n < NIT:
                        h, qt = items[n]
                        if n == 0:
                            v.wait_ge(sq, mask_ld)
                        v.wait_ge(tm, c_mm[n])
                        c_msk[n] = inc(v.tensor_tensor(scs[n % 2][:, :], scs[n % 2][:, :],
                                                       abv(mask_o + 640 * qt, 640), ALU.add), "vv")
                    if 0 <= n - 2 < NIT:
                        m = n - 2
                        mh, mqt = items[m]
                        v.wait_ge(tm, c_pv[m])
                        c_ae[m] = inc(v.tensor_copy(abv(attn_o + 1024 * mh + 128 * mqt, 128),
                                                    pvs[m % 2][:, :]), "vv")
                    if 0 <= n - 1 < NIT:
                        m = n - 1
                        v.wait_ge(tm, c_tD[m])
                        c_pTe[m] = inc(v.tensor_copy(abv(pT_o + 640 * (m % 2), 640), pTp[:, :]), "vv")

                @block.scalar
                def _(s, n=n):
                    if n < NIT:
                        if n == 0:
                            s.wait_ge(sq, p0_sq)
                        s.wait_ge(vv, c_msk[n])
                        if n >= 2:
                            s.wait_ge(tm, c_tD[n - 2])
                        c_exp[n] = inc(
                            s.activation(abv(probs_o + 640 * (n % 2), 640), scs[n % 2][:, :], AF.Exp,
                                         bias=0.0, scale=1.0 / 128.0,
                                         accum_out=sums_sb[:, 2 * (n % 2):2 * (n % 2) + 1]), "ss")

                if n < NIT:
                    @block.vector
                    def _(v, n=n):
                        v.wait_ge(ss, c_exp[n])
                        c_rec[n] = inc(v.reciprocal(sums_sb[:, 2 * (n % 2) + 1:2 * (n % 2) + 2],
                                                    sums_sb[:, 2 * (n % 2):2 * (n % 2) + 1]), "vv")
                        v.wait_ge(vv, c_rec[n])
                        c_D[n] = inc(v.scalar_tensor_tensor(
                            Dm_sb[:, 128 * (n % 2):128 * (n % 2) + 128], d01_sb[:, :],
                            sums_sb[:, 2 * (n % 2) + 1:2 * (n % 2) + 2], d01_sb[:, :],
                            ALU.mult, ALU.mult), "vv")

            if debug and "attn" in dumps:
                @block.gpsimd
                def _(g):
                    g.wait_ge(vv, cnt["vv"])
                    inc(g.dma_start(dbg["attn"][:, :], abv(attn_o, KC * T)), "so", 16)

            barrier()

        if upto <= 2:
            finish_prefix()
            return nc

        with nc.psum_tensor("psW0", [128, 1024], F32) as psW0, \
             nc.psum_tensor("psW1", [128, 1024], F32) as psW1:
            psQ = [psW0, psW1]

            def proj_wo():
                mm = [None] * OC
                ev = [None] * OC
                tm0 = cnt["tm"]
                for o in range(OC):
                    @block.gpsimd
                    def _(g, o=o, tm0=tm0):
                        if o >= 2:
                            g.wait_ge(tm, mm[o - 2])
                        else:
                            g.wait_ge(tm, tm0)
                        inc(g.dma_start(abv(wbuf_o + 2048 * (o % 2), 2048), wo_os[o, :, :]),
                            "we" if o % 2 == 0 else "wo", 16)
                    ld = cnt["we" if o % 2 == 0 else "wo"]

                    @block.tensor
                    def _(t, o=o, ld=ld):
                        t.wait_ge(we if o % 2 == 0 else wo, ld)
                        if o >= 2:
                            t.wait_ge(vv, ev[o - 2])
                        last = None
                        for tt in range(2):
                            for k in range(KC):
                                last = t.matmul(psQ[o % 2][:, 512 * tt:512 * tt + 512],
                                                abv(wbuf_o + 2048 * (o % 2) + 128 * k, 128),
                                                abv(attn_o + 1024 * k + 512 * tt, 512),
                                                start=(k == 0), stop=(k == KC - 1))
                        mm[o] = inc(last, "tm")

                    @block.vector
                    def _(v, o=o):
                        v.wait_ge(tm, mm[o])
                        ev[o] = inc(v.tensor_tensor(S_sb[:, 1024 * o:1024 * o + 1024],
                                                    S_sb[:, 1024 * o:1024 * o + 1024],
                                                    psQ[o % 2][:, :], ALU.add), "vv")
            proj_wo()
            barrier()

        if upto <= 3:
            finish_prefix()
            return nc

        _ln_id = [0]

        def emit_ln(write_bf16_to=None):
            _ln_id[0] += 1
            li = _ln_id[0]
            with nc.psum_tensor(f"psS1_{li}", [1, 512], F32) as psS1, \
                 nc.psum_tensor(f"psS2_{li}", [1, 512], F32) as psS2, \
                 nc.psum_tensor(f"psB1_{li}", [128, 512], F32) as psB1, \
                 nc.psum_tensor(f"psB2_{li}", [128, 512], F32) as psB2:
                for tt in range(2):
                    @block.tensor
                    def _(t, tt=tt):
                        t.wait_ge(sq, p0_sq)
                        last = None
                        for k in range(KC):
                            last = t.matmul(psS1[:1, :], onc_sb[:, :1],
                                            S_sb[:, 1024 * k + 512 * tt:1024 * k + 512 * tt + 512],
                                            start=(k == 0), stop=(k == KC - 1))
                        inc(last, "tm")
                    c_sq = [None] * KC
                    c_smm = [None] * KC
                    for k in range(KC):
                        @block.scalar
                        def _(s, k=k, tt=tt):
                            if k >= 2:
                                s.wait_ge(tm, c_smm[k - 2])
                            c_sq[k] = inc(
                                s.activation(sq_sb[:, 512 * (k % 2):512 * (k % 2) + 512],
                                             S_sb[:, 1024 * k + 512 * tt:1024 * k + 512 * tt + 512],
                                             AF.Square), "ss")

                        @block.tensor
                        def _(t, k=k):
                            t.wait_ge(ss, c_sq[k])
                            c_smm[k] = inc(
                                t.matmul(psS2[:1, :], onc_sb[:, :1],
                                         sq_sb[:, 512 * (k % 2):512 * (k % 2) + 512],
                                         start=(k == 0), stop=(k == KC - 1)), "tm")
                    tm_sums = cnt["tm"]

                    @block.vector
                    def _(v):
                        v.wait_ge(tm, tm_sums)
                        inc(v.tensor_scalar(st_sb[:1, 0:512], psS1[:1, :], 1.0 / D, None, ALU.mult), "vv")
                    c1 = cnt["vv"]

                    @block.scalar
                    def _(s):
                        s.wait_ge(vv, c1)
                        inc(s.activation(st_sb[:1, 1024:1536], st_sb[:1, 0:512], AF.Square), "ss")
                    c2 = cnt["ss"]

                    @block.vector
                    def _(v):
                        v.wait_ge(ss, c2)
                        inc(v.scalar_tensor_tensor(st_sb[:1, 512:1024], psS2[:1, :], 1.0 / D,
                                                   st_sb[:1, 1024:1536], ALU.mult, ALU.subtract), "vv")
                    c3 = cnt["vv"]

                    @block.scalar
                    def _(s):
                        s.wait_ge(vv, c3)
                        inc(s.activation(st_sb[:1, 512:1024], st_sb[:1, 512:1024],
                                         AF.Copy, bias=EPS), "ss")
                    c4 = cnt["ss"]

                    @block.vector
                    def _(v):
                        v.wait_ge(ss, c4)
                        inc(v.reciprocal(st_sb[:1, 512:1024], st_sb[:1, 512:1024]), "vv")
                    vv_rec = cnt["vv"]

                    @block.scalar
                    def _(s):
                        s.wait_ge(vv, vv_rec)
                        inc(s.activation(st_sb[:1, 512:1024], st_sb[:1, 512:1024], AF.Sqrt), "ss")
                    ss_rstd = cnt["ss"]

                    @block.vector
                    def _(v):
                        v.wait_ge(ss, ss_rstd)
                        inc(v.tensor_tensor(st_sb[:1, 1024:1536], st_sb[:1, 0:512],
                                            st_sb[:1, 512:1024], ALU.mult), "vv")
                    vv_mr = cnt["vv"]

                    @block.tensor
                    def _(t):
                        t.wait_ge(vv, vv_mr)
                        t.matmul(psB1[:, :], onr_sb[:1, :], st_sb[:1, 512:1024], start=True, stop=True)
                        inc(t.matmul(psB2[:, :], onr_sb[:1, :], st_sb[:1, 1024:1536], start=True, stop=True), "tm")
                    tm_bc = cnt["tm"]

                    ln_done = [None] * KC

                    @block.vector
                    def _(v, tt=tt, ln_done=ln_done):
                        v.wait_ge(tm, tm_bc)
                        for k in range(KC):
                            sl = S_sb[:, 1024 * k + 512 * tt:1024 * k + 512 * tt + 512]
                            inc(v.tensor_tensor(sl, sl, psB1[:, :], ALU.mult), "vv")
                        mid = cnt["vv"]
                        v.wait_ge(vv, mid)
                        for k in range(KC):
                            sl = S_sb[:, 1024 * k + 512 * tt:1024 * k + 512 * tt + 512]
                            ln_done[k] = inc(v.tensor_tensor(sl, sl, psB2[:, :], ALU.subtract), "vv")

                    if write_bf16_to is not None:
                        @block.scalar
                        def _(s, tt=tt, ln_done=ln_done):
                            for k in range(KC):
                                s.wait_ge(vv, ln_done[k])
                                inc(s.copy(abv(write_bf16_to + 1024 * k + 512 * tt, 512),
                                           S_sb[:, 1024 * k + 512 * tt:1024 * k + 512 * tt + 512]), "ss")

                    barrier()

        emit_ln(write_bf16_to=y1_o)

        if debug and "y1" in dumps:
            @block.sync
            def _(sy):
                sy.wait_ge(ss, cnt["ss"])
                inc(sy.dma_start(dbg["y1"][:, :], S_sb[:, :]), "so", 16)
            barrier()

        if upto <= 4:
            finish_prefix()
            return nc

        for half in range(2):
            with nc.psum_tensor(f"psG0_{half}", [128, 1024], F32) as psG0, \
                 nc.psum_tensor(f"psU0_{half}", [128, 1024], F32) as psU0, \
                 nc.psum_tensor(f"psG1_{half}", [128, 1024], F32) as psG1, \
                 nc.psum_tensor(f"psU1_{half}", [128, 1024], F32) as psU1:
                psG = [psG0, psG1]
                psU = [psU0, psU1]
                c_ms = [None] * 32
                c_g = [[None, None] for _ in range(32)]
                c_u = [None] * 32
                c_sil = [[None, None] for _ in range(32)]
                c_h = [[None, None] for _ in range(32)]

                for i in range(32):
                    ii = 32 * half + i

                    @block.gpsimd
                    def _(g, i=i, ii=ii):
                        if i >= 2:
                            g.wait_ge(tm, c_u[i - 2])
                        pn = "we" if i % 2 == 0 else "wo"
                        cload = inc(g.dma_start(abv(wgu_o + 4096 * (i % 2), 4096), negc[:, :]), pn, 16)
                        g.wait_ge(sems[pn], cload)
                        inc(g.dma_start(abv(wgu_o + 4096 * (i % 2), 2048),
                                        wg_is[ii, :, :], accum_op=ALU.add), pn, 16)
                        inc(g.dma_start(abv(wgu_o + 4096 * (i % 2) + 2048, 2048),
                                        wu_is[ii, :, :], accum_op=ALU.add), pn, 16)
                    ld = cnt["we" if i % 2 == 0 else "wo"]

                    @block.tensor
                    def _(t, i=i, ld=ld):
                        t.wait_ge(we if i % 2 == 0 else wo, ld)
                        if i >= 2:
                            t.wait_ge(ss, c_sil[i - 2][1])
                            t.wait_ge(vv, c_h[i - 2][1])
                        for tt in range(2):
                            last = None
                            for k in range(KC):
                                last = t.matmul(psG[i % 2][:, 512 * tt:512 * tt + 512],
                                                abv(wgu_o + 4096 * (i % 2) + 128 * k, 128),
                                                abv(y1_o + 1024 * k + 512 * tt, 512),
                                                start=(k == 0), stop=(k == KC - 1))
                            c_g[i][tt] = inc(last, "tm")
                        last = None
                        for tt in range(2):
                            for k in range(KC):
                                last = t.matmul(psU[i % 2][:, 512 * tt:512 * tt + 512],
                                                abv(wgu_o + 4096 * (i % 2) + 2048 + 128 * k, 128),
                                                abv(y1_o + 1024 * k + 512 * tt, 512),
                                                start=(k == 0), stop=(k == KC - 1))
                        c_u[i] = inc(last, "tm")

                    for tt in range(2):
                        @block.scalar
                        def _(s, i=i, ii=ii, tt=tt):
                            s.wait_ge(tm, c_g[i][tt])
                            if i >= 1:
                                s.wait_ge(vv, c_h[i - 1][tt])
                            c_sil[i][tt] = inc(
                                s.activation(sq_sb[:, 512 * tt:512 * tt + 512],
                                             psG[i % 2][:, 512 * tt:512 * tt + 512], AF.Silu,
                                             bias=0.0, scale=scl_sb[:, ii:ii + 1]), "ss")

                        @block.vector
                        def _(v, i=i, ii=ii, tt=tt):
                            v.wait_ge(ss, c_sil[i][tt])
                            if tt == 1:
                                v.wait_ge(tm, c_u[i])
                            else:
                                v.wait_ge(tm, c_u[i])
                            c_h[i][tt] = inc(
                                v.scalar_tensor_tensor(abv(h_o + 1024 * i + 512 * tt, 512),
                                                       psU[i % 2][:, 512 * tt:512 * tt + 512],
                                                       scl_sb[:, 64 + ii:64 + ii + 1],
                                                       sq_sb[:, 512 * tt:512 * tt + 512],
                                                       ALU.mult, ALU.mult), "vv")

                barrier()

            if debug and "h" in dumps and half == 0:
                @block.gpsimd
                def _(g):
                    g.wait_ge(vv, cnt["vv"])
                    inc(g.dma_start(dbg["h"][:, :], abv(h_o, 32 * 1024)), "so", 16)
                barrier()

            if upto <= 5 and half == 0:
                break

            with nc.psum_tensor(f"psD0_{half}", [128, 1024], F32) as psD0, \
                 nc.psum_tensor(f"psD1_{half}", [128, 1024], F32) as psD1:
                psD = [psD0, psD1]
                c_wms = [None] * OC
                d_mm = [None] * OC
                d_ev = [None] * OC

                for o in range(OC):
                    @block.gpsimd
                    def _(g, o=o, half=half):
                        if o >= 2:
                            g.wait_ge(tm, d_mm[o - 2])
                        pn = "we" if o % 2 == 0 else "wo"
                        cload = inc(g.dma_start(abv(wd_o + 4096 * (o % 2), 4096), negc[:, :]), pn, 16)
                        g.wait_ge(sems[pn], cload)
                        inc(g.dma_start(abv(wd_o + 4096 * (o % 2), 2048),
                                        wd_os[half, o, :, 0:2048], accum_op=ALU.add), pn, 16)
                        inc(g.dma_start(abv(wd_o + 4096 * (o % 2) + 2048, 2048),
                                        wd_os[half, o, :, 2048:4096], accum_op=ALU.add), pn, 16)
                    ld = cnt["we" if o % 2 == 0 else "wo"]

                    @block.tensor
                    def _(t, o=o, ld=ld):
                        t.wait_ge(we if o % 2 == 0 else wo, ld)
                        if o >= 2:
                            t.wait_ge(vv, d_ev[o - 2])
                        last = None
                        for tt in range(2):
                            for i in range(32):
                                last = t.matmul(psD[o % 2][:, 512 * tt:512 * tt + 512],
                                                abv(wd_o + 4096 * (o % 2) + 128 * i, 128),
                                                abv(h_o + 1024 * i + 512 * tt, 512),
                                                start=(i == 0), stop=(i == 31))
                        d_mm[o] = inc(last, "tm")

                    @block.vector
                    def _(v, o=o):
                        v.wait_ge(tm, d_mm[o])
                        d_ev[o] = inc(
                            v.scalar_tensor_tensor(S_sb[:, 1024 * o:1024 * o + 1024],
                                                   psD[o % 2][:, :],
                                                   scl_sb[:, 128 + o:128 + o + 1],
                                                   S_sb[:, 1024 * o:1024 * o + 1024],
                                                   ALU.mult, ALU.add), "vv")

                barrier()
            if upto <= 6 and half == 0:
                break

        if upto <= 6:
            finish_prefix()
            return nc

        if debug and "r2" in dumps:
            @block.sync
            def _(sy):
                sy.wait_ge(vv, cnt["vv"])
                inc(sy.dma_start(dbg["r2"][:, :], S_sb[:, :]), "so", 16)
            barrier()

        emit_ln(write_bf16_to=None)

        @block.sync
        def _(sy):
            sy.wait_ge(vv, cnt["vv"])
            inc(sy.dma_start(y_out[:, :], S_sb[:, :]), "so", 16)
            sy.wait_ge(so, cnt["so"])

    return nc


def _prep_shared(inp):
    Wq = np.asarray(inp["L0_attn_Wq"])
    Wk = np.asarray(inp["L0_attn_Wk"])
    Wv = np.asarray(inp["L0_attn_Wv"])
    Wo = np.asarray(inp["L0_attn_Wo"])
    Wg = np.asarray(inp["L0_ffn_gate_w"])
    Wu = np.asarray(inp["L0_ffn_up_w"])
    Wd = np.asarray(inp["L0_ffn_down_w"])
    gs = np.asarray(inp["L0_ffn_gate_scale"])
    us = np.asarray(inp["L0_ffn_up_scale"])
    ds = np.asarray(inp["L0_ffn_down_scale"])

    def oslab(W, oc):
        return np.ascontiguousarray(
            W.reshape(KC, 128, oc, 128).transpose(2, 1, 0, 3).reshape(oc, 128, KC * 128))

    wq_os = oslab(Wq, OC).astype(np.float32)
    wk_os = oslab(Wk, HKV).astype(np.float32)
    wo_os = oslab(Wo, OC).astype(np.float32)
    wv_ks = np.ascontiguousarray(Wv.reshape(KC, 128, 512)).astype(np.float32)
    wg_is = np.ascontiguousarray(
        Wg.reshape(IC, 128, KC, 128).transpose(0, 3, 2, 1).reshape(IC, 128, KC * 128)).astype(np.int32)
    wu_is = np.ascontiguousarray(
        Wu.reshape(IC, 128, KC, 128).transpose(0, 3, 2, 1).reshape(IC, 128, KC * 128)).astype(np.int32)
    wd_os = np.ascontiguousarray(
        Wd.reshape(OC, 128, 2, 32, 128).transpose(2, 0, 4, 3, 1).reshape(2, OC, 128, 32 * 128)).astype(np.int32)
    scales = np.concatenate([gs.reshape(IC, 128).T, us.reshape(IC, 128).T,
                             ds.reshape(OC, 128).T], axis=1).astype(np.float32)
    d01 = np.eye(128).astype(ml_dtypes.bfloat16)
    ones_c = np.ones((128, 1), np.float32)
    ones_r = np.ones((1, 128), np.float32)
    negc = np.full((128, 4096), -128.0, dtype=ml_dtypes.bfloat16)
    return dict(negc=negc, wq_os=wq_os, wk_os=wk_os, wv_ks=wv_ks, wo_os=wo_os,
                wg_is=wg_is, wu_is=wu_is, wd_os=wd_os, scales=scales,
                d01=d01, ones_c=ones_c, ones_r=ones_r)


def _mk_masks(part):
    r = np.arange(128)[:, None]
    c = np.arange(640)[None, :]
    band = (c > r) & (c <= r + 512)
    m = np.empty((NQT, 128, 640), np.float32)
    for qt in range(NQT):
        valid = band.copy()
        if part == 0:
            valid = valid & (c >= 512 - 128 * qt)
        m[qt] = np.where(valid, 0.0, NEG)
    return np.ascontiguousarray(m.transpose(1, 0, 2).reshape(128, NQT * 640)).astype(ml_dtypes.bfloat16)


def _prep_core(x, c):
    b, part = c // 4, c % 4
    tok0 = T * part
    xt = np.ascontiguousarray(np.asarray(x)[b].T)
    x_own = np.ascontiguousarray(
        xt[:, tok0:tok0 + T].reshape(KC, 128, T).transpose(1, 0, 2).reshape(128, KC * T)).astype(np.float32)
    win = np.zeros((D, TK), np.float32)
    lo = max(0, tok0 - HALO)
    win[:, HALO - (tok0 - lo):HALO] = xt[:, lo:tok0]
    win[:, HALO:] = xt[:, tok0:tok0 + T]
    x_all = np.ascontiguousarray(
        win.reshape(KC, 128, TK).transpose(1, 0, 2).reshape(128, KC * TK)).astype(np.float32)
    return dict(x_own=x_own, x_all=x_all, masks=_mk_masks(part))


def kernel(**inputs):
    shared = _prep_shared(inputs)
    x = inputs["L0_attn_hidden_states"]
    in_maps = []
    for c in range(NCORES):
        m = dict(shared)
        m.update(_prep_core(x, c))
        in_maps.append(m)
    nc = build_nc(debug=False)
    res = run_bass_kernel_spmd(nc, in_maps, core_ids=list(range(NCORES)))
    out = np.empty((B, S, D), np.float32)
    for c in range(NCORES):
        b, part = c // 4, c % 4
        y = np.asarray(res.results[c]["y"]).reshape(128, KC, T).transpose(1, 0, 2).reshape(D, T)
        out[b, T * part:T * part + T, :] = y.T
    return out


# version 04
# speedup vs baseline: 1.6096x; 1.0432x over previous
import contextlib
import numpy as np
import ml_dtypes
import concourse.bass as bass
import concourse.mybir as mybir
from concourse.bass_utils import run_bass_kernel_spmd

F32 = mybir.dt.float32
BF16 = mybir.dt.bfloat16
I32 = mybir.dt.int32
AF = mybir.ActivationFunctionType
ALU = mybir.AluOpType

B, S, D = 2, 4096, 2048
HQ, HKV, HD, WIN = 16, 4, 128, 512
NCORES = 8
T = 1024
HALO = 512
TK = T + HALO
KC = D // 128
OC = D // 128
IC = 8192 // 128
NQT = T // 128
NKT = TK // 128
EPS = 1e-5
NEG = -1.0e30


def build_nc(debug=False, upto=99, dumps=None):
    if dumps is None:
        dumps = {"qkv", "attn", "y1", "h", "r2"} if debug else set()
    nc = bass.Bass("TRN2", target_bir_lowering=False)

    x_own = nc.dram_tensor("x_own", [128, KC * T], F32, kind="ExternalInput")
    x_all = nc.dram_tensor("x_all", [128, KC * TK], F32, kind="ExternalInput")
    masks = nc.dram_tensor("masks", [128, NQT * 640], BF16, kind="ExternalInput")
    wq_os = nc.dram_tensor("wq_os", [OC, 128, 2048], F32, kind="ExternalInput")
    wk_os = nc.dram_tensor("wk_os", [HKV, 128, 2048], F32, kind="ExternalInput")
    wv_ks = nc.dram_tensor("wv_ks", [KC, 128, 512], F32, kind="ExternalInput")
    wo_os = nc.dram_tensor("wo_os", [OC, 128, 2048], F32, kind="ExternalInput")
    wg_is = nc.dram_tensor("wg_is", [IC, 128, 2048], I32, kind="ExternalInput")
    wu_is = nc.dram_tensor("wu_is", [IC, 128, 2048], I32, kind="ExternalInput")
    wd_os = nc.dram_tensor("wd_os", [2, OC, 128, 4096], I32, kind="ExternalInput")
    scales = nc.dram_tensor("scales", [128, 144], F32, kind="ExternalInput")
    d01_h = nc.dram_tensor("d01", [128, 128], BF16, kind="ExternalInput")
    ones_c = nc.dram_tensor("ones_c", [128, 1], F32, kind="ExternalInput")
    ones_r = nc.dram_tensor("ones_r", [1, 128], F32, kind="ExternalInput")
    negc = nc.dram_tensor("negc", [128, 4096], BF16, kind="ExternalInput")
    y_out = nc.dram_tensor("y", [128, KC * T], F32, kind="ExternalOutput")

    dbg = {}
    if debug:
        dbg["q"] = nc.dram_tensor("dbg_q", [128, HQ * T], F32, kind="ExternalOutput")
        dbg["k"] = nc.dram_tensor("dbg_k", [128, HKV * TK], F32, kind="ExternalOutput")
        dbg["vT"] = nc.dram_tensor("dbg_vT", [128, NKT * 512], F32, kind="ExternalOutput")
        dbg["attn"] = nc.dram_tensor("dbg_attn", [128, KC * T], F32, kind="ExternalOutput")
        dbg["y1"] = nc.dram_tensor("dbg_y1", [128, KC * T], F32, kind="ExternalOutput")
        dbg["h"] = nc.dram_tensor("dbg_h", [128, 32 * T], F32, kind="ExternalOutput")
        dbg["r2"] = nc.dram_tensor("dbg_r2", [128, KC * T], F32, kind="ExternalOutput")

    with contextlib.ExitStack() as ctx:
        S_sb = ctx.enter_context(nc.sbuf_tensor("S_sb", [128, KC * T], F32))
        ab = ctx.enter_context(nc.sbuf_tensor("ab", [128, 65536], BF16))
        sq_sb = ctx.enter_context(nc.sbuf_tensor("sq_sb", [128, 1024], F32))
        st_sb = ctx.enter_context(nc.sbuf_tensor("st_sb", [1, 1536], F32))
        scl_sb = ctx.enter_context(nc.sbuf_tensor("scl_sb", [128, 144], F32))
        d01_sb = ctx.enter_context(nc.sbuf_tensor("d01_sb", [128, 128], BF16))
        onc_sb = ctx.enter_context(nc.sbuf_tensor("onc_sb", [128, 1], F32))
        onr_sb = ctx.enter_context(nc.sbuf_tensor("onr_sb", [1, 128], F32))
        sums_sb = ctx.enter_context(nc.sbuf_tensor("sums_sb", [128, 4], F32))
        Dm_sb = ctx.enter_context(nc.sbuf_tensor("Dm_sb", [128, 256], BF16))

        def abv(off, n):
            return ab[:, off:off + n]

        x_bf_o, q_bf_o, k_bf_o, vT_o, wvres_o, wbuf_o = 0, 24576, 40960, 47104, 53248, 61440
        attn_o, probs_o, pT_o, mask_o = 0, 16384, 17664, 18944
        h_o, y1_o, wgu_o, wd_o = 0, 32768, 49152, 57344

        sg = ctx.enter_context(nc.semaphore("sg"))
        sq = ctx.enter_context(nc.semaphore("sq"))
        tm = ctx.enter_context(nc.semaphore("tm"))
        vv = ctx.enter_context(nc.semaphore("vv"))
        ss = ctx.enter_context(nc.semaphore("ss"))
        so = ctx.enter_context(nc.semaphore("so"))
        we = ctx.enter_context(nc.semaphore("we"))
        wo = ctx.enter_context(nc.semaphore("wo"))
        sems = {"sg": sg, "sq": sq, "tm": tm, "vv": vv, "ss": ss, "so": so,
                "we": we, "wo": wo}
        cnt = {k: 0 for k in sems}

        def inc(instr, name, step=1):
            cnt[name] += step
            instr.then_inc(sems[name], step)
            return cnt[name]

        block = ctx.enter_context(nc.Block())

        _bar_prev = [{k: 0 for k in sems}]

        def barrier():
            snap = {k: cnt[k] for k in sems}
            prev = _bar_prev[0]
            own_of = {"tm": block.tensor, "vv": block.vector, "ss": block.scalar}
            for own, engfn in (("tm", block.tensor), ("vv", block.vector),
                               ("ss", block.scalar), (None, block.gpsimd),
                               (None, block.sync)):
                def _f(e, snap=snap, own=own, prev=dict(prev)):
                    for k, v in snap.items():
                        if v > 0 and k != own and v > prev[k]:
                            e.wait_ge(sems[k], v)
                engfn(_f)
            _bar_prev[0] = snap


        def finish_prefix():
            @block.sync
            def _(sy):
                sy.wait_ge(so, cnt["so"])
        @block.sync
        def _(sy):
            inc(sy.dma_start(S_sb[:, :], x_own[:, :]), "sq", 16)
            inc(sy.dma_start(scl_sb[:, :], scales[:, :]), "sq", 16)
            inc(sy.dma_start(d01_sb[:, :], d01_h[:, :]), "sq", 16)
            inc(sy.dma_start(onc_sb[:, :], ones_c[:, :]), "sq", 16)
            inc(sy.dma_start(onr_sb[:1, :], ones_r[:1, :]), "sq", 16)
        p0_sq = cnt["sq"]

        @block.gpsimd
        def _(g):
            inc(g.dma_start(abv(x_bf_o, KC * TK), x_all[:, :]), "sg", 16)
            for k in range(KC):
                inc(g.dma_start(abv(wvres_o + 512 * k, 512), wv_ks[k, :, :]), "sg", 16)
        wv_ld = cnt["sg"]

        with nc.psum_tensor("psA", [128, 512], F32) as psA, \
             nc.psum_tensor("psB", [128, 512], F32) as psB, \
             nc.psum_tensor("psQ0", [128, 1024], F32) as psQ0, \
             nc.psum_tensor("psQ1", [128, 1024], F32) as psQ1:
            psV = [psA, psB]
            psQ = [psQ0, psQ1]

            vt_mm = [None] * NKT
            vt_ev = [None] * NKT
            for j in range(NKT):
                @block.tensor
                def _(t, j=j):
                    if j == 0:
                        t.wait_ge(sg, wv_ld)
                    if j >= 2:
                        t.wait_ge(vv, vt_ev[j - 2])
                    ps = psV[j % 2]
                    last = None
                    for k in range(KC):
                        last = t.matmul(ps[:, :], abv(x_bf_o + k * TK + j * 128, 128),
                                        abv(wvres_o + 512 * k, 512),
                                        start=(k == 0), stop=(k == KC - 1))
                    vt_mm[j] = inc(last, "tm")

                @block.vector
                def _(v, j=j):
                    v.wait_ge(tm, vt_mm[j])
                    vt_ev[j] = inc(v.tensor_copy(abv(vT_o + 512 * j, 512), psV[j % 2][:, :]), "vv")

            def proj(noc, wsrc, rhs_off, rhs_stride, rhs_base, ntt, out_kind, out_off):
                mm = [None] * noc
                ev = [None] * noc
                tm0 = cnt["tm"]
                for o in range(noc):
                    @block.gpsimd
                    def _(g, o=o, tm0=tm0):
                        if o >= 2:
                            g.wait_ge(tm, mm[o - 2])
                        else:
                            g.wait_ge(tm, tm0)
                        inc(g.dma_start(abv(wbuf_o + 2048 * (o % 2), 2048), wsrc[o, :, :]),
                            "we" if o % 2 == 0 else "wo", 16)
                    ld = cnt["we" if o % 2 == 0 else "wo"]

                    @block.tensor
                    def _(t, o=o, ld=ld):
                        t.wait_ge(we if o % 2 == 0 else wo, ld)
                        if o >= 2:
                            t.wait_ge(vv, ev[o - 2])
                        last = None
                        if ntt == 2:
                            ps = psQ[o % 2]
                            for tt in range(2):
                                for k in range(KC):
                                    last = t.matmul(ps[:, 512 * tt:512 * tt + 512],
                                                    abv(wbuf_o + 2048 * (o % 2) + 128 * k, 128),
                                                    abv(rhs_base + k * rhs_stride + rhs_off + 512 * tt, 512),
                                                    start=(k == 0), stop=(k == KC - 1))
                        else:
                            for tt in range(3):
                                ps = psQ[o % 2] if tt < 2 else psV[o % 2]
                                sl = ps[:, 512 * tt:512 * tt + 512] if tt < 2 else ps[:, :]
                                for k in range(KC):
                                    last = t.matmul(sl,
                                                    abv(wbuf_o + 2048 * (o % 2) + 128 * k, 128),
                                                    abv(rhs_base + k * rhs_stride + rhs_off + 512 * tt, 512),
                                                    start=(k == 0), stop=(k == KC - 1))
                        mm[o] = inc(last, "tm")

                    @block.vector
                    def _(v, o=o):
                        v.wait_ge(tm, mm[o])
                        if out_kind == "copy":
                            if ntt == 2:
                                ev[o] = inc(v.tensor_copy(abv(out_off + 1024 * o, 1024),
                                                          psQ[o % 2][:, :]), "vv")
                            else:
                                v.tensor_copy(abv(out_off + 1536 * o, 1024), psQ[o % 2][:, :])
                                ev[o] = inc(v.tensor_copy(abv(out_off + 1536 * o + 1024, 512),
                                                          psV[o % 2][:, :]), "vv")
                        else:
                            ev[o] = inc(v.tensor_tensor(S_sb[:, 1024 * o:1024 * o + 1024],
                                                        S_sb[:, 1024 * o:1024 * o + 1024],
                                                        psQ[o % 2][:, :], ALU.add), "vv")
                return ev

            proj(OC, wq_os, HALO, TK, x_bf_o, 2, "copy", q_bf_o)
            vt_guard = cnt["vv"]

            @block.tensor
            def _(t):
                t.wait_ge(vv, vt_guard)
            proj(HKV, wk_os, 0, TK, x_bf_o, 3, "copy", k_bf_o)

            if debug and "qkv" in dumps:
                @block.gpsimd
                def _(g):
                    g.wait_ge(vv, cnt["vv"])
                    inc(g.dma_start(dbg["q"][:, :], abv(q_bf_o, HQ * 1024)), "so", 16)
                    inc(g.dma_start(dbg["k"][:, :], abv(k_bf_o, HKV * 1536)), "so", 16)
                    inc(g.dma_start(dbg["vT"][:, :], abv(vT_o, NKT * 512)), "so", 16)

            barrier()

        if upto <= 1:
            finish_prefix()
            return nc

        @block.sync
        def _(sy):
            inc(sy.dma_start(abv(mask_o, 8 * 640), masks[:, :]), "sq", 16)
        mask_ld = cnt["sq"]

        with nc.psum_tensor("sc0", [128, 640], F32) as sc0, \
             nc.psum_tensor("sc1", [128, 640], F32) as sc1, \
             nc.psum_tensor("pT", [128, 640], F32) as pTp, \
             nc.psum_tensor("pv0", [128, 128], F32) as pv0, \
             nc.psum_tensor("pv1", [128, 128], F32) as pv1:
            scs = [sc0, sc1]
            pvs = [pv0, pv1]
            items = [(h, qt) for h in range(HQ) for qt in range(NQT)]
            NIT = len(items)
            ss0_attn = cnt["ss"]
            c_mm = [None] * NIT
            c_msk = [None] * NIT
            c_exp = [ss0_attn + n + 1 for n in range(NIT)]
            c_rec = [None] * NIT
            c_D = [None] * NIT
            c_tD = [None] * NIT
            c_pTe = [None] * NIT
            c_pv = [None] * NIT
            c_ae = [None] * NIT

            for n in range(NIT + 2):
                @block.tensor
                def _(t, n=n):
                    if n < NIT:
                        h, qt = items[n]
                        if n >= 2:
                            t.wait_ge(ss, c_exp[n - 2])
                        qs = q_bf_o + 1024 * h + 128 * qt
                        ks = k_bf_o + 1536 * (h // 4) + 128 * qt
                        sc = scs[n % 2]
                        t.matmul(sc[:, 0:512], abv(qs, 128), abv(ks, 512), start=True, stop=True)
                        c_mm[n] = inc(t.matmul(sc[:, 512:640], abv(qs, 128), abv(ks + 512, 128),
                                               start=True, stop=True), "tm")
                    if 0 <= n - 1 < NIT:
                        m = n - 1
                        t.wait_ge(vv, c_D[m])
                        if m >= 1:
                            t.wait_ge(vv, c_pTe[m - 1])
                        last = None
                        for b in range(5):
                            last = t.matmul(pTp[:, 128 * b:128 * b + 128],
                                            abv(probs_o + 640 * (m % 2) + 128 * b, 128),
                                            Dm_sb[:, 128 * (m % 2):128 * (m % 2) + 128],
                                            start=True, stop=True)
                        c_tD[m] = inc(last, "tm")
                    if 0 <= n - 2 < NIT:
                        m = n - 2
                        t.wait_ge(vv, c_pTe[m])
                        if m >= 2:
                            t.wait_ge(vv, c_ae[m - 2])
                        mh, mqt = items[m]
                        last = None
                        for b in range(5):
                            last = t.matmul(pvs[m % 2][:, :],
                                            abv(vT_o + 512 * (mqt + b) + 128 * (mh // 4), 128),
                                            abv(pT_o + 640 * (m % 2) + 128 * b, 128),
                                            start=(b == 0), stop=(b == 4))
                        c_pv[m] = inc(last, "tm")

                @block.vector
                def _(v, n=n):
                    if n < NIT:
                        h, qt = items[n]
                        if n == 0:
                            v.wait_ge(sq, mask_ld)
                        v.wait_ge(tm, c_mm[n])
                        c_msk[n] = inc(v.tensor_tensor(scs[n % 2][:, :], scs[n % 2][:, :],
                                                       abv(mask_o + 640 * qt, 640), ALU.add), "vv")
                    if 0 <= n - 2 < NIT:
                        m = n - 2
                        mh, mqt = items[m]
                        v.wait_ge(tm, c_pv[m])
                        c_ae[m] = inc(v.tensor_copy(abv(attn_o + 1024 * mh + 128 * mqt, 128),
                                                    pvs[m % 2][:, :]), "vv")
                    if 0 <= n - 1 < NIT:
                        m = n - 1
                        v.wait_ge(tm, c_tD[m])
                        c_pTe[m] = inc(v.tensor_copy(abv(pT_o + 640 * (m % 2), 640), pTp[:, :]), "vv")

                if n < NIT:
                    @block.vector
                    def _(v, n=n):
                        v.wait_ge(ss, c_exp[n])
                        c_rec[n] = inc(v.reciprocal(sums_sb[:, 2 * (n % 2) + 1:2 * (n % 2) + 2],
                                                    sums_sb[:, 2 * (n % 2):2 * (n % 2) + 1]), "vv")
                        v.wait_ge(vv, c_rec[n])
                        c_D[n] = inc(v.scalar_tensor_tensor(
                            Dm_sb[:, 128 * (n % 2):128 * (n % 2) + 128], d01_sb[:, :],
                            sums_sb[:, 2 * (n % 2) + 1:2 * (n % 2) + 2], d01_sb[:, :],
                            ALU.mult, ALU.mult), "vv")

            @block.scalar
            def _(s):
                s.wait_ge(sq, p0_sq)
                for n in range(NIT):
                    s.wait_ge(vv, c_msk[n])
                    if n >= 2:
                        s.wait_ge(tm, c_tD[n - 2])
                    got = inc(
                        s.activation(abv(probs_o + 640 * (n % 2), 640), scs[n % 2][:, :], AF.Exp,
                                     bias=0.0, scale=1.0 / 128.0,
                                     accum_out=sums_sb[:, 2 * (n % 2):2 * (n % 2) + 1]), "ss")
                    assert got == c_exp[n]

            if debug and "attn" in dumps:
                @block.gpsimd
                def _(g):
                    g.wait_ge(vv, cnt["vv"])
                    inc(g.dma_start(dbg["attn"][:, :], abv(attn_o, KC * T)), "so", 16)

            barrier()

        if upto <= 2:
            finish_prefix()
            return nc

        with nc.psum_tensor("psW0", [128, 1024], F32) as psW0, \
             nc.psum_tensor("psW1", [128, 1024], F32) as psW1:
            psQ = [psW0, psW1]

            def proj_wo():
                mm = [None] * OC
                ev = [None] * OC
                tm0 = cnt["tm"]
                for o in range(OC):
                    @block.gpsimd
                    def _(g, o=o, tm0=tm0):
                        if o >= 2:
                            g.wait_ge(tm, mm[o - 2])
                        else:
                            g.wait_ge(tm, tm0)
                        inc(g.dma_start(abv(wbuf_o + 2048 * (o % 2), 2048), wo_os[o, :, :]),
                            "we" if o % 2 == 0 else "wo", 16)
                    ld = cnt["we" if o % 2 == 0 else "wo"]

                    @block.tensor
                    def _(t, o=o, ld=ld):
                        t.wait_ge(we if o % 2 == 0 else wo, ld)
                        if o >= 2:
                            t.wait_ge(vv, ev[o - 2])
                        last = None
                        for tt in range(2):
                            for k in range(KC):
                                last = t.matmul(psQ[o % 2][:, 512 * tt:512 * tt + 512],
                                                abv(wbuf_o + 2048 * (o % 2) + 128 * k, 128),
                                                abv(attn_o + 1024 * k + 512 * tt, 512),
                                                start=(k == 0), stop=(k == KC - 1))
                        mm[o] = inc(last, "tm")

                    @block.vector
                    def _(v, o=o):
                        v.wait_ge(tm, mm[o])
                        ev[o] = inc(v.tensor_tensor(S_sb[:, 1024 * o:1024 * o + 1024],
                                                    S_sb[:, 1024 * o:1024 * o + 1024],
                                                    psQ[o % 2][:, :], ALU.add), "vv")
            proj_wo()
            barrier()

        if upto <= 3:
            finish_prefix()
            return nc

        _ln_id = [0]

        def emit_ln(write_bf16_to=None):
            _ln_id[0] += 1
            li = _ln_id[0]
            with nc.psum_tensor(f"psS1_{li}", [1, 512], F32) as psS1, \
                 nc.psum_tensor(f"psS2_{li}", [1, 512], F32) as psS2, \
                 nc.psum_tensor(f"psB1_{li}", [128, 512], F32) as psB1, \
                 nc.psum_tensor(f"psB2_{li}", [128, 512], F32) as psB2:
                for tt in range(2):
                    @block.tensor
                    def _(t, tt=tt):
                        t.wait_ge(sq, p0_sq)
                        last = None
                        for k in range(KC):
                            last = t.matmul(psS1[:1, :], onc_sb[:, :1],
                                            S_sb[:, 1024 * k + 512 * tt:1024 * k + 512 * tt + 512],
                                            start=(k == 0), stop=(k == KC - 1))
                        inc(last, "tm")
                    tm0_sq = cnt["tm"]
                    c_smm = [tm0_sq + k + 1 for k in range(KC)]
                    ss0_sq = cnt["ss"]
                    c_sq = [ss0_sq + k + 1 for k in range(KC)]

                    @block.scalar
                    def _(s, tt=tt):
                        for k in range(KC):
                            if k >= 2:
                                s.wait_ge(tm, c_smm[k - 2])
                            got = inc(
                                s.activation(sq_sb[:, 512 * (k % 2):512 * (k % 2) + 512],
                                             S_sb[:, 1024 * k + 512 * tt:1024 * k + 512 * tt + 512],
                                             AF.Square), "ss")
                            assert got == c_sq[k]

                    @block.tensor
                    def _(t):
                        for k in range(KC):
                            t.wait_ge(ss, c_sq[k])
                            got = t.matmul(psS2[:1, :], onc_sb[:, :1],
                                           sq_sb[:, 512 * (k % 2):512 * (k % 2) + 512],
                                           start=(k == 0), stop=(k == KC - 1))
                            assert inc(got, "tm") == c_smm[k]
                    tm_sums = cnt["tm"]

                    @block.vector
                    def _(v):
                        v.wait_ge(tm, tm_sums)
                        inc(v.tensor_scalar(st_sb[:1, 0:512], psS1[:1, :], 1.0 / D, None, ALU.mult), "vv")
                    c1 = cnt["vv"]

                    @block.scalar
                    def _(s):
                        s.wait_ge(vv, c1)
                        inc(s.activation(st_sb[:1, 1024:1536], st_sb[:1, 0:512], AF.Square), "ss")
                    c2 = cnt["ss"]

                    @block.vector
                    def _(v):
                        v.wait_ge(ss, c2)
                        inc(v.scalar_tensor_tensor(st_sb[:1, 512:1024], psS2[:1, :], 1.0 / D,
                                                   st_sb[:1, 1024:1536], ALU.mult, ALU.subtract), "vv")
                    c3 = cnt["vv"]

                    @block.scalar
                    def _(s):
                        s.wait_ge(vv, c3)
                        inc(s.activation(st_sb[:1, 512:1024], st_sb[:1, 512:1024],
                                         AF.Copy, bias=EPS), "ss")
                    c4 = cnt["ss"]

                    @block.vector
                    def _(v):
                        v.wait_ge(ss, c4)
                        inc(v.reciprocal(st_sb[:1, 512:1024], st_sb[:1, 512:1024]), "vv")
                    vv_rec = cnt["vv"]

                    @block.scalar
                    def _(s):
                        s.wait_ge(vv, vv_rec)
                        inc(s.activation(st_sb[:1, 512:1024], st_sb[:1, 512:1024], AF.Sqrt), "ss")
                    ss_rstd = cnt["ss"]

                    @block.vector
                    def _(v):
                        v.wait_ge(ss, ss_rstd)
                        inc(v.tensor_tensor(st_sb[:1, 1024:1536], st_sb[:1, 0:512],
                                            st_sb[:1, 512:1024], ALU.mult), "vv")
                    vv_mr = cnt["vv"]

                    @block.tensor
                    def _(t):
                        t.wait_ge(vv, vv_mr)
                        t.matmul(psB1[:, :], onr_sb[:1, :], st_sb[:1, 512:1024], start=True, stop=True)
                        inc(t.matmul(psB2[:, :], onr_sb[:1, :], st_sb[:1, 1024:1536], start=True, stop=True), "tm")
                    tm_bc = cnt["tm"]

                    ln_done = [None] * KC

                    @block.vector
                    def _(v, tt=tt, ln_done=ln_done):
                        v.wait_ge(tm, tm_bc)
                        for k in range(KC):
                            sl = S_sb[:, 1024 * k + 512 * tt:1024 * k + 512 * tt + 512]
                            inc(v.tensor_tensor(sl, sl, psB1[:, :], ALU.mult), "vv")
                        mid = cnt["vv"]
                        v.wait_ge(vv, mid)
                        for k in range(KC):
                            sl = S_sb[:, 1024 * k + 512 * tt:1024 * k + 512 * tt + 512]
                            ln_done[k] = inc(v.tensor_tensor(sl, sl, psB2[:, :], ALU.subtract), "vv")

                    if write_bf16_to is not None:
                        @block.scalar
                        def _(s, tt=tt, ln_done=ln_done):
                            for k in range(KC):
                                s.wait_ge(vv, ln_done[k])
                                inc(s.copy(abv(write_bf16_to + 1024 * k + 512 * tt, 512),
                                           S_sb[:, 1024 * k + 512 * tt:1024 * k + 512 * tt + 512]), "ss")

                    barrier()

        emit_ln(write_bf16_to=y1_o)

        if debug and "y1" in dumps:
            @block.sync
            def _(sy):
                sy.wait_ge(ss, cnt["ss"])
                inc(sy.dma_start(dbg["y1"][:, :], S_sb[:, :]), "so", 16)
            barrier()

        if upto <= 4:
            finish_prefix()
            return nc

        for half in range(2):
            with nc.psum_tensor(f"psG0_{half}", [128, 1024], F32) as psG0, \
                 nc.psum_tensor(f"psU0_{half}", [128, 1024], F32) as psU0, \
                 nc.psum_tensor(f"psG1_{half}", [128, 1024], F32) as psG1, \
                 nc.psum_tensor(f"psU1_{half}", [128, 1024], F32) as psU1:
                psG = [psG0, psG1]
                psU = [psU0, psU1]
                c_ms = [None] * 32
                c_g = [[None, None] for _ in range(32)]
                c_u = [None] * 32
                ss0_ffn = cnt["ss"]
                c_sil = [[ss0_ffn + 2 * i + tt + 1 for tt in range(2)] for i in range(32)]
                c_h = [[None, None] for _ in range(32)]

                for i in range(32):
                    ii = 32 * half + i

                    @block.gpsimd
                    def _(g, i=i, ii=ii):
                        if i >= 2:
                            g.wait_ge(tm, c_u[i - 2])
                        pn = "we" if i % 2 == 0 else "wo"
                        cload = inc(g.dma_start(abv(wgu_o + 4096 * (i % 2), 4096), negc[:, :]), pn, 16)
                        g.wait_ge(sems[pn], cload)
                        inc(g.dma_start(abv(wgu_o + 4096 * (i % 2), 2048),
                                        wg_is[ii, :, :], accum_op=ALU.add), pn, 16)
                        inc(g.dma_start(abv(wgu_o + 4096 * (i % 2) + 2048, 2048),
                                        wu_is[ii, :, :], accum_op=ALU.add), pn, 16)
                    ld = cnt["we" if i % 2 == 0 else "wo"]

                    @block.tensor
                    def _(t, i=i, ld=ld):
                        t.wait_ge(we if i % 2 == 0 else wo, ld)
                        if i >= 2:
                            t.wait_ge(ss, c_sil[i - 2][1])
                            t.wait_ge(vv, c_h[i - 2][1])
                        for tt in range(2):
                            last = None
                            for k in range(KC):
                                last = t.matmul(psG[i % 2][:, 512 * tt:512 * tt + 512],
                                                abv(wgu_o + 4096 * (i % 2) + 128 * k, 128),
                                                abv(y1_o + 1024 * k + 512 * tt, 512),
                                                start=(k == 0), stop=(k == KC - 1))
                            c_g[i][tt] = inc(last, "tm")
                        last = None
                        for tt in range(2):
                            for k in range(KC):
                                last = t.matmul(psU[i % 2][:, 512 * tt:512 * tt + 512],
                                                abv(wgu_o + 4096 * (i % 2) + 2048 + 128 * k, 128),
                                                abv(y1_o + 1024 * k + 512 * tt, 512),
                                                start=(k == 0), stop=(k == KC - 1))
                        c_u[i] = inc(last, "tm")

                    for tt in range(2):
                        @block.vector
                        def _(v, i=i, ii=ii, tt=tt):
                            v.wait_ge(ss, c_sil[i][tt])
                            v.wait_ge(tm, c_u[i])
                            c_h[i][tt] = inc(
                                v.scalar_tensor_tensor(abv(h_o + 1024 * i + 512 * tt, 512),
                                                       psU[i % 2][:, 512 * tt:512 * tt + 512],
                                                       scl_sb[:, 64 + ii:64 + ii + 1],
                                                       sq_sb[:, 512 * tt:512 * tt + 512],
                                                       ALU.mult, ALU.mult), "vv")

                @block.scalar
                def _(s, half=half):
                    for i in range(32):
                        ii = 32 * half + i
                        for tt in range(2):
                            s.wait_ge(tm, c_g[i][tt])
                            if i >= 1:
                                s.wait_ge(vv, c_h[i - 1][tt])
                            got = inc(
                                s.activation(sq_sb[:, 512 * tt:512 * tt + 512],
                                             psG[i % 2][:, 512 * tt:512 * tt + 512], AF.Silu,
                                             bias=0.0, scale=scl_sb[:, ii:ii + 1]), "ss")
                            assert got == c_sil[i][tt]

                barrier()

            if debug and "h" in dumps and half == 0:
                @block.gpsimd
                def _(g):
                    g.wait_ge(vv, cnt["vv"])
                    inc(g.dma_start(dbg["h"][:, :], abv(h_o, 32 * 1024)), "so", 16)
                barrier()

            if upto <= 5 and half == 0:
                break

            with nc.psum_tensor(f"psD0_{half}", [128, 1024], F32) as psD0, \
                 nc.psum_tensor(f"psD1_{half}", [128, 1024], F32) as psD1:
                psD = [psD0, psD1]
                c_wms = [None] * OC
                d_mm = [None] * OC
                d_ev = [None] * OC

                for o in range(OC):
                    @block.gpsimd
                    def _(g, o=o, half=half):
                        if o >= 2:
                            g.wait_ge(tm, d_mm[o - 2])
                        pn = "we" if o % 2 == 0 else "wo"
                        cload = inc(g.dma_start(abv(wd_o + 4096 * (o % 2), 4096), negc[:, :]), pn, 16)
                        g.wait_ge(sems[pn], cload)
                        inc(g.dma_start(abv(wd_o + 4096 * (o % 2), 2048),
                                        wd_os[half, o, :, 0:2048], accum_op=ALU.add), pn, 16)
                        inc(g.dma_start(abv(wd_o + 4096 * (o % 2) + 2048, 2048),
                                        wd_os[half, o, :, 2048:4096], accum_op=ALU.add), pn, 16)
                    ld = cnt["we" if o % 2 == 0 else "wo"]

                    @block.tensor
                    def _(t, o=o, ld=ld):
                        t.wait_ge(we if o % 2 == 0 else wo, ld)
                        if o >= 2:
                            t.wait_ge(vv, d_ev[o - 2])
                        last = None
                        for tt in range(2):
                            for i in range(32):
                                last = t.matmul(psD[o % 2][:, 512 * tt:512 * tt + 512],
                                                abv(wd_o + 4096 * (o % 2) + 128 * i, 128),
                                                abv(h_o + 1024 * i + 512 * tt, 512),
                                                start=(i == 0), stop=(i == 31))
                        d_mm[o] = inc(last, "tm")

                    @block.vector
                    def _(v, o=o):
                        v.wait_ge(tm, d_mm[o])
                        d_ev[o] = inc(
                            v.scalar_tensor_tensor(S_sb[:, 1024 * o:1024 * o + 1024],
                                                   psD[o % 2][:, :],
                                                   scl_sb[:, 128 + o:128 + o + 1],
                                                   S_sb[:, 1024 * o:1024 * o + 1024],
                                                   ALU.mult, ALU.add), "vv")

                barrier()
            if upto <= 6 and half == 0:
                break

        if upto <= 6:
            finish_prefix()
            return nc

        if debug and "r2" in dumps:
            @block.sync
            def _(sy):
                sy.wait_ge(vv, cnt["vv"])
                inc(sy.dma_start(dbg["r2"][:, :], S_sb[:, :]), "so", 16)
            barrier()

        emit_ln(write_bf16_to=None)

        @block.sync
        def _(sy):
            sy.wait_ge(vv, cnt["vv"])
            inc(sy.dma_start(y_out[:, :], S_sb[:, :]), "so", 16)
            sy.wait_ge(so, cnt["so"])

    return nc


def _prep_shared(inp):
    Wq = np.asarray(inp["L0_attn_Wq"])
    Wk = np.asarray(inp["L0_attn_Wk"])
    Wv = np.asarray(inp["L0_attn_Wv"])
    Wo = np.asarray(inp["L0_attn_Wo"])
    Wg = np.asarray(inp["L0_ffn_gate_w"])
    Wu = np.asarray(inp["L0_ffn_up_w"])
    Wd = np.asarray(inp["L0_ffn_down_w"])
    gs = np.asarray(inp["L0_ffn_gate_scale"])
    us = np.asarray(inp["L0_ffn_up_scale"])
    ds = np.asarray(inp["L0_ffn_down_scale"])

    def oslab(W, oc):
        return np.ascontiguousarray(
            W.reshape(KC, 128, oc, 128).transpose(2, 1, 0, 3).reshape(oc, 128, KC * 128))

    wq_os = oslab(Wq, OC).astype(np.float32)
    wk_os = oslab(Wk, HKV).astype(np.float32)
    wo_os = oslab(Wo, OC).astype(np.float32)
    wv_ks = np.ascontiguousarray(Wv.reshape(KC, 128, 512)).astype(np.float32)
    wg_is = np.ascontiguousarray(
        Wg.reshape(IC, 128, KC, 128).transpose(0, 3, 2, 1).reshape(IC, 128, KC * 128)).astype(np.int32)
    wu_is = np.ascontiguousarray(
        Wu.reshape(IC, 128, KC, 128).transpose(0, 3, 2, 1).reshape(IC, 128, KC * 128)).astype(np.int32)
    wd_os = np.ascontiguousarray(
        Wd.reshape(OC, 128, 2, 32, 128).transpose(2, 0, 4, 3, 1).reshape(2, OC, 128, 32 * 128)).astype(np.int32)
    scales = np.concatenate([gs.reshape(IC, 128).T, us.reshape(IC, 128).T,
                             ds.reshape(OC, 128).T], axis=1).astype(np.float32)
    d01 = np.eye(128).astype(ml_dtypes.bfloat16)
    ones_c = np.ones((128, 1), np.float32)
    ones_r = np.ones((1, 128), np.float32)
    negc = np.full((128, 4096), -128.0, dtype=ml_dtypes.bfloat16)
    return dict(negc=negc, wq_os=wq_os, wk_os=wk_os, wv_ks=wv_ks, wo_os=wo_os,
                wg_is=wg_is, wu_is=wu_is, wd_os=wd_os, scales=scales,
                d01=d01, ones_c=ones_c, ones_r=ones_r)


def _mk_masks(part):
    r = np.arange(128)[:, None]
    c = np.arange(640)[None, :]
    band = (c > r) & (c <= r + 512)
    m = np.empty((NQT, 128, 640), np.float32)
    for qt in range(NQT):
        valid = band.copy()
        if part == 0:
            valid = valid & (c >= 512 - 128 * qt)
        m[qt] = np.where(valid, 0.0, NEG)
    return np.ascontiguousarray(m.transpose(1, 0, 2).reshape(128, NQT * 640)).astype(ml_dtypes.bfloat16)


def _prep_core(x, c):
    b, part = c // 4, c % 4
    tok0 = T * part
    xt = np.ascontiguousarray(np.asarray(x)[b].T)
    x_own = np.ascontiguousarray(
        xt[:, tok0:tok0 + T].reshape(KC, 128, T).transpose(1, 0, 2).reshape(128, KC * T)).astype(np.float32)
    win = np.zeros((D, TK), np.float32)
    lo = max(0, tok0 - HALO)
    win[:, HALO - (tok0 - lo):HALO] = xt[:, lo:tok0]
    win[:, HALO:] = xt[:, tok0:tok0 + T]
    x_all = np.ascontiguousarray(
        win.reshape(KC, 128, TK).transpose(1, 0, 2).reshape(128, KC * TK)).astype(np.float32)
    return dict(x_own=x_own, x_all=x_all, masks=_mk_masks(part))


def kernel(**inputs):
    shared = _prep_shared(inputs)
    x = inputs["L0_attn_hidden_states"]
    in_maps = []
    for c in range(NCORES):
        m = dict(shared)
        m.update(_prep_core(x, c))
        in_maps.append(m)
    nc = build_nc(debug=False)
    res = run_bass_kernel_spmd(nc, in_maps, core_ids=list(range(NCORES)))
    out = np.empty((B, S, D), np.float32)
    for c in range(NCORES):
        b, part = c // 4, c % 4
        y = np.asarray(res.results[c]["y"]).reshape(128, KC, T).transpose(1, 0, 2).reshape(D, T)
        out[b, T * part:T * part + T, :] = y.T
    return out


# version 05
# speedup vs baseline: 1.8370x; 1.1413x over previous
import contextlib
import numpy as np
import ml_dtypes
import concourse.bass as bass
import concourse.mybir as mybir
from concourse.bass_utils import run_bass_kernel_spmd

F32 = mybir.dt.float32
BF16 = mybir.dt.bfloat16
I32 = mybir.dt.int32
AF = mybir.ActivationFunctionType
ALU = mybir.AluOpType

B, S, D = 2, 4096, 2048
HQ, HKV, HD, WIN = 16, 4, 128, 512
NCORES = 8
T = 1024
HALO = 512
TK = T + HALO
KC = D // 128
OC = D // 128
IC = 8192 // 128
NQT = T // 128
NKT = TK // 128
EPS = 1e-5
NEG = -1.0e30


def build_nc(debug=False, upto=99, dumps=None):
    if dumps is None:
        dumps = {"qkv", "attn", "y1", "h", "r2"} if debug else set()
    nc = bass.Bass("TRN2", target_bir_lowering=False)

    x_own = nc.dram_tensor("x_own", [128, KC * T], F32, kind="ExternalInput")
    x_all = nc.dram_tensor("x_all", [128, KC * TK], F32, kind="ExternalInput")
    masks = nc.dram_tensor("masks", [128, NQT * 640], BF16, kind="ExternalInput")
    wq_os = nc.dram_tensor("wq_os", [OC, 128, 2048], F32, kind="ExternalInput")
    wk_os = nc.dram_tensor("wk_os", [HKV, 128, 2048], F32, kind="ExternalInput")
    wv_ks = nc.dram_tensor("wv_ks", [KC, 128, 512], F32, kind="ExternalInput")
    wo_os = nc.dram_tensor("wo_os", [OC, 128, 2048], F32, kind="ExternalInput")
    wg_is = nc.dram_tensor("wg_is", [IC, 128, 2048], I32, kind="ExternalInput")
    wu_is = nc.dram_tensor("wu_is", [IC, 128, 2048], I32, kind="ExternalInput")
    wd_os = nc.dram_tensor("wd_os", [2, OC, 128, 4096], I32, kind="ExternalInput")
    scales = nc.dram_tensor("scales", [128, 144], F32, kind="ExternalInput")
    d01_h = nc.dram_tensor("d01", [128, 128], BF16, kind="ExternalInput")
    ones_c = nc.dram_tensor("ones_c", [128, 1], F32, kind="ExternalInput")
    ones_r = nc.dram_tensor("ones_r", [1, 128], F32, kind="ExternalInput")
    negrow = nc.dram_tensor("negrow", [1, 128], F32, kind="ExternalInput")
    ones_cb = nc.dram_tensor("ones_cb", [128, 1], BF16, kind="ExternalInput")
    y_out = nc.dram_tensor("y", [128, KC * T], F32, kind="ExternalOutput")

    dbg = {}
    if debug:
        dbg["q"] = nc.dram_tensor("dbg_q", [128, HQ * T], F32, kind="ExternalOutput")
        dbg["k"] = nc.dram_tensor("dbg_k", [128, HKV * TK], F32, kind="ExternalOutput")
        dbg["vT"] = nc.dram_tensor("dbg_vT", [128, NKT * 512], F32, kind="ExternalOutput")
        dbg["attn"] = nc.dram_tensor("dbg_attn", [128, KC * T], F32, kind="ExternalOutput")
        dbg["y1"] = nc.dram_tensor("dbg_y1", [128, KC * T], F32, kind="ExternalOutput")
        dbg["h"] = nc.dram_tensor("dbg_h", [128, 32 * T], F32, kind="ExternalOutput")
        dbg["r2"] = nc.dram_tensor("dbg_r2", [128, KC * T], F32, kind="ExternalOutput")

    with contextlib.ExitStack() as ctx:
        S_sb = ctx.enter_context(nc.sbuf_tensor("S_sb", [128, KC * T], F32))
        ab = ctx.enter_context(nc.sbuf_tensor("ab", [128, 65536], BF16))
        sq_sb = ctx.enter_context(nc.sbuf_tensor("sq_sb", [128, 1024], F32))
        st_sb = ctx.enter_context(nc.sbuf_tensor("st_sb", [1, 1536], F32))
        scl_sb = ctx.enter_context(nc.sbuf_tensor("scl_sb", [128, 144], F32))
        d01_sb = ctx.enter_context(nc.sbuf_tensor("d01_sb", [128, 128], BF16))
        onc_sb = ctx.enter_context(nc.sbuf_tensor("onc_sb", [128, 1], F32))
        onr_sb = ctx.enter_context(nc.sbuf_tensor("onr_sb", [1, 128], F32))
        sums_sb = ctx.enter_context(nc.sbuf_tensor("sums_sb", [128, 6], F32))
        Dm_sb = ctx.enter_context(nc.sbuf_tensor("Dm_sb", [128, 384], BF16))
        nr_sb = ctx.enter_context(nc.sbuf_tensor("nr_sb", [1, 128], F32))
        oncb_sb = ctx.enter_context(nc.sbuf_tensor("oncb_sb", [128, 1], BF16))

        def abv(off, n):
            return ab[:, off:off + n]

        x_bf_o, q_bf_o, k_bf_o, vT_o, wvres_o, wbuf_o = 0, 24576, 40960, 47104, 53248, 61440
        attn_o, probs_o, pT_o, mask_o = 0, 16384, 18304, 53248
        h_o, y1_o, wgu_o, wd_o = 0, 32768, 49152, 57344

        sg = ctx.enter_context(nc.semaphore("sg"))
        sq = ctx.enter_context(nc.semaphore("sq"))
        tm = ctx.enter_context(nc.semaphore("tm"))
        vv = ctx.enter_context(nc.semaphore("vv"))
        ss = ctx.enter_context(nc.semaphore("ss"))
        so = ctx.enter_context(nc.semaphore("so"))
        we = ctx.enter_context(nc.semaphore("we"))
        wo = ctx.enter_context(nc.semaphore("wo"))
        sems = {"sg": sg, "sq": sq, "tm": tm, "vv": vv, "ss": ss, "so": so,
                "we": we, "wo": wo}
        cnt = {k: 0 for k in sems}

        def inc(instr, name, step=1):
            cnt[name] += step
            instr.then_inc(sems[name], step)
            return cnt[name]

        block = ctx.enter_context(nc.Block())

        _bar_prev = [{k: 0 for k in sems}]

        def barrier():
            snap = {k: cnt[k] for k in sems}
            prev = _bar_prev[0]
            own_of = {"tm": block.tensor, "vv": block.vector, "ss": block.scalar}
            for own, engfn in (("tm", block.tensor), ("vv", block.vector),
                               ("ss", block.scalar), (None, block.gpsimd),
                               (None, block.sync)):
                def _f(e, snap=snap, own=own, prev=dict(prev)):
                    for k, v in snap.items():
                        if v > 0 and k != own and v > prev[k]:
                            e.wait_ge(sems[k], v)
                engfn(_f)
            _bar_prev[0] = snap


        def finish_prefix():
            @block.sync
            def _(sy):
                sy.wait_ge(so, cnt["so"])
        @block.sync
        def _(sy):
            inc(sy.dma_start(S_sb[:, :], x_own[:, :]), "sq", 16)
            inc(sy.dma_start(scl_sb[:, :], scales[:, :]), "sq", 16)
            inc(sy.dma_start(d01_sb[:, :], d01_h[:, :]), "sq", 16)
            inc(sy.dma_start(onc_sb[:, :], ones_c[:, :]), "sq", 16)
            inc(sy.dma_start(onr_sb[:1, :], ones_r[:1, :]), "sq", 16)
            inc(sy.dma_start(nr_sb[:1, :], negrow[:1, :]), "sq", 16)
            inc(sy.dma_start(oncb_sb[:, :], ones_cb[:, :]), "sq", 16)
        p0_sq = cnt["sq"]

        @block.gpsimd
        def _(g):
            inc(g.dma_start(abv(x_bf_o, KC * TK), x_all[:, :]), "sg", 16)
            for k in range(KC):
                inc(g.dma_start(abv(wvres_o + 512 * k, 512), wv_ks[k, :, :]), "sg", 16)
        wv_ld = cnt["sg"]

        with nc.psum_tensor("psA", [128, 512], F32) as psA, \
             nc.psum_tensor("psB", [128, 512], F32) as psB, \
             nc.psum_tensor("psQ0", [128, 1024], F32) as psQ0, \
             nc.psum_tensor("psQ1", [128, 1024], F32) as psQ1:
            psV = [psA, psB]
            psQ = [psQ0, psQ1]

            vt_mm = [None] * NKT
            vt_ev = [None] * NKT
            for j in range(NKT):
                @block.tensor
                def _(t, j=j):
                    if j == 0:
                        t.wait_ge(sg, wv_ld)
                    if j >= 2:
                        t.wait_ge(vv, vt_ev[j - 2])
                    ps = psV[j % 2]
                    last = None
                    for k in range(KC):
                        last = t.matmul(ps[:, :], abv(x_bf_o + k * TK + j * 128, 128),
                                        abv(wvres_o + 512 * k, 512),
                                        start=(k == 0), stop=(k == KC - 1))
                    vt_mm[j] = inc(last, "tm")

                @block.vector
                def _(v, j=j):
                    v.wait_ge(tm, vt_mm[j])
                    vt_ev[j] = inc(v.tensor_copy(abv(vT_o + 512 * j, 512), psV[j % 2][:, :]), "vv")

            def proj(noc, wsrc, rhs_off, rhs_stride, rhs_base, ntt, out_kind, out_off):
                mm = [None] * noc
                ev = [None] * noc
                tm0 = cnt["tm"]
                for o in range(noc):
                    @block.gpsimd
                    def _(g, o=o, tm0=tm0):
                        if o >= 2:
                            g.wait_ge(tm, mm[o - 2])
                        else:
                            g.wait_ge(tm, tm0)
                        inc(g.dma_start(abv(wbuf_o + 2048 * (o % 2), 2048), wsrc[o, :, :]),
                            "we" if o % 2 == 0 else "wo", 16)
                    ld = cnt["we" if o % 2 == 0 else "wo"]

                    @block.tensor
                    def _(t, o=o, ld=ld):
                        t.wait_ge(we if o % 2 == 0 else wo, ld)
                        if o >= 2:
                            t.wait_ge(vv, ev[o - 2])
                        last = None
                        if ntt == 2:
                            ps = psQ[o % 2]
                            for tt in range(2):
                                for k in range(KC):
                                    last = t.matmul(ps[:, 512 * tt:512 * tt + 512],
                                                    abv(wbuf_o + 2048 * (o % 2) + 128 * k, 128),
                                                    abv(rhs_base + k * rhs_stride + rhs_off + 512 * tt, 512),
                                                    start=(k == 0), stop=(k == KC - 1))
                        else:
                            for tt in range(3):
                                ps = psQ[o % 2] if tt < 2 else psV[o % 2]
                                sl = ps[:, 512 * tt:512 * tt + 512] if tt < 2 else ps[:, :]
                                for k in range(KC):
                                    last = t.matmul(sl,
                                                    abv(wbuf_o + 2048 * (o % 2) + 128 * k, 128),
                                                    abv(rhs_base + k * rhs_stride + rhs_off + 512 * tt, 512),
                                                    start=(k == 0), stop=(k == KC - 1))
                        mm[o] = inc(last, "tm")

                    @block.vector
                    def _(v, o=o):
                        v.wait_ge(tm, mm[o])
                        if out_kind == "copy":
                            if ntt == 2:
                                ev[o] = inc(v.tensor_copy(abv(out_off + 1024 * o, 1024),
                                                          psQ[o % 2][:, :]), "vv")
                            else:
                                v.tensor_copy(abv(out_off + 1536 * o, 1024), psQ[o % 2][:, :])
                                ev[o] = inc(v.tensor_copy(abv(out_off + 1536 * o + 1024, 512),
                                                          psV[o % 2][:, :]), "vv")
                        else:
                            ev[o] = inc(v.tensor_tensor(S_sb[:, 1024 * o:1024 * o + 1024],
                                                        S_sb[:, 1024 * o:1024 * o + 1024],
                                                        psQ[o % 2][:, :], ALU.add), "vv")
                return ev

            proj(OC, wq_os, HALO, TK, x_bf_o, 2, "copy", q_bf_o)
            vt_guard = cnt["vv"]

            @block.tensor
            def _(t):
                t.wait_ge(vv, vt_guard)
            proj(HKV, wk_os, 0, TK, x_bf_o, 3, "copy", k_bf_o)

            if debug and "qkv" in dumps:
                @block.gpsimd
                def _(g):
                    g.wait_ge(vv, cnt["vv"])
                    inc(g.dma_start(dbg["q"][:, :], abv(q_bf_o, HQ * 1024)), "so", 16)
                    inc(g.dma_start(dbg["k"][:, :], abv(k_bf_o, HKV * 1536)), "so", 16)
                    inc(g.dma_start(dbg["vT"][:, :], abv(vT_o, NKT * 512)), "so", 16)

            barrier()

        if upto <= 1:
            finish_prefix()
            return nc

        @block.sync
        def _(sy):
            inc(sy.dma_start(abv(mask_o, 8 * 640), masks[:, :]), "sq", 16)
        mask_ld = cnt["sq"]

        with nc.psum_tensor("sc0", [128, 640], F32) as sc0, \
             nc.psum_tensor("sc1", [128, 640], F32) as sc1, \
             nc.psum_tensor("pT", [128, 640], F32) as pTp, \
             nc.psum_tensor("pv0", [128, 128], F32) as pv0, \
             nc.psum_tensor("pv1", [128, 128], F32) as pv1:
            scs = [sc0, sc1]
            pvs = [pv0, pv1]
            items = [(h, qt) for h in range(HQ) for qt in range(NQT)]
            NIT = len(items)
            ss0_attn = cnt["ss"]
            c_mm = [None] * NIT
            c_msk = [None] * NIT
            c_exp = [ss0_attn + n + 1 for n in range(NIT)]
            c_rec = [None] * NIT
            c_D = [None] * NIT
            c_tD = [None] * NIT
            c_pTe = [None] * NIT
            c_pv = [None] * NIT
            c_ae = [None] * NIT

            for n in range(NIT + 3):
                @block.tensor
                def _(t, n=n):
                    if n < NIT:
                        h, qt = items[n]
                        if n >= 2:
                            t.wait_ge(ss, c_exp[n - 2])
                        qs = q_bf_o + 1024 * h + 128 * qt
                        ks = k_bf_o + 1536 * (h // 4) + 128 * qt
                        sc = scs[n % 2]
                        t.matmul(sc[:, 0:512], abv(qs, 128), abv(ks, 512), start=True, stop=True)
                        c_mm[n] = inc(t.matmul(sc[:, 512:640], abv(qs, 128), abv(ks + 512, 128),
                                               start=True, stop=True), "tm")
                    if 0 <= n - 2 < NIT:
                        m = n - 2
                        t.wait_ge(vv, c_D[m])
                        if m >= 1:
                            t.wait_ge(vv, c_pTe[m - 1])
                        last = None
                        for b in range(5):
                            last = t.matmul(pTp[:, 128 * b:128 * b + 128],
                                            abv(probs_o + 640 * (m % 3) + 128 * b, 128),
                                            Dm_sb[:, 128 * (m % 3):128 * (m % 3) + 128],
                                            start=True, stop=True)
                        c_tD[m] = inc(last, "tm")
                    if 0 <= n - 3 < NIT:
                        m = n - 3
                        t.wait_ge(vv, c_pTe[m])
                        if m >= 2:
                            t.wait_ge(vv, c_ae[m - 2])
                        mh, mqt = items[m]
                        last = None
                        for b in range(5):
                            last = t.matmul(pvs[m % 2][:, :],
                                            abv(vT_o + 512 * (mqt + b) + 128 * (mh // 4), 128),
                                            abv(pT_o + 640 * (m % 2) + 128 * b, 128),
                                            start=(b == 0), stop=(b == 4))
                        c_pv[m] = inc(last, "tm")

                @block.vector
                def _(v, n=n):
                    if n < NIT:
                        h, qt = items[n]
                        if n == 0:
                            v.wait_ge(sq, mask_ld)
                        v.wait_ge(tm, c_mm[n])
                        c_msk[n] = inc(v.tensor_tensor(scs[n % 2][:, :], scs[n % 2][:, :],
                                                       abv(mask_o + 640 * qt, 640), ALU.add), "vv")
                    if 0 <= n - 3 < NIT:
                        m = n - 3
                        mh, mqt = items[m]
                        v.wait_ge(tm, c_pv[m])
                        c_ae[m] = inc(v.tensor_copy(abv(attn_o + 1024 * mh + 128 * mqt, 128),
                                                    pvs[m % 2][:, :]), "vv")
                    if 0 <= n - 2 < NIT:
                        m = n - 2
                        v.wait_ge(tm, c_tD[m])
                        c_pTe[m] = inc(v.tensor_copy(abv(pT_o + 640 * (m % 2), 640), pTp[:, :]), "vv")
                    if n < NIT:
                        v.wait_ge(ss, c_exp[n])
                        c_rec[n] = inc(v.reciprocal(sums_sb[:, 2 * (n % 3) + 1:2 * (n % 3) + 2],
                                                    sums_sb[:, 2 * (n % 3):2 * (n % 3) + 1]), "vv")
                        v.wait_ge(vv, c_rec[n])
                        c_D[n] = inc(v.scalar_tensor_tensor(
                            Dm_sb[:, 128 * (n % 3):128 * (n % 3) + 128], d01_sb[:, :],
                            sums_sb[:, 2 * (n % 3) + 1:2 * (n % 3) + 2], d01_sb[:, :],
                            ALU.mult, ALU.mult), "vv")

            @block.scalar
            def _(s):
                s.wait_ge(sq, p0_sq)
                for n in range(NIT):
                    s.wait_ge(vv, c_msk[n])
                    if n >= 3:
                        s.wait_ge(tm, c_tD[n - 3])
                    got = inc(
                        s.activation(abv(probs_o + 640 * (n % 3), 640), scs[n % 2][:, :], AF.Exp,
                                     bias=0.0, scale=1.0 / 128.0,
                                     accum_out=sums_sb[:, 2 * (n % 3):2 * (n % 3) + 1]), "ss")
                    assert got == c_exp[n]

            if debug and "attn" in dumps:
                @block.gpsimd
                def _(g):
                    g.wait_ge(vv, cnt["vv"])
                    inc(g.dma_start(dbg["attn"][:, :], abv(attn_o, KC * T)), "so", 16)

            barrier()

        if upto <= 2:
            finish_prefix()
            return nc

        with nc.psum_tensor("psW0", [128, 1024], F32) as psW0, \
             nc.psum_tensor("psW1", [128, 1024], F32) as psW1:
            psQ = [psW0, psW1]

            def proj_wo():
                mm = [None] * OC
                ev = [None] * OC
                tm0 = cnt["tm"]
                for o in range(OC):
                    @block.gpsimd
                    def _(g, o=o, tm0=tm0):
                        if o >= 2:
                            g.wait_ge(tm, mm[o - 2])
                        else:
                            g.wait_ge(tm, tm0)
                        inc(g.dma_start(abv(wbuf_o + 2048 * (o % 2), 2048), wo_os[o, :, :]),
                            "we" if o % 2 == 0 else "wo", 16)
                    ld = cnt["we" if o % 2 == 0 else "wo"]

                    @block.tensor
                    def _(t, o=o, ld=ld):
                        t.wait_ge(we if o % 2 == 0 else wo, ld)
                        if o >= 2:
                            t.wait_ge(vv, ev[o - 2])
                        last = None
                        for tt in range(2):
                            for k in range(KC):
                                last = t.matmul(psQ[o % 2][:, 512 * tt:512 * tt + 512],
                                                abv(wbuf_o + 2048 * (o % 2) + 128 * k, 128),
                                                abv(attn_o + 1024 * k + 512 * tt, 512),
                                                start=(k == 0), stop=(k == KC - 1))
                        mm[o] = inc(last, "tm")

                    @block.vector
                    def _(v, o=o):
                        v.wait_ge(tm, mm[o])
                        ev[o] = inc(v.tensor_tensor(S_sb[:, 1024 * o:1024 * o + 1024],
                                                    S_sb[:, 1024 * o:1024 * o + 1024],
                                                    psQ[o % 2][:, :], ALU.add), "vv")
            proj_wo()
            barrier()

        if upto <= 3:
            finish_prefix()
            return nc

        _ln_id = [0]

        def emit_ln(write_bf16_to=None):
            _ln_id[0] += 1
            li = _ln_id[0]
            with nc.psum_tensor(f"psS1_{li}", [1, 512], F32) as psS1, \
                 nc.psum_tensor(f"psS2_{li}", [1, 512], F32) as psS2, \
                 nc.psum_tensor(f"psB1_{li}", [128, 512], F32) as psB1, \
                 nc.psum_tensor(f"psB2_{li}", [128, 512], F32) as psB2:
                for tt in range(2):
                    @block.tensor
                    def _(t, tt=tt):
                        t.wait_ge(sq, p0_sq)
                        last = None
                        for k in range(KC):
                            last = t.matmul(psS1[:1, :], onc_sb[:, :1],
                                            S_sb[:, 1024 * k + 512 * tt:1024 * k + 512 * tt + 512],
                                            start=(k == 0), stop=(k == KC - 1))
                        inc(last, "tm")
                    tm0_sq = cnt["tm"]
                    c_smm = [tm0_sq + k + 1 for k in range(KC)]
                    ss0_sq = cnt["ss"]
                    c_sq = [ss0_sq + k + 1 for k in range(KC)]

                    @block.scalar
                    def _(s, tt=tt):
                        for k in range(KC):
                            if k >= 2:
                                s.wait_ge(tm, c_smm[k - 2])
                            got = inc(
                                s.activation(sq_sb[:, 512 * (k % 2):512 * (k % 2) + 512],
                                             S_sb[:, 1024 * k + 512 * tt:1024 * k + 512 * tt + 512],
                                             AF.Square), "ss")
                            assert got == c_sq[k]

                    @block.tensor
                    def _(t):
                        for k in range(KC):
                            t.wait_ge(ss, c_sq[k])
                            got = t.matmul(psS2[:1, :], onc_sb[:, :1],
                                           sq_sb[:, 512 * (k % 2):512 * (k % 2) + 512],
                                           start=(k == 0), stop=(k == KC - 1))
                            assert inc(got, "tm") == c_smm[k]
                    tm_sums = cnt["tm"]

                    @block.vector
                    def _(v):
                        v.wait_ge(tm, tm_sums)
                        inc(v.tensor_scalar(st_sb[:1, 0:512], psS1[:1, :], 1.0 / D, None, ALU.mult), "vv")
                    c1 = cnt["vv"]

                    @block.scalar
                    def _(s):
                        s.wait_ge(vv, c1)
                        inc(s.activation(st_sb[:1, 1024:1536], st_sb[:1, 0:512], AF.Square), "ss")
                    c2 = cnt["ss"]

                    @block.vector
                    def _(v):
                        v.wait_ge(ss, c2)
                        inc(v.scalar_tensor_tensor(st_sb[:1, 512:1024], psS2[:1, :], 1.0 / D,
                                                   st_sb[:1, 1024:1536], ALU.mult, ALU.subtract), "vv")
                    c3 = cnt["vv"]

                    @block.scalar
                    def _(s):
                        s.wait_ge(vv, c3)
                        inc(s.activation(st_sb[:1, 512:1024], st_sb[:1, 512:1024],
                                         AF.Copy, bias=EPS), "ss")
                    c4 = cnt["ss"]

                    @block.vector
                    def _(v):
                        v.wait_ge(ss, c4)
                        inc(v.reciprocal(st_sb[:1, 512:1024], st_sb[:1, 512:1024]), "vv")
                    vv_rec = cnt["vv"]

                    @block.scalar
                    def _(s):
                        s.wait_ge(vv, vv_rec)
                        inc(s.activation(st_sb[:1, 512:1024], st_sb[:1, 512:1024], AF.Sqrt), "ss")
                    ss_rstd = cnt["ss"]

                    @block.vector
                    def _(v):
                        v.wait_ge(ss, ss_rstd)
                        inc(v.tensor_tensor(st_sb[:1, 1024:1536], st_sb[:1, 0:512],
                                            st_sb[:1, 512:1024], ALU.mult), "vv")
                    vv_mr = cnt["vv"]

                    @block.tensor
                    def _(t):
                        t.wait_ge(vv, vv_mr)
                        t.matmul(psB1[:, :], onr_sb[:1, :], st_sb[:1, 512:1024], start=True, stop=True)
                        inc(t.matmul(psB2[:, :], onr_sb[:1, :], st_sb[:1, 1024:1536], start=True, stop=True), "tm")
                    tm_bc = cnt["tm"]

                    ln_done = [None] * KC

                    @block.vector
                    def _(v, tt=tt, ln_done=ln_done):
                        v.wait_ge(tm, tm_bc)
                        for k in range(KC):
                            sl = S_sb[:, 1024 * k + 512 * tt:1024 * k + 512 * tt + 512]
                            inc(v.tensor_tensor(sl, sl, psB1[:, :], ALU.mult), "vv")
                        mid = cnt["vv"]
                        v.wait_ge(vv, mid)
                        for k in range(KC):
                            sl = S_sb[:, 1024 * k + 512 * tt:1024 * k + 512 * tt + 512]
                            ln_done[k] = inc(v.tensor_tensor(sl, sl, psB2[:, :], ALU.subtract), "vv")

                    if write_bf16_to is not None:
                        @block.scalar
                        def _(s, tt=tt, ln_done=ln_done):
                            for k in range(KC):
                                s.wait_ge(vv, ln_done[k])
                                inc(s.copy(abv(write_bf16_to + 1024 * k + 512 * tt, 512),
                                           S_sb[:, 1024 * k + 512 * tt:1024 * k + 512 * tt + 512]), "ss")

                    barrier()

        emit_ln(write_bf16_to=y1_o)

        if debug and "y1" in dumps:
            @block.sync
            def _(sy):
                sy.wait_ge(ss, cnt["ss"])
                inc(sy.dma_start(dbg["y1"][:, :], S_sb[:, :]), "so", 16)
            barrier()

        if upto <= 4:
            finish_prefix()
            return nc

        for half in range(2):
            with nc.psum_tensor(f"psG0_{half}", [128, 1024], F32) as psG0, \
                 nc.psum_tensor(f"psU0_{half}", [128, 1024], F32) as psU0, \
                 nc.psum_tensor(f"psG1_{half}", [128, 1024], F32) as psG1, \
                 nc.psum_tensor(f"psU1_{half}", [128, 1024], F32) as psU1:
                psG = [psG0, psG1]
                psU = [psU0, psU1]
                c_ms = [None] * 32
                c_g = [[None, None] for _ in range(32)]
                c_u = [None] * 32
                ss0_ffn = cnt["ss"]
                c_sil = [[ss0_ffn + 2 * i + tt + 1 for tt in range(2)] for i in range(32)]
                c_h = [[None, None] for _ in range(32)]

                for i in range(32):
                    ii = 32 * half + i

                    @block.gpsimd
                    def _(g, i=i, ii=ii):
                        if i >= 2:
                            g.wait_ge(tm, c_u[i - 2])
                        pn = "we" if i % 2 == 0 else "wo"
                        inc(g.dma_start(abv(wgu_o + 4096 * (i % 2), 2048), wg_is[ii, :, :]), pn, 16)
                        inc(g.dma_start(abv(wgu_o + 4096 * (i % 2) + 2048, 2048), wu_is[ii, :, :]), pn, 16)
                    ld = cnt["we" if i % 2 == 0 else "wo"]

                    @block.tensor
                    def _(t, i=i, ld=ld):
                        t.wait_ge(we if i % 2 == 0 else wo, ld)
                        if i >= 2:
                            t.wait_ge(ss, c_sil[i - 2][1])
                            t.wait_ge(vv, c_h[i - 2][1])
                        for tt in range(2):
                            last = None
                            for k in range(KC):
                                last = t.matmul(psG[i % 2][:, 512 * tt:512 * tt + 512],
                                                abv(wgu_o + 4096 * (i % 2) + 128 * k, 128),
                                                abv(y1_o + 1024 * k + 512 * tt, 512),
                                                start=(k == 0), stop=(k == KC - 1))
                            c_g[i][tt] = inc(last, "tm")
                        last = None
                        for tt in range(2):
                            for k in range(KC):
                                last = t.matmul(psU[i % 2][:, 512 * tt:512 * tt + 512],
                                                abv(wgu_o + 4096 * (i % 2) + 2048 + 128 * k, 128),
                                                abv(y1_o + 1024 * k + 512 * tt, 512),
                                                start=(k == 0), stop=(k == KC - 1))
                        c_u[i] = inc(last, "tm")

                    for tt in range(2):
                        @block.vector
                        def _(v, i=i, ii=ii, tt=tt):
                            v.wait_ge(ss, c_sil[i][tt])
                            v.wait_ge(tm, c_u[i])
                            c_h[i][tt] = inc(
                                v.scalar_tensor_tensor(abv(h_o + 1024 * i + 512 * tt, 512),
                                                       psU[i % 2][:, 512 * tt:512 * tt + 512],
                                                       scl_sb[:, 64 + ii:64 + ii + 1],
                                                       sq_sb[:, 512 * tt:512 * tt + 512],
                                                       ALU.mult, ALU.mult), "vv")

                @block.scalar
                def _(s, half=half):
                    for i in range(32):
                        ii = 32 * half + i
                        for tt in range(2):
                            s.wait_ge(tm, c_g[i][tt])
                            if i >= 1:
                                s.wait_ge(vv, c_h[i - 1][tt])
                            got = inc(
                                s.activation(sq_sb[:, 512 * tt:512 * tt + 512],
                                             psG[i % 2][:, 512 * tt:512 * tt + 512], AF.Silu,
                                             bias=0.0, scale=scl_sb[:, ii:ii + 1]), "ss")
                            assert got == c_sil[i][tt]

                barrier()

            if debug and "h" in dumps and half == 0:
                @block.gpsimd
                def _(g):
                    g.wait_ge(vv, cnt["vv"])
                    inc(g.dma_start(dbg["h"][:, :], abv(h_o, 32 * 1024)), "so", 16)
                barrier()

            if upto <= 5 and half == 0:
                break

            with nc.psum_tensor(f"psD0_{half}", [128, 1024], F32) as psD0, \
                 nc.psum_tensor(f"psD1_{half}", [128, 1024], F32) as psD1, \
                 nc.psum_tensor(f"psH_{half}", [1, 1024], F32) as psH:
                psD = [psD0, psD1]
                c_wms = [None] * OC
                d_mm = [None] * OC
                d_ev = [None] * OC

                @block.tensor
                def _(t, half=half):
                    last = None
                    for i in range(32):
                        t.wait_ge(vv, c_h[i][1])
                        for tt in range(2):
                            last = t.matmul(psH[:1, 512 * tt:512 * tt + 512], oncb_sb[:, :1],
                                            abv(h_o + 1024 * i + 512 * tt, 512),
                                            start=(i == 0), stop=(i == 31))
                    inc(last, "tm")
                tm_hs = cnt["tm"]

                @block.vector
                def _(v):
                    v.wait_ge(tm, tm_hs)
                    inc(v.tensor_copy(st_sb[:1, 0:1024], psH[:1, :]), "vv")
                vv_hs = cnt["vv"]

                for o in range(OC):
                    @block.gpsimd
                    def _(g, o=o, half=half):
                        if o >= 2:
                            g.wait_ge(tm, d_mm[o - 2])
                        pn = "we" if o % 2 == 0 else "wo"
                        inc(g.dma_start(abv(wd_o + 4096 * (o % 2), 4096), wd_os[half, o, :, :]), pn, 16)
                    ld = cnt["we" if o % 2 == 0 else "wo"]

                    @block.tensor
                    def _(t, o=o, ld=ld):
                        t.wait_ge(we if o % 2 == 0 else wo, ld)
                        if o >= 2:
                            t.wait_ge(vv, d_ev[o - 2])
                        if o == 0:
                            t.wait_ge(vv, vv_hs)
                        last = None
                        for tt in range(2):
                            t.matmul(psD[o % 2][:, 512 * tt:512 * tt + 512],
                                     nr_sb[:1, :], st_sb[:1, 512 * tt:512 * tt + 512],
                                     start=True, stop=False)
                            for i in range(32):
                                last = t.matmul(psD[o % 2][:, 512 * tt:512 * tt + 512],
                                                abv(wd_o + 4096 * (o % 2) + 128 * i, 128),
                                                abv(h_o + 1024 * i + 512 * tt, 512),
                                                start=False, stop=(i == 31))
                        d_mm[o] = inc(last, "tm")

                    @block.vector
                    def _(v, o=o):
                        v.wait_ge(tm, d_mm[o])
                        d_ev[o] = inc(
                            v.scalar_tensor_tensor(S_sb[:, 1024 * o:1024 * o + 1024],
                                                   psD[o % 2][:, :],
                                                   scl_sb[:, 128 + o:128 + o + 1],
                                                   S_sb[:, 1024 * o:1024 * o + 1024],
                                                   ALU.mult, ALU.add), "vv")

                barrier()
            if upto <= 6 and half == 0:
                break

        if upto <= 6:
            finish_prefix()
            return nc

        if debug and "r2" in dumps:
            @block.sync
            def _(sy):
                sy.wait_ge(vv, cnt["vv"])
                inc(sy.dma_start(dbg["r2"][:, :], S_sb[:, :]), "so", 16)
            barrier()

        emit_ln(write_bf16_to=None)

        @block.sync
        def _(sy):
            sy.wait_ge(vv, cnt["vv"])
            inc(sy.dma_start(y_out[:, :], S_sb[:, :]), "so", 16)
            sy.wait_ge(so, cnt["so"])

    return nc


def _prep_shared(inp):
    Wq = np.asarray(inp["L0_attn_Wq"])
    Wk = np.asarray(inp["L0_attn_Wk"])
    Wv = np.asarray(inp["L0_attn_Wv"])
    Wo = np.asarray(inp["L0_attn_Wo"])
    Wg = np.asarray(inp["L0_ffn_gate_w"])
    Wu = np.asarray(inp["L0_ffn_up_w"])
    Wd = np.asarray(inp["L0_ffn_down_w"])
    gs = np.asarray(inp["L0_ffn_gate_scale"])
    us = np.asarray(inp["L0_ffn_up_scale"])
    ds = np.asarray(inp["L0_ffn_down_scale"])

    def oslab(W, oc):
        return np.ascontiguousarray(
            W.reshape(KC, 128, oc, 128).transpose(2, 1, 0, 3).reshape(oc, 128, KC * 128))

    wq_os = oslab(Wq, OC).astype(np.float32)
    wk_os = oslab(Wk, HKV).astype(np.float32)
    wo_os = oslab(Wo, OC).astype(np.float32)
    wv_ks = np.ascontiguousarray(Wv.reshape(KC, 128, 512)).astype(np.float32)
    wg_is = np.ascontiguousarray(
        Wg.reshape(IC, 128, KC, 128).transpose(0, 3, 2, 1).reshape(IC, 128, KC * 128)).astype(np.int32)
    wu_is = np.ascontiguousarray(
        Wu.reshape(IC, 128, KC, 128).transpose(0, 3, 2, 1).reshape(IC, 128, KC * 128)).astype(np.int32)
    wd_os = np.ascontiguousarray(
        Wd.reshape(OC, 128, 2, 32, 128).transpose(2, 0, 4, 3, 1).reshape(2, OC, 128, 32 * 128)).astype(np.int32)
    scales = np.concatenate([gs.reshape(IC, 128).T, us.reshape(IC, 128).T,
                             ds.reshape(OC, 128).T], axis=1).astype(np.float32)
    d01 = np.eye(128).astype(ml_dtypes.bfloat16)
    ones_c = np.ones((128, 1), np.float32)
    ones_r = np.ones((1, 128), np.float32)
    negrow = np.full((1, 128), -128.0, np.float32)
    ones_cb = np.ones((128, 1), ml_dtypes.bfloat16)
    return dict(negrow=negrow, ones_cb=ones_cb, wq_os=wq_os, wk_os=wk_os, wv_ks=wv_ks, wo_os=wo_os,
                wg_is=wg_is, wu_is=wu_is, wd_os=wd_os, scales=scales,
                d01=d01, ones_c=ones_c, ones_r=ones_r)


def _mk_masks(part):
    r = np.arange(128)[:, None]
    c = np.arange(640)[None, :]
    band = (c > r) & (c <= r + 512)
    m = np.empty((NQT, 128, 640), np.float32)
    for qt in range(NQT):
        valid = band.copy()
        if part == 0:
            valid = valid & (c >= 512 - 128 * qt)
        m[qt] = np.where(valid, 0.0, NEG)
    return np.ascontiguousarray(m.transpose(1, 0, 2).reshape(128, NQT * 640)).astype(ml_dtypes.bfloat16)


def _prep_core(x, c):
    b, part = c // 4, c % 4
    tok0 = T * part
    xt = np.ascontiguousarray(np.asarray(x)[b].T)
    x_own = np.ascontiguousarray(
        xt[:, tok0:tok0 + T].reshape(KC, 128, T).transpose(1, 0, 2).reshape(128, KC * T)).astype(np.float32)
    win = np.zeros((D, TK), np.float32)
    lo = max(0, tok0 - HALO)
    win[:, HALO - (tok0 - lo):HALO] = xt[:, lo:tok0]
    win[:, HALO:] = xt[:, tok0:tok0 + T]
    x_all = np.ascontiguousarray(
        win.reshape(KC, 128, TK).transpose(1, 0, 2).reshape(128, KC * TK)).astype(np.float32)
    return dict(x_own=x_own, x_all=x_all, masks=_mk_masks(part))


def kernel(**inputs):
    shared = _prep_shared(inputs)
    x = inputs["L0_attn_hidden_states"]
    in_maps = []
    for c in range(NCORES):
        m = dict(shared)
        m.update(_prep_core(x, c))
        in_maps.append(m)
    nc = build_nc(debug=False)
    res = run_bass_kernel_spmd(nc, in_maps, core_ids=list(range(NCORES)))
    out = np.empty((B, S, D), np.float32)
    for c in range(NCORES):
        b, part = c // 4, c % 4
        y = np.asarray(res.results[c]["y"]).reshape(128, KC, T).transpose(1, 0, 2).reshape(D, T)
        out[b, T * part:T * part + T, :] = y.T
    return out


# version 06
# speedup vs baseline: 1.8470x; 1.0055x over previous
import contextlib
import numpy as np
import ml_dtypes
import concourse.bass as bass
import concourse.mybir as mybir
from concourse.bass_utils import run_bass_kernel_spmd

F32 = mybir.dt.float32
BF16 = mybir.dt.bfloat16
I32 = mybir.dt.int32
AF = mybir.ActivationFunctionType
ALU = mybir.AluOpType

B, S, D = 2, 4096, 2048
HQ, HKV, HD, WIN = 16, 4, 128, 512
NCORES = 8
T = 1024
HALO = 512
TK = T + HALO
KC = D // 128
OC = D // 128
IC = 8192 // 128
NQT = T // 128
NKT = TK // 128
EPS = 1e-5
NEG = -1.0e30


def build_nc(debug=False, upto=99, dumps=None):
    if dumps is None:
        dumps = {"qkv", "attn", "y1", "h", "r2"} if debug else set()
    nc = bass.Bass("TRN2", target_bir_lowering=False)

    x_own = nc.dram_tensor("x_own", [128, KC * T], F32, kind="ExternalInput")
    x_all = nc.dram_tensor("x_all", [128, KC * TK], F32, kind="ExternalInput")
    masks = nc.dram_tensor("masks", [128, NQT * 640], BF16, kind="ExternalInput")
    wq_os = nc.dram_tensor("wq_os", [OC, 128, 2048], F32, kind="ExternalInput")
    wk_os = nc.dram_tensor("wk_os", [HKV, 128, 2048], F32, kind="ExternalInput")
    wv_ks = nc.dram_tensor("wv_ks", [KC, 128, 512], F32, kind="ExternalInput")
    wo_os = nc.dram_tensor("wo_os", [OC, 128, 2048], F32, kind="ExternalInput")
    wg_is = nc.dram_tensor("wg_is", [IC, 128, 2048], I32, kind="ExternalInput")
    wu_is = nc.dram_tensor("wu_is", [IC, 128, 2048], I32, kind="ExternalInput")
    wd_os = nc.dram_tensor("wd_os", [2, OC, 128, 4096], I32, kind="ExternalInput")
    scales = nc.dram_tensor("scales", [128, 144], F32, kind="ExternalInput")
    d01_h = nc.dram_tensor("d01", [128, 128], BF16, kind="ExternalInput")
    ones_c = nc.dram_tensor("ones_c", [128, 1], F32, kind="ExternalInput")
    ones_r = nc.dram_tensor("ones_r", [1, 128], F32, kind="ExternalInput")
    negrow = nc.dram_tensor("negrow", [1, 128], F32, kind="ExternalInput")
    ones_cb = nc.dram_tensor("ones_cb", [128, 1], BF16, kind="ExternalInput")
    eps_row = nc.dram_tensor("eps_row", [1, 512], F32, kind="ExternalInput")
    y_out = nc.dram_tensor("y", [128, KC * T], F32, kind="ExternalOutput")

    dbg = {}
    if debug:
        dbg["q"] = nc.dram_tensor("dbg_q", [128, HQ * T], F32, kind="ExternalOutput")
        dbg["k"] = nc.dram_tensor("dbg_k", [128, HKV * TK], F32, kind="ExternalOutput")
        dbg["vT"] = nc.dram_tensor("dbg_vT", [128, NKT * 512], F32, kind="ExternalOutput")
        dbg["attn"] = nc.dram_tensor("dbg_attn", [128, KC * T], F32, kind="ExternalOutput")
        dbg["y1"] = nc.dram_tensor("dbg_y1", [128, KC * T], F32, kind="ExternalOutput")
        dbg["h"] = nc.dram_tensor("dbg_h", [128, 32 * T], F32, kind="ExternalOutput")
        dbg["r2"] = nc.dram_tensor("dbg_r2", [128, KC * T], F32, kind="ExternalOutput")

    with contextlib.ExitStack() as ctx:
        S_sb = ctx.enter_context(nc.sbuf_tensor("S_sb", [128, KC * T], F32))
        ab = ctx.enter_context(nc.sbuf_tensor("ab", [128, 65536], BF16))
        sq_sb = ctx.enter_context(nc.sbuf_tensor("sq_sb", [128, 1024], F32))
        st_sb = ctx.enter_context(nc.sbuf_tensor("st_sb", [1, 1024], F32))
        scl_sb = ctx.enter_context(nc.sbuf_tensor("scl_sb", [128, 144], F32))
        d01_sb = ctx.enter_context(nc.sbuf_tensor("d01_sb", [128, 128], BF16))
        onc_sb = ctx.enter_context(nc.sbuf_tensor("onc_sb", [128, 1], F32))
        onr_sb = ctx.enter_context(nc.sbuf_tensor("onr_sb", [1, 128], F32))
        sums_sb = ctx.enter_context(nc.sbuf_tensor("sums_sb", [128, 6], F32))
        Dm_sb = ctx.enter_context(nc.sbuf_tensor("Dm_sb", [128, 384], BF16))
        nr_sb = ctx.enter_context(nc.sbuf_tensor("nr_sb", [1, 128], F32))
        oncb_sb = ctx.enter_context(nc.sbuf_tensor("oncb_sb", [128, 1], BF16))
        eps_sb = ctx.enter_context(nc.sbuf_tensor("eps_sb", [1, 512], F32))

        def abv(off, n):
            return ab[:, off:off + n]

        x_bf_o, q_bf_o, k_bf_o, vT_o, wvres_o, wbuf_o = 0, 24576, 40960, 47104, 53248, 61440
        attn_o, probs_o, pT_o, mask_o = 0, 16384, 18304, 53248
        h_o, y1_o, wgu_o, wd_o = 0, 32768, 49152, 57344

        sg = ctx.enter_context(nc.semaphore("sg"))
        sq = ctx.enter_context(nc.semaphore("sq"))
        tm = ctx.enter_context(nc.semaphore("tm"))
        vv = ctx.enter_context(nc.semaphore("vv"))
        ss = ctx.enter_context(nc.semaphore("ss"))
        so = ctx.enter_context(nc.semaphore("so"))
        we = ctx.enter_context(nc.semaphore("we"))
        wo = ctx.enter_context(nc.semaphore("wo"))
        sems = {"sg": sg, "sq": sq, "tm": tm, "vv": vv, "ss": ss, "so": so,
                "we": we, "wo": wo}
        cnt = {k: 0 for k in sems}

        def inc(instr, name, step=1):
            cnt[name] += step
            instr.then_inc(sems[name], step)
            return cnt[name]

        block = ctx.enter_context(nc.Block())

        _bar_prev = [{k: 0 for k in sems}]

        def barrier():
            snap = {k: cnt[k] for k in sems}
            prev = _bar_prev[0]
            own_of = {"tm": block.tensor, "vv": block.vector, "ss": block.scalar}
            for own, engfn in (("tm", block.tensor), ("vv", block.vector),
                               ("ss", block.scalar), (None, block.gpsimd),
                               (None, block.sync)):
                def _f(e, snap=snap, own=own, prev=dict(prev)):
                    for k, v in snap.items():
                        if v > 0 and k != own and v > prev[k]:
                            e.wait_ge(sems[k], v)
                engfn(_f)
            _bar_prev[0] = snap


        def finish_prefix():
            @block.sync
            def _(sy):
                sy.wait_ge(so, cnt["so"])
        @block.sync
        def _(sy):
            inc(sy.dma_start(S_sb[:, :], x_own[:, :]), "sq", 16)
            inc(sy.dma_start(scl_sb[:, :], scales[:, :]), "sq", 16)
            inc(sy.dma_start(d01_sb[:, :], d01_h[:, :]), "sq", 16)
            inc(sy.dma_start(onc_sb[:, :], ones_c[:, :]), "sq", 16)
            inc(sy.dma_start(onr_sb[:1, :], ones_r[:1, :]), "sq", 16)
            inc(sy.dma_start(nr_sb[:1, :], negrow[:1, :]), "sq", 16)
            inc(sy.dma_start(oncb_sb[:, :], ones_cb[:, :]), "sq", 16)
            inc(sy.dma_start(eps_sb[:1, :], eps_row[:1, :]), "sq", 16)
        p0_sq = cnt["sq"]

        @block.gpsimd
        def _(g):
            inc(g.dma_start(abv(x_bf_o, KC * TK), x_all[:, :]), "sg", 16)
            for k in range(KC):
                inc(g.dma_start(abv(wvres_o + 512 * k, 512), wv_ks[k, :, :]), "sg", 16)
        wv_ld = cnt["sg"]

        with nc.psum_tensor("psA", [128, 512], F32) as psA, \
             nc.psum_tensor("psB", [128, 512], F32) as psB, \
             nc.psum_tensor("psQ0", [128, 1024], F32) as psQ0, \
             nc.psum_tensor("psQ1", [128, 1024], F32) as psQ1:
            psV = [psA, psB]
            psQ = [psQ0, psQ1]

            vt_mm = [None] * NKT
            vt_ev = [None] * NKT
            for j in range(NKT):
                @block.tensor
                def _(t, j=j):
                    if j == 0:
                        t.wait_ge(sg, wv_ld)
                    if j >= 2:
                        t.wait_ge(vv, vt_ev[j - 2])
                    ps = psV[j % 2]
                    last = None
                    for k in range(KC):
                        last = t.matmul(ps[:, :], abv(x_bf_o + k * TK + j * 128, 128),
                                        abv(wvres_o + 512 * k, 512),
                                        start=(k == 0), stop=(k == KC - 1))
                    vt_mm[j] = inc(last, "tm")

                @block.vector
                def _(v, j=j):
                    v.wait_ge(tm, vt_mm[j])
                    vt_ev[j] = inc(v.tensor_copy(abv(vT_o + 512 * j, 512), psV[j % 2][:, :]), "vv")

            def proj(noc, wsrc, rhs_off, rhs_stride, rhs_base, ntt, out_kind, out_off):
                mm = [None] * noc
                ev = [None] * noc
                tm0 = cnt["tm"]
                for o in range(noc):
                    @block.gpsimd
                    def _(g, o=o, tm0=tm0):
                        if o >= 2:
                            g.wait_ge(tm, mm[o - 2])
                        else:
                            g.wait_ge(tm, tm0)
                        inc(g.dma_start(abv(wbuf_o + 2048 * (o % 2), 2048), wsrc[o, :, :]),
                            "we" if o % 2 == 0 else "wo", 16)
                    ld = cnt["we" if o % 2 == 0 else "wo"]

                    @block.tensor
                    def _(t, o=o, ld=ld):
                        t.wait_ge(we if o % 2 == 0 else wo, ld)
                        if o >= 2:
                            t.wait_ge(vv, ev[o - 2])
                        last = None
                        if ntt == 2:
                            ps = psQ[o % 2]
                            for tt in range(2):
                                for k in range(KC):
                                    last = t.matmul(ps[:, 512 * tt:512 * tt + 512],
                                                    abv(wbuf_o + 2048 * (o % 2) + 128 * k, 128),
                                                    abv(rhs_base + k * rhs_stride + rhs_off + 512 * tt, 512),
                                                    start=(k == 0), stop=(k == KC - 1))
                        else:
                            for tt in range(3):
                                ps = psQ[o % 2] if tt < 2 else psV[o % 2]
                                sl = ps[:, 512 * tt:512 * tt + 512] if tt < 2 else ps[:, :]
                                for k in range(KC):
                                    last = t.matmul(sl,
                                                    abv(wbuf_o + 2048 * (o % 2) + 128 * k, 128),
                                                    abv(rhs_base + k * rhs_stride + rhs_off + 512 * tt, 512),
                                                    start=(k == 0), stop=(k == KC - 1))
                        mm[o] = inc(last, "tm")

                    @block.vector
                    def _(v, o=o):
                        v.wait_ge(tm, mm[o])
                        if out_kind == "copy":
                            if ntt == 2:
                                ev[o] = inc(v.tensor_copy(abv(out_off + 1024 * o, 1024),
                                                          psQ[o % 2][:, :]), "vv")
                            else:
                                v.tensor_copy(abv(out_off + 1536 * o, 1024), psQ[o % 2][:, :])
                                ev[o] = inc(v.tensor_copy(abv(out_off + 1536 * o + 1024, 512),
                                                          psV[o % 2][:, :]), "vv")
                        else:
                            ev[o] = inc(v.tensor_tensor(S_sb[:, 1024 * o:1024 * o + 1024],
                                                        S_sb[:, 1024 * o:1024 * o + 1024],
                                                        psQ[o % 2][:, :], ALU.add), "vv")
                return ev

            proj(OC, wq_os, HALO, TK, x_bf_o, 2, "copy", q_bf_o)
            vt_guard = cnt["vv"]

            @block.tensor
            def _(t):
                t.wait_ge(vv, vt_guard)
            proj(HKV, wk_os, 0, TK, x_bf_o, 3, "copy", k_bf_o)

            if debug and "qkv" in dumps:
                @block.gpsimd
                def _(g):
                    g.wait_ge(vv, cnt["vv"])
                    inc(g.dma_start(dbg["q"][:, :], abv(q_bf_o, HQ * 1024)), "so", 16)
                    inc(g.dma_start(dbg["k"][:, :], abv(k_bf_o, HKV * 1536)), "so", 16)
                    inc(g.dma_start(dbg["vT"][:, :], abv(vT_o, NKT * 512)), "so", 16)

            barrier()

        if upto <= 1:
            finish_prefix()
            return nc

        @block.sync
        def _(sy):
            inc(sy.dma_start(abv(mask_o, 8 * 640), masks[:, :]), "sq", 16)
        mask_ld = cnt["sq"]

        with nc.psum_tensor("sc0", [128, 640], F32) as sc0, \
             nc.psum_tensor("sc1", [128, 640], F32) as sc1, \
             nc.psum_tensor("pT", [128, 640], F32) as pTp, \
             nc.psum_tensor("pv0", [128, 128], F32) as pv0, \
             nc.psum_tensor("pv1", [128, 128], F32) as pv1:
            scs = [sc0, sc1]
            pvs = [pv0, pv1]
            items = [(h, qt) for h in range(HQ) for qt in range(NQT)]
            NIT = len(items)
            ss0_attn = cnt["ss"]
            c_mm = [None] * NIT
            c_msk = [None] * NIT
            c_exp = [None] * NIT
            c_pTe = [None] * NIT
            _ss = ss0_attn
            for n in range(NIT + 2):
                if n < NIT:
                    _ss += 1
                    c_exp[n] = _ss
                if 0 <= n - 2 < NIT:
                    _ss += 1
                    c_pTe[n - 2] = _ss
            c_rec = [None] * NIT
            c_D = [None] * NIT
            c_tD = [None] * NIT
            c_pv = [None] * NIT
            c_ae = [None] * NIT

            for n in range(NIT + 3):
                @block.tensor
                def _(t, n=n):
                    if n < NIT:
                        h, qt = items[n]
                        if n >= 2:
                            t.wait_ge(ss, c_exp[n - 2])
                        qs = q_bf_o + 1024 * h + 128 * qt
                        ks = k_bf_o + 1536 * (h // 4) + 128 * qt
                        sc = scs[n % 2]
                        t.matmul(sc[:, 0:512], abv(qs, 128), abv(ks, 512), start=True, stop=True)
                        c_mm[n] = inc(t.matmul(sc[:, 512:640], abv(qs, 128), abv(ks + 512, 128),
                                               start=True, stop=True), "tm")
                    if 0 <= n - 2 < NIT:
                        m = n - 2
                        t.wait_ge(vv, c_D[m])
                        if m >= 1:
                            t.wait_ge(ss, c_pTe[m - 1])
                        last = None
                        for b in range(5):
                            last = t.matmul(pTp[:, 128 * b:128 * b + 128],
                                            abv(probs_o + 640 * (m % 3) + 128 * b, 128),
                                            Dm_sb[:, 128 * (m % 3):128 * (m % 3) + 128],
                                            start=True, stop=True)
                        c_tD[m] = inc(last, "tm")
                    if 0 <= n - 3 < NIT:
                        m = n - 3
                        t.wait_ge(ss, c_pTe[m])
                        if m >= 2:
                            t.wait_ge(vv, c_ae[m - 2])
                        mh, mqt = items[m]
                        last = None
                        for b in range(5):
                            last = t.matmul(pvs[m % 2][:, :],
                                            abv(vT_o + 512 * (mqt + b) + 128 * (mh // 4), 128),
                                            abv(pT_o + 640 * (m % 2) + 128 * b, 128),
                                            start=(b == 0), stop=(b == 4))
                        c_pv[m] = inc(last, "tm")

                @block.vector
                def _(v, n=n):
                    if n < NIT:
                        h, qt = items[n]
                        if n == 0:
                            v.wait_ge(sq, mask_ld)
                        v.wait_ge(tm, c_mm[n])
                        c_msk[n] = inc(v.tensor_tensor(scs[n % 2][:, :], scs[n % 2][:, :],
                                                       abv(mask_o + 640 * qt, 640), ALU.add), "vv")
                    if 0 <= n - 3 < NIT:
                        m = n - 3
                        mh, mqt = items[m]
                        v.wait_ge(tm, c_pv[m])
                        c_ae[m] = inc(v.tensor_copy(abv(attn_o + 1024 * mh + 128 * mqt, 128),
                                                    pvs[m % 2][:, :]), "vv")
                    if n < NIT:
                        v.wait_ge(ss, c_exp[n])
                        c_rec[n] = inc(v.reciprocal(sums_sb[:, 2 * (n % 3) + 1:2 * (n % 3) + 2],
                                                    sums_sb[:, 2 * (n % 3):2 * (n % 3) + 1]), "vv")
                        v.wait_ge(vv, c_rec[n])
                        c_D[n] = inc(v.scalar_tensor_tensor(
                            Dm_sb[:, 128 * (n % 3):128 * (n % 3) + 128], d01_sb[:, :],
                            sums_sb[:, 2 * (n % 3) + 1:2 * (n % 3) + 2], d01_sb[:, :],
                            ALU.mult, ALU.mult), "vv")

            @block.scalar
            def _(s):
                s.wait_ge(sq, p0_sq)
                for n in range(NIT + 2):
                    if n < NIT:
                        s.wait_ge(vv, c_msk[n])
                        if n >= 3:
                            s.wait_ge(tm, c_tD[n - 3])
                        got = inc(
                            s.activation(abv(probs_o + 640 * (n % 3), 640), scs[n % 2][:, :], AF.Exp,
                                         bias=0.0, scale=1.0 / 128.0,
                                         accum_out=sums_sb[:, 2 * (n % 3):2 * (n % 3) + 1]), "ss")
                        assert got == c_exp[n]
                    if 0 <= n - 2 < NIT:
                        m = n - 2
                        s.wait_ge(tm, c_tD[m])
                        got = inc(s.copy(abv(pT_o + 640 * (m % 2), 640), pTp[:, :]), "ss")
                        assert got == c_pTe[m]

            if debug and "attn" in dumps:
                @block.gpsimd
                def _(g):
                    g.wait_ge(vv, cnt["vv"])
                    inc(g.dma_start(dbg["attn"][:, :], abv(attn_o, KC * T)), "so", 16)

            barrier()

        if upto <= 2:
            finish_prefix()
            return nc

        with nc.psum_tensor("psW0", [128, 1024], F32) as psW0, \
             nc.psum_tensor("psW1", [128, 1024], F32) as psW1:
            psQ = [psW0, psW1]

            def proj_wo():
                mm = [None] * OC
                ev = [None] * OC
                tm0 = cnt["tm"]
                for o in range(OC):
                    @block.gpsimd
                    def _(g, o=o, tm0=tm0):
                        if o >= 2:
                            g.wait_ge(tm, mm[o - 2])
                        else:
                            g.wait_ge(tm, tm0)
                        inc(g.dma_start(abv(wbuf_o + 2048 * (o % 2), 2048), wo_os[o, :, :]),
                            "we" if o % 2 == 0 else "wo", 16)
                    ld = cnt["we" if o % 2 == 0 else "wo"]

                    @block.tensor
                    def _(t, o=o, ld=ld):
                        t.wait_ge(we if o % 2 == 0 else wo, ld)
                        if o >= 2:
                            t.wait_ge(vv, ev[o - 2])
                        last = None
                        for tt in range(2):
                            for k in range(KC):
                                last = t.matmul(psQ[o % 2][:, 512 * tt:512 * tt + 512],
                                                abv(wbuf_o + 2048 * (o % 2) + 128 * k, 128),
                                                abv(attn_o + 1024 * k + 512 * tt, 512),
                                                start=(k == 0), stop=(k == KC - 1))
                        mm[o] = inc(last, "tm")

                    @block.vector
                    def _(v, o=o):
                        v.wait_ge(tm, mm[o])
                        ev[o] = inc(v.tensor_tensor(S_sb[:, 1024 * o:1024 * o + 1024],
                                                    S_sb[:, 1024 * o:1024 * o + 1024],
                                                    psQ[o % 2][:, :], ALU.add), "vv")
            proj_wo()
            barrier()

        if upto <= 3:
            finish_prefix()
            return nc

        _ln_id = [0]

        def emit_ln(write_bf16_to=None):
            _ln_id[0] += 1
            li = _ln_id[0]
            with nc.psum_tensor(f"psS1_{li}", [1, 1024], F32) as psS1, \
                 nc.psum_tensor(f"psS2_{li}", [1, 1024], F32) as psS2, \
                 nc.psum_tensor(f"psB1_{li}", [128, 1024], F32) as psB1, \
                 nc.psum_tensor(f"psB2_{li}", [128, 1024], F32) as psB2:
                @block.tensor
                def _(t):
                    t.wait_ge(sq, p0_sq)
                    last = None
                    for tt in range(2):
                        for k in range(KC):
                            last = t.matmul(psS1[:1, 512 * tt:512 * tt + 512], onc_sb[:, :1],
                                            S_sb[:, 1024 * k + 512 * tt:1024 * k + 512 * tt + 512],
                                            start=(k == 0), stop=(k == KC - 1))
                    inc(last, "tm")
                tm0_sq = cnt["tm"]
                c_smm = [tm0_sq + j + 1 for j in range(2 * KC)]
                ss0_sq = cnt["ss"]
                c_sq = [ss0_sq + j + 1 for j in range(2 * KC)]

                @block.scalar
                def _(s):
                    for j in range(2 * KC):
                        tt, k = j // KC, j % KC
                        if j >= 2:
                            s.wait_ge(tm, c_smm[j - 2])
                        got = inc(
                            s.activation(sq_sb[:, 512 * (j % 2):512 * (j % 2) + 512],
                                         S_sb[:, 1024 * k + 512 * tt:1024 * k + 512 * tt + 512],
                                         AF.Square), "ss")
                        assert got == c_sq[j]

                @block.tensor
                def _(t):
                    for j in range(2 * KC):
                        tt, k = j // KC, j % KC
                        t.wait_ge(ss, c_sq[j])
                        got = t.matmul(psS2[:1, 512 * tt:512 * tt + 512], onc_sb[:, :1],
                                       sq_sb[:, 512 * (j % 2):512 * (j % 2) + 512],
                                       start=(k == 0), stop=False)
                        assert inc(got, "tm") == c_smm[j]
                    for tt in range(2):
                        inc(t.matmul(psS2[:1, 512 * tt:512 * tt + 512], onr_sb[:1, :1],
                                     eps_sb[:1, :], start=False, stop=True), "tm")
                tm_sums = cnt["tm"]

                psM = sq_sb[0:1, 0:1024]
                psQ2 = st_sb[:1, 0:1024]
                @block.vector
                def _(v):
                    v.wait_ge(tm, tm_sums)
                    inc(v.tensor_scalar(psM, psS1[:1, :], 1.0 / D, None, ALU.mult), "vv")
                c1 = cnt["vv"]

                @block.scalar
                def _(s):
                    s.wait_ge(vv, c1)
                    inc(s.activation(psQ2, psM, AF.Square), "ss")
                c2 = cnt["ss"]

                @block.vector
                def _(v):
                    v.wait_ge(ss, c2)
                    inc(v.scalar_tensor_tensor(psS2[:1, :], psS2[:1, :], 1.0 / D,
                                               psQ2, ALU.mult, ALU.subtract), "vv")
                    v.wait_ge(vv, cnt["vv"])
                    inc(v.reciprocal(psS2[:1, :], psS2[:1, :]), "vv")
                c3 = cnt["vv"]

                @block.scalar
                def _(s):
                    s.wait_ge(vv, c3)
                    inc(s.activation(psS2[:1, :], psS2[:1, :], AF.Sqrt), "ss")
                c4 = cnt["ss"]

                @block.vector
                def _(v):
                    v.wait_ge(ss, c4)
                    inc(v.tensor_tensor(psQ2, psM, psS2[:1, :], ALU.mult), "vv")
                    v.wait_ge(vv, cnt["vv"])
                    inc(v.tensor_copy(psM, psS2[:1, :]), "vv")
                c5 = cnt["vv"]

                @block.tensor
                def _(t):
                    t.wait_ge(vv, c5)
                    t.matmul(psB1[:, 0:512], onr_sb[:1, :], sq_sb[0:1, 0:512], start=True, stop=True)
                    t.matmul(psB1[:, 512:1024], onr_sb[:1, :], sq_sb[0:1, 512:1024], start=True, stop=True)
                    t.matmul(psB2[:, 0:512], onr_sb[:1, :], st_sb[:1, 0:512], start=True, stop=True)
                    inc(t.matmul(psB2[:, 512:1024], onr_sb[:1, :], st_sb[:1, 512:1024],
                                 start=True, stop=True), "tm")
                tm_bc = cnt["tm"]

                ln_done = [None] * KC

                @block.vector
                def _(v, ln_done=ln_done):
                    v.wait_ge(tm, tm_bc)
                    for k in range(KC):
                        sl = S_sb[:, 1024 * k:1024 * k + 1024]
                        inc(v.tensor_tensor(sl, sl, psB1[:, :], ALU.mult), "vv")
                    mid = cnt["vv"]
                    v.wait_ge(vv, mid)
                    for k in range(KC):
                        sl = S_sb[:, 1024 * k:1024 * k + 1024]
                        ln_done[k] = inc(v.tensor_tensor(sl, sl, psB2[:, :], ALU.subtract), "vv")

                if write_bf16_to is not None:
                    @block.scalar
                    def _(s, ln_done=ln_done):
                        for k in range(KC):
                            s.wait_ge(vv, ln_done[k])
                            inc(s.copy(abv(write_bf16_to + 1024 * k, 1024),
                                       S_sb[:, 1024 * k:1024 * k + 1024]), "ss")

                barrier()

        emit_ln(write_bf16_to=y1_o)

        if debug and "y1" in dumps:
            @block.sync
            def _(sy):
                sy.wait_ge(ss, cnt["ss"])
                inc(sy.dma_start(dbg["y1"][:, :], S_sb[:, :]), "so", 16)
            barrier()

        if upto <= 4:
            finish_prefix()
            return nc

        for half in range(2):
            with nc.psum_tensor(f"psG0_{half}", [128, 1024], F32) as psG0, \
                 nc.psum_tensor(f"psU0_{half}", [128, 1024], F32) as psU0, \
                 nc.psum_tensor(f"psG1_{half}", [128, 1024], F32) as psG1, \
                 nc.psum_tensor(f"psU1_{half}", [128, 1024], F32) as psU1:
                psG = [psG0, psG1]
                psU = [psU0, psU1]
                c_ms = [None] * 32
                c_g = [[None, None] for _ in range(32)]
                c_u = [None] * 32
                ss0_ffn = cnt["ss"]
                c_sil = [[ss0_ffn + 2 * i + tt + 1 for tt in range(2)] for i in range(32)]
                c_h = [[None, None] for _ in range(32)]

                for i in range(32):
                    ii = 32 * half + i

                    @block.gpsimd
                    def _(g, i=i, ii=ii):
                        if i >= 2:
                            g.wait_ge(tm, c_u[i - 2])
                        pn = "we" if i % 2 == 0 else "wo"
                        inc(g.dma_start(abv(wgu_o + 4096 * (i % 2), 2048), wg_is[ii, :, :]), pn, 16)
                        inc(g.dma_start(abv(wgu_o + 4096 * (i % 2) + 2048, 2048), wu_is[ii, :, :]), pn, 16)
                    ld = cnt["we" if i % 2 == 0 else "wo"]

                    @block.tensor
                    def _(t, i=i, ld=ld):
                        t.wait_ge(we if i % 2 == 0 else wo, ld)
                        if i >= 2:
                            t.wait_ge(ss, c_sil[i - 2][1])
                            t.wait_ge(vv, c_h[i - 2][1])
                        for tt in range(2):
                            last = None
                            for k in range(KC):
                                last = t.matmul(psG[i % 2][:, 512 * tt:512 * tt + 512],
                                                abv(wgu_o + 4096 * (i % 2) + 128 * k, 128),
                                                abv(y1_o + 1024 * k + 512 * tt, 512),
                                                start=(k == 0), stop=(k == KC - 1))
                            c_g[i][tt] = inc(last, "tm")
                        last = None
                        for tt in range(2):
                            for k in range(KC):
                                last = t.matmul(psU[i % 2][:, 512 * tt:512 * tt + 512],
                                                abv(wgu_o + 4096 * (i % 2) + 2048 + 128 * k, 128),
                                                abv(y1_o + 1024 * k + 512 * tt, 512),
                                                start=(k == 0), stop=(k == KC - 1))
                        c_u[i] = inc(last, "tm")

                    for tt in range(2):
                        @block.vector
                        def _(v, i=i, ii=ii, tt=tt):
                            v.wait_ge(ss, c_sil[i][tt])
                            v.wait_ge(tm, c_u[i])
                            c_h[i][tt] = inc(
                                v.scalar_tensor_tensor(abv(h_o + 1024 * i + 512 * tt, 512),
                                                       psU[i % 2][:, 512 * tt:512 * tt + 512],
                                                       scl_sb[:, 64 + ii:64 + ii + 1],
                                                       sq_sb[:, 512 * tt:512 * tt + 512],
                                                       ALU.mult, ALU.mult), "vv")

                @block.scalar
                def _(s, half=half):
                    for i in range(32):
                        ii = 32 * half + i
                        for tt in range(2):
                            s.wait_ge(tm, c_g[i][tt])
                            if i >= 1:
                                s.wait_ge(vv, c_h[i - 1][tt])
                            got = inc(
                                s.activation(sq_sb[:, 512 * tt:512 * tt + 512],
                                             psG[i % 2][:, 512 * tt:512 * tt + 512], AF.Silu,
                                             bias=0.0, scale=scl_sb[:, ii:ii + 1]), "ss")
                            assert got == c_sil[i][tt]

                barrier()

            if debug and "h" in dumps and half == 0:
                @block.gpsimd
                def _(g):
                    g.wait_ge(vv, cnt["vv"])
                    inc(g.dma_start(dbg["h"][:, :], abv(h_o, 32 * 1024)), "so", 16)
                barrier()

            if upto <= 5 and half == 0:
                break

            with nc.psum_tensor(f"psD0_{half}", [128, 1024], F32) as psD0, \
                 nc.psum_tensor(f"psD1_{half}", [128, 1024], F32) as psD1, \
                 nc.psum_tensor(f"psH_{half}", [1, 1024], F32) as psH:
                psD = [psD0, psD1]
                c_wms = [None] * OC
                d_mm = [None] * OC
                d_ev = [None] * OC

                @block.tensor
                def _(t, half=half):
                    last = None
                    for i in range(32):
                        t.wait_ge(vv, c_h[i][1])
                        for tt in range(2):
                            last = t.matmul(psH[:1, 512 * tt:512 * tt + 512], oncb_sb[:, :1],
                                            abv(h_o + 1024 * i + 512 * tt, 512),
                                            start=(i == 0), stop=(i == 31))
                    inc(last, "tm")
                tm_hs = cnt["tm"]

                @block.vector
                def _(v):
                    v.wait_ge(tm, tm_hs)
                    inc(v.tensor_copy(st_sb[:1, 0:1024], psH[:1, :]), "vv")
                vv_hs = cnt["vv"]

                for o in range(OC):
                    @block.gpsimd
                    def _(g, o=o, half=half):
                        if o >= 2:
                            g.wait_ge(tm, d_mm[o - 2])
                        pn = "we" if o % 2 == 0 else "wo"
                        inc(g.dma_start(abv(wd_o + 4096 * (o % 2), 4096), wd_os[half, o, :, :]), pn, 16)
                    ld = cnt["we" if o % 2 == 0 else "wo"]

                    @block.tensor
                    def _(t, o=o, ld=ld):
                        t.wait_ge(we if o % 2 == 0 else wo, ld)
                        if o >= 2:
                            t.wait_ge(vv, d_ev[o - 2])
                        if o == 0:
                            t.wait_ge(vv, vv_hs)
                        last = None
                        for tt in range(2):
                            t.matmul(psD[o % 2][:, 512 * tt:512 * tt + 512],
                                     nr_sb[:1, :], st_sb[:1, 512 * tt:512 * tt + 512],
                                     start=True, stop=False)
                            for i in range(32):
                                last = t.matmul(psD[o % 2][:, 512 * tt:512 * tt + 512],
                                                abv(wd_o + 4096 * (o % 2) + 128 * i, 128),
                                                abv(h_o + 1024 * i + 512 * tt, 512),
                                                start=False, stop=(i == 31))
                        d_mm[o] = inc(last, "tm")

                    @block.vector
                    def _(v, o=o):
                        v.wait_ge(tm, d_mm[o])
                        d_ev[o] = inc(
                            v.scalar_tensor_tensor(S_sb[:, 1024 * o:1024 * o + 1024],
                                                   psD[o % 2][:, :],
                                                   scl_sb[:, 128 + o:128 + o + 1],
                                                   S_sb[:, 1024 * o:1024 * o + 1024],
                                                   ALU.mult, ALU.add), "vv")

                barrier()
            if upto <= 6 and half == 0:
                break

        if upto <= 6:
            finish_prefix()
            return nc

        if debug and "r2" in dumps:
            @block.sync
            def _(sy):
                sy.wait_ge(vv, cnt["vv"])
                inc(sy.dma_start(dbg["r2"][:, :], S_sb[:, :]), "so", 16)
            barrier()

        emit_ln(write_bf16_to=None)

        @block.sync
        def _(sy):
            sy.wait_ge(vv, cnt["vv"])
            inc(sy.dma_start(y_out[:, :], S_sb[:, :]), "so", 16)
            sy.wait_ge(so, cnt["so"])

    return nc


def _prep_shared(inp):
    Wq = np.asarray(inp["L0_attn_Wq"])
    Wk = np.asarray(inp["L0_attn_Wk"])
    Wv = np.asarray(inp["L0_attn_Wv"])
    Wo = np.asarray(inp["L0_attn_Wo"])
    Wg = np.asarray(inp["L0_ffn_gate_w"])
    Wu = np.asarray(inp["L0_ffn_up_w"])
    Wd = np.asarray(inp["L0_ffn_down_w"])
    gs = np.asarray(inp["L0_ffn_gate_scale"])
    us = np.asarray(inp["L0_ffn_up_scale"])
    ds = np.asarray(inp["L0_ffn_down_scale"])

    def oslab(W, oc):
        return np.ascontiguousarray(
            W.reshape(KC, 128, oc, 128).transpose(2, 1, 0, 3).reshape(oc, 128, KC * 128))

    wq_os = oslab(Wq, OC).astype(np.float32)
    wk_os = oslab(Wk, HKV).astype(np.float32)
    wo_os = oslab(Wo, OC).astype(np.float32)
    wv_ks = np.ascontiguousarray(Wv.reshape(KC, 128, 512)).astype(np.float32)
    wg_is = np.ascontiguousarray(
        Wg.reshape(IC, 128, KC, 128).transpose(0, 3, 2, 1).reshape(IC, 128, KC * 128)).astype(np.int32)
    wu_is = np.ascontiguousarray(
        Wu.reshape(IC, 128, KC, 128).transpose(0, 3, 2, 1).reshape(IC, 128, KC * 128)).astype(np.int32)
    wd_os = np.ascontiguousarray(
        Wd.reshape(OC, 128, 2, 32, 128).transpose(2, 0, 4, 3, 1).reshape(2, OC, 128, 32 * 128)).astype(np.int32)
    scales = np.concatenate([gs.reshape(IC, 128).T, us.reshape(IC, 128).T,
                             ds.reshape(OC, 128).T], axis=1).astype(np.float32)
    d01 = np.eye(128).astype(ml_dtypes.bfloat16)
    ones_c = np.ones((128, 1), np.float32)
    ones_r = np.ones((1, 128), np.float32)
    negrow = np.full((1, 128), -128.0, np.float32)
    ones_cb = np.ones((128, 1), ml_dtypes.bfloat16)
    eps_row = np.full((1, 512), float(D) * EPS, np.float32)
    return dict(negrow=negrow, ones_cb=ones_cb, eps_row=eps_row, wq_os=wq_os, wk_os=wk_os, wv_ks=wv_ks, wo_os=wo_os,
                wg_is=wg_is, wu_is=wu_is, wd_os=wd_os, scales=scales,
                d01=d01, ones_c=ones_c, ones_r=ones_r)


def _mk_masks(part):
    r = np.arange(128)[:, None]
    c = np.arange(640)[None, :]
    band = (c > r) & (c <= r + 512)
    m = np.empty((NQT, 128, 640), np.float32)
    for qt in range(NQT):
        valid = band.copy()
        if part == 0:
            valid = valid & (c >= 512 - 128 * qt)
        m[qt] = np.where(valid, 0.0, NEG)
    return np.ascontiguousarray(m.transpose(1, 0, 2).reshape(128, NQT * 640)).astype(ml_dtypes.bfloat16)


def _prep_core(x, c):
    b, part = c // 4, c % 4
    tok0 = T * part
    xt = np.ascontiguousarray(np.asarray(x)[b].T)
    x_own = np.ascontiguousarray(
        xt[:, tok0:tok0 + T].reshape(KC, 128, T).transpose(1, 0, 2).reshape(128, KC * T)).astype(np.float32)
    win = np.zeros((D, TK), np.float32)
    lo = max(0, tok0 - HALO)
    win[:, HALO - (tok0 - lo):HALO] = xt[:, lo:tok0]
    win[:, HALO:] = xt[:, tok0:tok0 + T]
    x_all = np.ascontiguousarray(
        win.reshape(KC, 128, TK).transpose(1, 0, 2).reshape(128, KC * TK)).astype(np.float32)
    return dict(x_own=x_own, x_all=x_all, masks=_mk_masks(part))


def kernel(**inputs):
    shared = _prep_shared(inputs)
    x = inputs["L0_attn_hidden_states"]
    in_maps = []
    for c in range(NCORES):
        m = dict(shared)
        m.update(_prep_core(x, c))
        in_maps.append(m)
    nc = build_nc(debug=False)
    res = run_bass_kernel_spmd(nc, in_maps, core_ids=list(range(NCORES)))
    out = np.empty((B, S, D), np.float32)
    for c in range(NCORES):
        b, part = c // 4, c % 4
        y = np.asarray(res.results[c]["y"]).reshape(128, KC, T).transpose(1, 0, 2).reshape(D, T)
        out[b, T * part:T * part + T, :] = y.T
    return out


# version 07
# speedup vs baseline: 1.8802x; 1.0180x over previous
import contextlib
import numpy as np
import ml_dtypes
import concourse.bass as bass
import concourse.mybir as mybir
from concourse.bass_utils import run_bass_kernel_spmd

F32 = mybir.dt.float32
BF16 = mybir.dt.bfloat16
I32 = mybir.dt.int32
AF = mybir.ActivationFunctionType
ALU = mybir.AluOpType

B, S, D = 2, 4096, 2048
HQ, HKV, HD, WIN = 16, 4, 128, 512
NCORES = 8
T = 1024
HALO = 512
TK = T + HALO
KC = D // 128
OC = D // 128
IC = 8192 // 128
NQT = T // 128
NKT = TK // 128
EPS = 1e-5
NEG = -1.0e30


def build_nc(debug=False, upto=99, dumps=None):
    if dumps is None:
        dumps = {"qkv", "attn", "y1", "h", "r2"} if debug else set()
    nc = bass.Bass("TRN2", target_bir_lowering=False)

    x_own = nc.dram_tensor("x_own", [128, KC * T], F32, kind="ExternalInput")
    x_all = nc.dram_tensor("x_all", [128, KC * TK], F32, kind="ExternalInput")
    masks = nc.dram_tensor("masks", [128, NQT * 640], BF16, kind="ExternalInput")
    wq_os = nc.dram_tensor("wq_os", [OC, 128, 2048], F32, kind="ExternalInput")
    wk_os = nc.dram_tensor("wk_os", [HKV, 128, 2048], F32, kind="ExternalInput")
    wv_ks = nc.dram_tensor("wv_ks", [KC, 128, 512], F32, kind="ExternalInput")
    wo_os = nc.dram_tensor("wo_os", [OC, 128, 2048], F32, kind="ExternalInput")
    wg_is = nc.dram_tensor("wg_is", [IC, 128, 2048], I32, kind="ExternalInput")
    wu_is = nc.dram_tensor("wu_is", [IC, 128, 2048], I32, kind="ExternalInput")
    wd_os = nc.dram_tensor("wd_os", [2, OC, 128, 4096], I32, kind="ExternalInput")
    scales = nc.dram_tensor("scales", [128, 144], F32, kind="ExternalInput")
    d01_h = nc.dram_tensor("d01", [128, 128], BF16, kind="ExternalInput")
    ones_c = nc.dram_tensor("ones_c", [128, 1], F32, kind="ExternalInput")
    ones_r = nc.dram_tensor("ones_r", [1, 128], F32, kind="ExternalInput")
    negrow = nc.dram_tensor("negrow", [1, 128], F32, kind="ExternalInput")
    ones_cb = nc.dram_tensor("ones_cb", [128, 1], BF16, kind="ExternalInput")
    eps_row = nc.dram_tensor("eps_row", [1, 512], F32, kind="ExternalInput")
    y_out = nc.dram_tensor("y", [128, KC * T], F32, kind="ExternalOutput")

    dbg = {}
    if debug:
        dbg["q"] = nc.dram_tensor("dbg_q", [128, HQ * T], F32, kind="ExternalOutput")
        dbg["k"] = nc.dram_tensor("dbg_k", [128, HKV * TK], F32, kind="ExternalOutput")
        dbg["vT"] = nc.dram_tensor("dbg_vT", [128, NKT * 512], F32, kind="ExternalOutput")
        dbg["attn"] = nc.dram_tensor("dbg_attn", [128, KC * T], F32, kind="ExternalOutput")
        dbg["y1"] = nc.dram_tensor("dbg_y1", [128, KC * T], F32, kind="ExternalOutput")
        dbg["h"] = nc.dram_tensor("dbg_h", [128, 32 * T], F32, kind="ExternalOutput")
        dbg["r2"] = nc.dram_tensor("dbg_r2", [128, KC * T], F32, kind="ExternalOutput")

    with contextlib.ExitStack() as ctx:
        S_sb = ctx.enter_context(nc.sbuf_tensor("S_sb", [128, KC * T], F32))
        ab = ctx.enter_context(nc.sbuf_tensor("ab", [128, 65536], BF16))
        sq_sb = ctx.enter_context(nc.sbuf_tensor("sq_sb", [128, 1024], F32))
        st_sb = ctx.enter_context(nc.sbuf_tensor("st_sb", [1, 1024], F32))
        scl_sb = ctx.enter_context(nc.sbuf_tensor("scl_sb", [128, 144], F32))
        d01_sb = ctx.enter_context(nc.sbuf_tensor("d01_sb", [128, 128], BF16))
        onc_sb = ctx.enter_context(nc.sbuf_tensor("onc_sb", [128, 1], F32))
        onr_sb = ctx.enter_context(nc.sbuf_tensor("onr_sb", [1, 128], F32))
        sums_sb = ctx.enter_context(nc.sbuf_tensor("sums_sb", [128, 6], F32))
        Dm_sb = ctx.enter_context(nc.sbuf_tensor("Dm_sb", [128, 384], BF16))
        nr_sb = ctx.enter_context(nc.sbuf_tensor("nr_sb", [1, 128], F32))
        oncb_sb = ctx.enter_context(nc.sbuf_tensor("oncb_sb", [128, 1], BF16))
        eps_sb = ctx.enter_context(nc.sbuf_tensor("eps_sb", [1, 512], F32))

        def abv(off, n):
            return ab[:, off:off + n]

        x_bf_o, q_bf_o, k_bf_o, vT_o, wvres_o, wbuf_o = 0, 24576, 40960, 47104, 53248, 61440
        attn_o, probs_o, pT_o, mask_o = 0, 16384, 18304, 53248
        h_o, y1_o, wgu_o, wd_o = 0, 32768, 49152, 57344

        sg = ctx.enter_context(nc.semaphore("sg"))
        sq = ctx.enter_context(nc.semaphore("sq"))
        tm = ctx.enter_context(nc.semaphore("tm"))
        vv = ctx.enter_context(nc.semaphore("vv"))
        ss = ctx.enter_context(nc.semaphore("ss"))
        so = ctx.enter_context(nc.semaphore("so"))
        we = ctx.enter_context(nc.semaphore("we"))
        wo = ctx.enter_context(nc.semaphore("wo"))
        sems = {"sg": sg, "sq": sq, "tm": tm, "vv": vv, "ss": ss, "so": so,
                "we": we, "wo": wo}
        cnt = {k: 0 for k in sems}

        def inc(instr, name, step=1):
            cnt[name] += step
            instr.then_inc(sems[name], step)
            return cnt[name]

        def wdma(g, pn, dst, src):
            g.wait_ge(sems[pn], cnt[pn])
            return inc(g.dma_start(dst, src), pn, 16)

        block = ctx.enter_context(nc.Block())

        _bar_prev = [{k: 0 for k in sems}]

        def barrier():
            snap = {k: cnt[k] for k in sems}
            prev = _bar_prev[0]
            own_of = {"tm": block.tensor, "vv": block.vector, "ss": block.scalar}
            for own, engfn in (("tm", block.tensor), ("vv", block.vector),
                               ("ss", block.scalar), (None, block.gpsimd),
                               (None, block.sync)):
                def _f(e, snap=snap, own=own, prev=dict(prev)):
                    for k, v in snap.items():
                        if v > 0 and k != own and v > prev[k]:
                            e.wait_ge(sems[k], v)
                engfn(_f)
            _bar_prev[0] = snap


        def finish_prefix():
            @block.sync
            def _(sy):
                sy.wait_ge(so, cnt["so"])
        @block.sync
        def _(sy):
            inc(sy.dma_start(S_sb[:, :], x_own[:, :]), "sq", 16)
            inc(sy.dma_start(scl_sb[:, :], scales[:, :]), "sq", 16)
            inc(sy.dma_start(d01_sb[:, :], d01_h[:, :]), "sq", 16)
            inc(sy.dma_start(onc_sb[:, :], ones_c[:, :]), "sq", 16)
            inc(sy.dma_start(onr_sb[:1, :], ones_r[:1, :]), "sq", 16)
            inc(sy.dma_start(nr_sb[:1, :], negrow[:1, :]), "sq", 16)
            inc(sy.dma_start(oncb_sb[:, :], ones_cb[:, :]), "sq", 16)
            inc(sy.dma_start(eps_sb[:1, :], eps_row[:1, :]), "sq", 16)
        p0_sq = cnt["sq"]

        @block.gpsimd
        def _(g):
            inc(g.dma_start(abv(x_bf_o, KC * TK), x_all[:, :]), "sg", 16)
            for k in range(KC):
                inc(g.dma_start(abv(wvres_o + 512 * k, 512), wv_ks[k, :, :]), "sg", 16)
        wv_ld = cnt["sg"]

        with nc.psum_tensor("psA", [128, 512], F32) as psA, \
             nc.psum_tensor("psB", [128, 512], F32) as psB, \
             nc.psum_tensor("psQ0", [128, 1024], F32) as psQ0, \
             nc.psum_tensor("psQ1", [128, 1024], F32) as psQ1:
            psV = [psA, psB]
            psQ = [psQ0, psQ1]

            vt_mm = [None] * NKT
            vt_ev = [None] * NKT
            for j in range(NKT):
                @block.tensor
                def _(t, j=j):
                    if j == 0:
                        t.wait_ge(sg, wv_ld)
                    if j >= 2:
                        t.wait_ge(vv, vt_ev[j - 2])
                    ps = psV[j % 2]
                    last = None
                    for k in range(KC):
                        last = t.matmul(ps[:, :], abv(x_bf_o + k * TK + j * 128, 128),
                                        abv(wvres_o + 512 * k, 512),
                                        start=(k == 0), stop=(k == KC - 1))
                    vt_mm[j] = inc(last, "tm")

                @block.vector
                def _(v, j=j):
                    v.wait_ge(tm, vt_mm[j])
                    vt_ev[j] = inc(v.tensor_copy(abv(vT_o + 512 * j, 512), psV[j % 2][:, :]), "vv")

            def proj(noc, wsrc, rhs_off, rhs_stride, rhs_base, ntt, out_kind, out_off):
                mm = [None] * noc
                ev = [None] * noc
                tm0 = cnt["tm"]
                for o in range(noc):
                    @block.gpsimd
                    def _(g, o=o, tm0=tm0):
                        if o >= 2:
                            g.wait_ge(tm, mm[o - 2])
                        else:
                            g.wait_ge(tm, tm0)
                        wdma(g, "we" if o % 2 == 0 else "wo",
                             abv(wbuf_o + 2048 * (o % 2), 2048), wsrc[o, :, :])
                    ld = cnt["we" if o % 2 == 0 else "wo"]

                    @block.tensor
                    def _(t, o=o, ld=ld):
                        t.wait_ge(we if o % 2 == 0 else wo, ld)
                        if o >= 2:
                            t.wait_ge(vv, ev[o - 2])
                        last = None
                        if ntt == 2:
                            ps = psQ[o % 2]
                            for tt in range(2):
                                for k in range(KC):
                                    last = t.matmul(ps[:, 512 * tt:512 * tt + 512],
                                                    abv(wbuf_o + 2048 * (o % 2) + 128 * k, 128),
                                                    abv(rhs_base + k * rhs_stride + rhs_off + 512 * tt, 512),
                                                    start=(k == 0), stop=(k == KC - 1))
                        else:
                            for tt in range(3):
                                ps = psQ[o % 2] if tt < 2 else psV[o % 2]
                                sl = ps[:, 512 * tt:512 * tt + 512] if tt < 2 else ps[:, :]
                                for k in range(KC):
                                    last = t.matmul(sl,
                                                    abv(wbuf_o + 2048 * (o % 2) + 128 * k, 128),
                                                    abv(rhs_base + k * rhs_stride + rhs_off + 512 * tt, 512),
                                                    start=(k == 0), stop=(k == KC - 1))
                        mm[o] = inc(last, "tm")

                    @block.vector
                    def _(v, o=o):
                        v.wait_ge(tm, mm[o])
                        if out_kind == "copy":
                            if ntt == 2:
                                ev[o] = inc(v.tensor_copy(abv(out_off + 1024 * o, 1024),
                                                          psQ[o % 2][:, :]), "vv")
                            else:
                                v.tensor_copy(abv(out_off + 1536 * o, 1024), psQ[o % 2][:, :])
                                ev[o] = inc(v.tensor_copy(abv(out_off + 1536 * o + 1024, 512),
                                                          psV[o % 2][:, :]), "vv")
                        else:
                            ev[o] = inc(v.tensor_tensor(S_sb[:, 1024 * o:1024 * o + 1024],
                                                        S_sb[:, 1024 * o:1024 * o + 1024],
                                                        psQ[o % 2][:, :], ALU.add), "vv")
                return ev

            proj(OC, wq_os, HALO, TK, x_bf_o, 2, "copy", q_bf_o)
            vt_guard = cnt["vv"]

            @block.tensor
            def _(t):
                t.wait_ge(vv, vt_guard)
            proj(HKV, wk_os, 0, TK, x_bf_o, 3, "copy", k_bf_o)

            if debug and "qkv" in dumps:
                @block.gpsimd
                def _(g):
                    g.wait_ge(vv, cnt["vv"])
                    inc(g.dma_start(dbg["q"][:, :], abv(q_bf_o, HQ * 1024)), "so", 16)
                    inc(g.dma_start(dbg["k"][:, :], abv(k_bf_o, HKV * 1536)), "so", 16)
                    inc(g.dma_start(dbg["vT"][:, :], abv(vT_o, NKT * 512)), "so", 16)

            barrier()

        if upto <= 1:
            finish_prefix()
            return nc

        @block.sync
        def _(sy):
            inc(sy.dma_start(abv(mask_o, 8 * 640), masks[:, :]), "sq", 16)
        mask_ld = cnt["sq"]

        with nc.psum_tensor("sc0", [128, 640], F32) as sc0, \
             nc.psum_tensor("sc1", [128, 640], F32) as sc1, \
             nc.psum_tensor("pT", [128, 640], F32) as pTp, \
             nc.psum_tensor("pv0", [128, 128], F32) as pv0, \
             nc.psum_tensor("pv1", [128, 128], F32) as pv1:
            scs = [sc0, sc1]
            pvs = [pv0, pv1]
            items = [(h, qt) for h in range(HQ) for qt in range(NQT)]
            NIT = len(items)
            ss0_attn = cnt["ss"]
            tm0_attn = cnt["tm"]
            vv0_attn = cnt["vv"]
            c_mm = [None] * NIT
            c_tD = [None] * NIT
            c_pv = [None] * NIT
            c_msk = [None] * NIT
            c_ae = [None] * NIT
            c_rec = [None] * NIT
            c_D = [None] * NIT
            c_exp = [None] * NIT
            c_pTe = [None] * NIT
            _t, _v, _s = tm0_attn, vv0_attn, ss0_attn
            for n in range(NIT + 3):
                if n < NIT:
                    _t += 1; c_mm[n] = _t
                if 0 <= n - 2 < NIT:
                    _t += 1; c_tD[n - 2] = _t
                if 0 <= n - 3 < NIT:
                    _t += 1; c_pv[n - 3] = _t
                if n < NIT:
                    _v += 1; c_msk[n] = _v
                if 0 <= n - 3 < NIT:
                    _v += 1; c_ae[n - 3] = _v
                if 0 <= n - 1 < NIT:
                    _v += 1; c_rec[n - 1] = _v
                    _v += 1; c_D[n - 1] = _v
            for n in range(NIT + 2):
                if n < NIT:
                    _s += 1; c_exp[n] = _s
                if 0 <= n - 2 < NIT:
                    _s += 1; c_pTe[n - 2] = _s

            @block.tensor
            def _(t):
                for n in range(NIT + 3):
                    if n < NIT:
                        h, qt = items[n]
                        if n >= 2:
                            t.wait_ge(ss, c_exp[n - 2])
                        qs = q_bf_o + 1024 * h + 128 * qt
                        ks = k_bf_o + 1536 * (h // 4) + 128 * qt
                        sc = scs[n % 2]
                        t.matmul(sc[:, 0:512], abv(qs, 128), abv(ks, 512), start=True, stop=True)
                        assert inc(t.matmul(sc[:, 512:640], abv(qs, 128), abv(ks + 512, 128),
                                            start=True, stop=True), "tm") == c_mm[n]
                    if 0 <= n - 2 < NIT:
                        m = n - 2
                        t.wait_ge(vv, c_D[m])
                        if m >= 1:
                            t.wait_ge(ss, c_pTe[m - 1])
                        last = None
                        for b in range(5):
                            last = t.matmul(pTp[:, 128 * b:128 * b + 128],
                                            abv(probs_o + 640 * (m % 3) + 128 * b, 128),
                                            Dm_sb[:, 128 * (m % 3):128 * (m % 3) + 128],
                                            start=True, stop=True)
                        assert inc(last, "tm") == c_tD[m]
                    if 0 <= n - 3 < NIT:
                        m = n - 3
                        t.wait_ge(ss, c_pTe[m])
                        if m >= 2:
                            t.wait_ge(vv, c_ae[m - 2])
                        mh, mqt = items[m]
                        last = None
                        for b in range(5):
                            last = t.matmul(pvs[m % 2][:, :],
                                            abv(vT_o + 512 * (mqt + b) + 128 * (mh // 4), 128),
                                            abv(pT_o + 640 * (m % 2) + 128 * b, 128),
                                            start=(b == 0), stop=(b == 4))
                        assert inc(last, "tm") == c_pv[m]

            @block.vector
            def _(v):
                for n in range(NIT + 3):
                    if n < NIT:
                        h, qt = items[n]
                        if n == 0:
                            v.wait_ge(sq, mask_ld)
                        v.wait_ge(tm, c_mm[n])
                        assert inc(v.tensor_tensor(scs[n % 2][:, :], scs[n % 2][:, :],
                                                   abv(mask_o + 640 * qt, 640), ALU.add), "vv") == c_msk[n]
                    if 0 <= n - 3 < NIT:
                        m = n - 3
                        mh, mqt = items[m]
                        v.wait_ge(tm, c_pv[m])
                        assert inc(v.tensor_copy(abv(attn_o + 1024 * mh + 128 * mqt, 128),
                                                 pvs[m % 2][:, :]), "vv") == c_ae[m]
                    if 0 <= n - 1 < NIT:
                        m = n - 1
                        v.wait_ge(ss, c_exp[m])
                        assert inc(v.reciprocal(sums_sb[:, 2 * (m % 3) + 1:2 * (m % 3) + 2],
                                                sums_sb[:, 2 * (m % 3):2 * (m % 3) + 1]), "vv") == c_rec[m]
                        v.wait_ge(vv, c_rec[m])
                        assert inc(v.scalar_tensor_tensor(
                            Dm_sb[:, 128 * (m % 3):128 * (m % 3) + 128], d01_sb[:, :],
                            sums_sb[:, 2 * (m % 3) + 1:2 * (m % 3) + 2], d01_sb[:, :],
                            ALU.mult, ALU.mult), "vv") == c_D[m]

            @block.scalar
            def _(s):
                s.wait_ge(sq, p0_sq)
                for n in range(NIT + 2):
                    if n < NIT:
                        s.wait_ge(vv, c_msk[n])
                        if n >= 3:
                            s.wait_ge(tm, c_tD[n - 3])
                        got = inc(
                            s.activation(abv(probs_o + 640 * (n % 3), 640), scs[n % 2][:, :], AF.Exp,
                                         bias=0.0, scale=1.0 / 128.0,
                                         accum_out=sums_sb[:, 2 * (n % 3):2 * (n % 3) + 1]), "ss")
                        assert got == c_exp[n]
                    if 0 <= n - 2 < NIT:
                        m = n - 2
                        s.wait_ge(tm, c_tD[m])
                        got = inc(s.copy(abv(pT_o + 640 * (m % 2), 640), pTp[:, :]), "ss")
                        assert got == c_pTe[m]

            pre_wo = {}

            @block.gpsimd
            def _(g):
                g.wait_ge(tm, tm0_attn)
                for o in range(2):
                    pn = "we" if o % 2 == 0 else "wo"
                    pre_wo[o] = wdma(g, pn, abv(wbuf_o + 2048 * (o % 2), 2048), wo_os[o, :, :])

            if debug and "attn" in dumps:
                @block.gpsimd
                def _(g):
                    g.wait_ge(vv, cnt["vv"])
                    inc(g.dma_start(dbg["attn"][:, :], abv(attn_o, KC * T)), "so", 16)

            barrier()

        if upto <= 2:
            finish_prefix()
            return nc

        with nc.psum_tensor("psW0", [128, 1024], F32) as psW0, \
             nc.psum_tensor("psW1", [128, 1024], F32) as psW1:
            psQ = [psW0, psW1]

            def proj_wo():
                mm = [None] * OC
                ev = [None] * OC
                for o in range(OC):
                    if o >= 2:
                        @block.gpsimd
                        def _(g, o=o):
                            g.wait_ge(tm, mm[o - 2])
                            wdma(g, "we" if o % 2 == 0 else "wo",
                                 abv(wbuf_o + 2048 * (o % 2), 2048), wo_os[o, :, :])
                        ld = cnt["we" if o % 2 == 0 else "wo"]
                    else:
                        ld = pre_wo[o]

                    @block.tensor
                    def _(t, o=o, ld=ld):
                        t.wait_ge(we if o % 2 == 0 else wo, ld)
                        if o >= 2:
                            t.wait_ge(vv, ev[o - 2])
                        last = None
                        for tt in range(2):
                            for k in range(KC):
                                last = t.matmul(psQ[o % 2][:, 512 * tt:512 * tt + 512],
                                                abv(wbuf_o + 2048 * (o % 2) + 128 * k, 128),
                                                abv(attn_o + 1024 * k + 512 * tt, 512),
                                                start=(k == 0), stop=(k == KC - 1))
                        mm[o] = inc(last, "tm")

                    @block.vector
                    def _(v, o=o):
                        v.wait_ge(tm, mm[o])
                        ev[o] = inc(v.tensor_tensor(S_sb[:, 1024 * o:1024 * o + 1024],
                                                    S_sb[:, 1024 * o:1024 * o + 1024],
                                                    psQ[o % 2][:, :], ALU.add), "vv")
            proj_wo()
            barrier()

        if upto <= 3:
            finish_prefix()
            return nc

        _ln_id = [0]

        def emit_ln(write_bf16_to=None):
            _ln_id[0] += 1
            li = _ln_id[0]
            with nc.psum_tensor(f"psS1_{li}", [1, 1024], F32) as psS1, \
                 nc.psum_tensor(f"psS2_{li}", [1, 1024], F32) as psS2, \
                 nc.psum_tensor(f"psB1_{li}", [128, 1024], F32) as psB1, \
                 nc.psum_tensor(f"psB2_{li}", [128, 1024], F32) as psB2:
                @block.tensor
                def _(t):
                    t.wait_ge(sq, p0_sq)
                    last = None
                    for tt in range(2):
                        for k in range(KC):
                            last = t.matmul(psS1[:1, 512 * tt:512 * tt + 512], onc_sb[:, :1],
                                            S_sb[:, 1024 * k + 512 * tt:1024 * k + 512 * tt + 512],
                                            start=(k == 0), stop=(k == KC - 1))
                    inc(last, "tm")
                tm0_sq = cnt["tm"]
                c_smm = [tm0_sq + j + 1 for j in range(2 * KC)]
                ss0_sq = cnt["ss"]
                c_sq = [ss0_sq + j + 1 for j in range(2 * KC)]

                @block.scalar
                def _(s):
                    for j in range(2 * KC):
                        tt, k = j // KC, j % KC
                        if j >= 2:
                            s.wait_ge(tm, c_smm[j - 2])
                        got = inc(
                            s.activation(sq_sb[:, 512 * (j % 2):512 * (j % 2) + 512],
                                         S_sb[:, 1024 * k + 512 * tt:1024 * k + 512 * tt + 512],
                                         AF.Square), "ss")
                        assert got == c_sq[j]

                @block.tensor
                def _(t):
                    for j in range(2 * KC):
                        tt, k = j // KC, j % KC
                        t.wait_ge(ss, c_sq[j])
                        got = t.matmul(psS2[:1, 512 * tt:512 * tt + 512], onc_sb[:, :1],
                                       sq_sb[:, 512 * (j % 2):512 * (j % 2) + 512],
                                       start=(k == 0), stop=False)
                        assert inc(got, "tm") == c_smm[j]
                    for tt in range(2):
                        inc(t.matmul(psS2[:1, 512 * tt:512 * tt + 512], onr_sb[:1, :1],
                                     eps_sb[:1, :], start=False, stop=True), "tm")
                tm_sums = cnt["tm"]

                psM = sq_sb[0:1, 0:1024]
                psQ2 = st_sb[:1, 0:1024]
                @block.vector
                def _(v):
                    v.wait_ge(tm, tm_sums)
                    inc(v.tensor_scalar(psM, psS1[:1, :], 1.0 / D, None, ALU.mult), "vv")
                c1 = cnt["vv"]

                @block.scalar
                def _(s):
                    s.wait_ge(vv, c1)
                    inc(s.activation(psQ2, psM, AF.Square), "ss")
                c2 = cnt["ss"]

                @block.vector
                def _(v):
                    v.wait_ge(ss, c2)
                    inc(v.scalar_tensor_tensor(psS2[:1, :], psS2[:1, :], 1.0 / D,
                                               psQ2, ALU.mult, ALU.subtract), "vv")
                    v.wait_ge(vv, cnt["vv"])
                    inc(v.reciprocal(psS2[:1, :], psS2[:1, :]), "vv")
                c3 = cnt["vv"]

                @block.scalar
                def _(s):
                    s.wait_ge(vv, c3)
                    inc(s.activation(psS2[:1, :], psS2[:1, :], AF.Sqrt), "ss")
                c4 = cnt["ss"]

                @block.vector
                def _(v):
                    v.wait_ge(ss, c4)
                    inc(v.tensor_tensor(psQ2, psM, psS2[:1, :], ALU.mult), "vv")
                    v.wait_ge(vv, cnt["vv"])
                    inc(v.tensor_copy(psM, psS2[:1, :]), "vv")
                c5 = cnt["vv"]

                @block.tensor
                def _(t):
                    t.wait_ge(vv, c5)
                    t.matmul(psB1[:, 0:512], onr_sb[:1, :], sq_sb[0:1, 0:512], start=True, stop=True)
                    t.matmul(psB1[:, 512:1024], onr_sb[:1, :], sq_sb[0:1, 512:1024], start=True, stop=True)
                    t.matmul(psB2[:, 0:512], onr_sb[:1, :], st_sb[:1, 0:512], start=True, stop=True)
                    inc(t.matmul(psB2[:, 512:1024], onr_sb[:1, :], st_sb[:1, 512:1024],
                                 start=True, stop=True), "tm")
                tm_bc = cnt["tm"]

                ln_done = [None] * KC

                @block.vector
                def _(v, ln_done=ln_done):
                    v.wait_ge(tm, tm_bc)
                    for k in range(KC):
                        sl = S_sb[:, 1024 * k:1024 * k + 1024]
                        inc(v.tensor_tensor(sl, sl, psB1[:, :], ALU.mult), "vv")
                    mid = cnt["vv"]
                    v.wait_ge(vv, mid)
                    for k in range(KC):
                        sl = S_sb[:, 1024 * k:1024 * k + 1024]
                        ln_done[k] = inc(v.tensor_tensor(sl, sl, psB2[:, :], ALU.subtract), "vv")

                if write_bf16_to is not None:
                    @block.scalar
                    def _(s, ln_done=ln_done):
                        for k in range(KC):
                            s.wait_ge(vv, ln_done[k])
                            inc(s.copy(abv(write_bf16_to + 1024 * k, 1024),
                                       S_sb[:, 1024 * k:1024 * k + 1024]), "ss")

                barrier()
                return ln_done

        emit_ln(write_bf16_to=y1_o)

        if debug and "y1" in dumps:
            @block.sync
            def _(sy):
                sy.wait_ge(ss, cnt["ss"])
                inc(sy.dma_start(dbg["y1"][:, :], S_sb[:, :]), "so", 16)
            barrier()

        if upto <= 4:
            finish_prefix()
            return nc

        pre_gu = {}
        pre_wd = {}

        @block.gpsimd
        def _(g):
            for i in range(2):
                e = wdma(g, "we", abv(wgu_o + 4096 * (i % 2), 2048), wg_is[i, :, :])
                o_ = wdma(g, "wo", abv(wgu_o + 4096 * (i % 2) + 2048, 2048), wu_is[i, :, :])
                pre_gu[(0, i)] = (e, o_)

        for half in range(2):
            with nc.psum_tensor(f"psG0_{half}", [128, 1024], F32) as psG0, \
                 nc.psum_tensor(f"psU0_{half}", [128, 1024], F32) as psU0, \
                 nc.psum_tensor(f"psG1_{half}", [128, 1024], F32) as psG1, \
                 nc.psum_tensor(f"psU1_{half}", [128, 1024], F32) as psU1:
                psG = [psG0, psG1]
                psU = [psU0, psU1]
                c_ms = [None] * 32
                c_g = [[None, None] for _ in range(32)]
                c_u = [None] * 32
                ss0_ffn = cnt["ss"]
                c_sil = [[ss0_ffn + 2 * i + tt + 1 for tt in range(2)] for i in range(32)]
                c_h = [[None, None] for _ in range(32)]

                @block.gpsimd
                def _(g, half=half):
                    for o in range(2):
                        pn = "we" if o % 2 == 0 else "wo"
                        pre_wd[(half, o)] = wdma(g, pn, abv(wd_o + 4096 * (o % 2), 4096),
                                                 wd_os[half, o, :, :])

                for i in range(32):
                    ii = 32 * half + i

                    if i >= 2:
                        @block.gpsimd
                        def _(g, i=i, ii=ii):
                            g.wait_ge(tm, c_u[i - 2])
                            wdma(g, "we", abv(wgu_o + 4096 * (i % 2), 2048), wg_is[ii, :, :])
                            wdma(g, "wo", abv(wgu_o + 4096 * (i % 2) + 2048, 2048), wu_is[ii, :, :])
                        ld = (cnt["we"], cnt["wo"])
                    else:
                        ld = pre_gu[(half, i)]

                    @block.tensor
                    def _(t, i=i, ld=ld):
                        t.wait_ge(we, ld[0])
                        t.wait_ge(wo, ld[1])
                        if i >= 2:
                            t.wait_ge(ss, c_sil[i - 2][1])
                            t.wait_ge(vv, c_h[i - 2][1])
                        for tt in range(2):
                            last = None
                            for k in range(KC):
                                last = t.matmul(psG[i % 2][:, 512 * tt:512 * tt + 512],
                                                abv(wgu_o + 4096 * (i % 2) + 128 * k, 128),
                                                abv(y1_o + 1024 * k + 512 * tt, 512),
                                                start=(k == 0), stop=(k == KC - 1))
                            c_g[i][tt] = inc(last, "tm")
                        last = None
                        for tt in range(2):
                            for k in range(KC):
                                last = t.matmul(psU[i % 2][:, 512 * tt:512 * tt + 512],
                                                abv(wgu_o + 4096 * (i % 2) + 2048 + 128 * k, 128),
                                                abv(y1_o + 1024 * k + 512 * tt, 512),
                                                start=(k == 0), stop=(k == KC - 1))
                        c_u[i] = inc(last, "tm")

                    for tt in range(2):
                        @block.vector
                        def _(v, i=i, ii=ii, tt=tt):
                            v.wait_ge(ss, c_sil[i][tt])
                            v.wait_ge(tm, c_u[i])
                            c_h[i][tt] = inc(
                                v.scalar_tensor_tensor(abv(h_o + 1024 * i + 512 * tt, 512),
                                                       psU[i % 2][:, 512 * tt:512 * tt + 512],
                                                       scl_sb[:, 64 + ii:64 + ii + 1],
                                                       sq_sb[:, 512 * tt:512 * tt + 512],
                                                       ALU.mult, ALU.mult), "vv")

                @block.scalar
                def _(s, half=half):
                    for i in range(32):
                        ii = 32 * half + i
                        for tt in range(2):
                            s.wait_ge(tm, c_g[i][tt])
                            if i >= 1:
                                s.wait_ge(vv, c_h[i - 1][tt])
                            got = inc(
                                s.activation(sq_sb[:, 512 * tt:512 * tt + 512],
                                             psG[i % 2][:, 512 * tt:512 * tt + 512], AF.Silu,
                                             bias=0.0, scale=scl_sb[:, ii:ii + 1]), "ss")
                            assert got == c_sil[i][tt]

                barrier()

            if debug and "h" in dumps and half == 0:
                @block.gpsimd
                def _(g):
                    g.wait_ge(vv, cnt["vv"])
                    inc(g.dma_start(dbg["h"][:, :], abv(h_o, 32 * 1024)), "so", 16)
                barrier()

            if upto <= 5 and half == 0:
                break

            with nc.psum_tensor(f"psD0_{half}", [128, 1024], F32) as psD0, \
                 nc.psum_tensor(f"psD1_{half}", [128, 1024], F32) as psD1, \
                 nc.psum_tensor(f"psH_{half}", [1, 1024], F32) as psH:
                psD = [psD0, psD1]
                c_wms = [None] * OC
                d_mm = [None] * OC
                d_ev = [None] * OC

                @block.tensor
                def _(t, half=half):
                    last = None
                    for i in range(32):
                        t.wait_ge(vv, c_h[i][1])
                        for tt in range(2):
                            last = t.matmul(psH[:1, 512 * tt:512 * tt + 512], oncb_sb[:, :1],
                                            abv(h_o + 1024 * i + 512 * tt, 512),
                                            start=(i == 0), stop=(i == 31))
                    inc(last, "tm")
                tm_hs = cnt["tm"]

                @block.vector
                def _(v):
                    v.wait_ge(tm, tm_hs)
                    inc(v.tensor_copy(st_sb[:1, 0:1024], psH[:1, :]), "vv")
                vv_hs = cnt["vv"]

                if half == 0:
                    @block.gpsimd
                    def _(g):
                        for i in range(2):
                            e = wdma(g, "we", abv(wgu_o + 4096 * (i % 2), 2048), wg_is[32 + i, :, :])
                            o_ = wdma(g, "wo", abv(wgu_o + 4096 * (i % 2) + 2048, 2048), wu_is[32 + i, :, :])
                            pre_gu[(1, i)] = (e, o_)

                for o in range(OC):
                    if o >= 2:
                        @block.gpsimd
                        def _(g, o=o, half=half):
                            g.wait_ge(tm, d_mm[o - 2])
                            wdma(g, "we" if o % 2 == 0 else "wo",
                                 abv(wd_o + 4096 * (o % 2), 4096), wd_os[half, o, :, :])
                        ld = cnt["we" if o % 2 == 0 else "wo"]
                    else:
                        ld = pre_wd[(half, o)]

                    @block.tensor
                    def _(t, o=o, ld=ld):
                        t.wait_ge(we if o % 2 == 0 else wo, ld)
                        if o >= 2:
                            t.wait_ge(vv, d_ev[o - 2])
                        if o == 0:
                            t.wait_ge(vv, vv_hs)
                        last = None
                        for tt in range(2):
                            t.matmul(psD[o % 2][:, 512 * tt:512 * tt + 512],
                                     nr_sb[:1, :], st_sb[:1, 512 * tt:512 * tt + 512],
                                     start=True, stop=False)
                            for i in range(32):
                                last = t.matmul(psD[o % 2][:, 512 * tt:512 * tt + 512],
                                                abv(wd_o + 4096 * (o % 2) + 128 * i, 128),
                                                abv(h_o + 1024 * i + 512 * tt, 512),
                                                start=False, stop=(i == 31))
                        d_mm[o] = inc(last, "tm")

                    @block.vector
                    def _(v, o=o):
                        v.wait_ge(tm, d_mm[o])
                        d_ev[o] = inc(
                            v.scalar_tensor_tensor(S_sb[:, 1024 * o:1024 * o + 1024],
                                                   psD[o % 2][:, :],
                                                   scl_sb[:, 128 + o:128 + o + 1],
                                                   S_sb[:, 1024 * o:1024 * o + 1024],
                                                   ALU.mult, ALU.add), "vv")

                barrier()
            if upto <= 6 and half == 0:
                break

        if upto <= 6:
            finish_prefix()
            return nc

        if debug and "r2" in dumps:
            @block.sync
            def _(sy):
                sy.wait_ge(vv, cnt["vv"])
                inc(sy.dma_start(dbg["r2"][:, :], S_sb[:, :]), "so", 16)
            barrier()

        ln2_done = emit_ln(write_bf16_to=None)

        @block.sync
        def _(sy):
            for k in range(KC):
                sy.wait_ge(vv, ln2_done[k])
                inc(sy.dma_start(y_out[:, 1024 * k:1024 * k + 1024],
                                 S_sb[:, 1024 * k:1024 * k + 1024]), "so", 16)
            sy.wait_ge(so, cnt["so"])

    return nc


def _prep_shared(inp):
    Wq = np.asarray(inp["L0_attn_Wq"])
    Wk = np.asarray(inp["L0_attn_Wk"])
    Wv = np.asarray(inp["L0_attn_Wv"])
    Wo = np.asarray(inp["L0_attn_Wo"])
    Wg = np.asarray(inp["L0_ffn_gate_w"])
    Wu = np.asarray(inp["L0_ffn_up_w"])
    Wd = np.asarray(inp["L0_ffn_down_w"])
    gs = np.asarray(inp["L0_ffn_gate_scale"])
    us = np.asarray(inp["L0_ffn_up_scale"])
    ds = np.asarray(inp["L0_ffn_down_scale"])

    def oslab(W, oc):
        return np.ascontiguousarray(
            W.reshape(KC, 128, oc, 128).transpose(2, 1, 0, 3).reshape(oc, 128, KC * 128))

    wq_os = oslab(Wq, OC).astype(np.float32)
    wk_os = oslab(Wk, HKV).astype(np.float32)
    wo_os = oslab(Wo, OC).astype(np.float32)
    wv_ks = np.ascontiguousarray(Wv.reshape(KC, 128, 512)).astype(np.float32)
    wg_is = np.ascontiguousarray(
        Wg.reshape(IC, 128, KC, 128).transpose(0, 3, 2, 1).reshape(IC, 128, KC * 128)).astype(np.int32)
    wu_is = np.ascontiguousarray(
        Wu.reshape(IC, 128, KC, 128).transpose(0, 3, 2, 1).reshape(IC, 128, KC * 128)).astype(np.int32)
    wd_os = np.ascontiguousarray(
        Wd.reshape(OC, 128, 2, 32, 128).transpose(2, 0, 4, 3, 1).reshape(2, OC, 128, 32 * 128)).astype(np.int32)
    scales = np.concatenate([gs.reshape(IC, 128).T, us.reshape(IC, 128).T,
                             ds.reshape(OC, 128).T], axis=1).astype(np.float32)
    d01 = np.eye(128).astype(ml_dtypes.bfloat16)
    ones_c = np.ones((128, 1), np.float32)
    ones_r = np.ones((1, 128), np.float32)
    negrow = np.full((1, 128), -128.0, np.float32)
    ones_cb = np.ones((128, 1), ml_dtypes.bfloat16)
    eps_row = np.full((1, 512), float(D) * EPS, np.float32)
    return dict(negrow=negrow, ones_cb=ones_cb, eps_row=eps_row, wq_os=wq_os, wk_os=wk_os, wv_ks=wv_ks, wo_os=wo_os,
                wg_is=wg_is, wu_is=wu_is, wd_os=wd_os, scales=scales,
                d01=d01, ones_c=ones_c, ones_r=ones_r)


def _mk_masks(part):
    r = np.arange(128)[:, None]
    c = np.arange(640)[None, :]
    band = (c > r) & (c <= r + 512)
    m = np.empty((NQT, 128, 640), np.float32)
    for qt in range(NQT):
        valid = band.copy()
        if part == 0:
            valid = valid & (c >= 512 - 128 * qt)
        m[qt] = np.where(valid, 0.0, NEG)
    return np.ascontiguousarray(m.transpose(1, 0, 2).reshape(128, NQT * 640)).astype(ml_dtypes.bfloat16)


def _prep_core(x, c):
    b, part = c // 4, c % 4
    tok0 = T * part
    xt = np.ascontiguousarray(np.asarray(x)[b].T)
    x_own = np.ascontiguousarray(
        xt[:, tok0:tok0 + T].reshape(KC, 128, T).transpose(1, 0, 2).reshape(128, KC * T)).astype(np.float32)
    win = np.zeros((D, TK), np.float32)
    lo = max(0, tok0 - HALO)
    win[:, HALO - (tok0 - lo):HALO] = xt[:, lo:tok0]
    win[:, HALO:] = xt[:, tok0:tok0 + T]
    x_all = np.ascontiguousarray(
        win.reshape(KC, 128, TK).transpose(1, 0, 2).reshape(128, KC * TK)).astype(np.float32)
    return dict(x_own=x_own, x_all=x_all, masks=_mk_masks(part))


def kernel(**inputs):
    shared = _prep_shared(inputs)
    x = inputs["L0_attn_hidden_states"]
    in_maps = []
    for c in range(NCORES):
        m = dict(shared)
        m.update(_prep_core(x, c))
        in_maps.append(m)
    nc = build_nc(debug=False)
    res = run_bass_kernel_spmd(nc, in_maps, core_ids=list(range(NCORES)))
    out = np.empty((B, S, D), np.float32)
    for c in range(NCORES):
        b, part = c // 4, c % 4
        y = np.asarray(res.results[c]["y"]).reshape(128, KC, T).transpose(1, 0, 2).reshape(D, T)
        out[b, T * part:T * part + T, :] = y.T
    return out


# version 08
# speedup vs baseline: 1.8977x; 1.0093x over previous
import contextlib
import numpy as np
import ml_dtypes
import concourse.bass as bass
import concourse.mybir as mybir
from concourse.bass_utils import run_bass_kernel_spmd

F32 = mybir.dt.float32
BF16 = mybir.dt.bfloat16
I32 = mybir.dt.int32
AF = mybir.ActivationFunctionType
ALU = mybir.AluOpType

B, S, D = 2, 4096, 2048
HQ, HKV, HD, WIN = 16, 4, 128, 512
NCORES = 8
T = 1024
HALO = 512
TK = T + HALO
KC = D // 128
OC = D // 128
IC = 8192 // 128
NQT = T // 128
NKT = TK // 128
EPS = 1e-5
NEG = -1.0e30


def build_nc(debug=False, upto=99, dumps=None):
    if dumps is None:
        dumps = {"qkv", "attn", "y1", "h", "r2"} if debug else set()
    nc = bass.Bass("TRN2", target_bir_lowering=False)

    x_own = nc.dram_tensor("x_own", [128, KC * T], F32, kind="ExternalInput")
    x_all = nc.dram_tensor("x_all", [128, KC * TK], F32, kind="ExternalInput")
    masks = nc.dram_tensor("masks", [128, NQT * 640], BF16, kind="ExternalInput")
    wq_os = nc.dram_tensor("wq_os", [OC, 128, 2048], F32, kind="ExternalInput")
    wk_os = nc.dram_tensor("wk_os", [HKV, 128, 2048], F32, kind="ExternalInput")
    wv_ks = nc.dram_tensor("wv_ks", [KC, 128, 512], F32, kind="ExternalInput")
    wo_os = nc.dram_tensor("wo_os", [OC, 128, 2048], F32, kind="ExternalInput")
    wg_is = nc.dram_tensor("wg_is", [IC, 128, 2048], I32, kind="ExternalInput")
    wu_is = nc.dram_tensor("wu_is", [IC, 128, 2048], I32, kind="ExternalInput")
    wd_os = nc.dram_tensor("wd_os", [2, OC, 128, 4096], I32, kind="ExternalInput")
    scales = nc.dram_tensor("scales", [128, 144], F32, kind="ExternalInput")
    d01_h = nc.dram_tensor("d01", [128, 128], BF16, kind="ExternalInput")
    ones_c = nc.dram_tensor("ones_c", [128, 1], F32, kind="ExternalInput")
    ones_r = nc.dram_tensor("ones_r", [1, 128], F32, kind="ExternalInput")
    negrow = nc.dram_tensor("negrow", [1, 128], F32, kind="ExternalInput")
    ones_cb = nc.dram_tensor("ones_cb", [128, 1], BF16, kind="ExternalInput")
    eps_row = nc.dram_tensor("eps_row", [1, 512], F32, kind="ExternalInput")
    y_out = nc.dram_tensor("y", [128, KC * T], F32, kind="ExternalOutput")

    dbg = {}
    if debug:
        dbg["q"] = nc.dram_tensor("dbg_q", [128, HQ * T], F32, kind="ExternalOutput")
        dbg["k"] = nc.dram_tensor("dbg_k", [128, HKV * TK], F32, kind="ExternalOutput")
        dbg["vT"] = nc.dram_tensor("dbg_vT", [128, NKT * 512], F32, kind="ExternalOutput")
        dbg["attn"] = nc.dram_tensor("dbg_attn", [128, KC * T], F32, kind="ExternalOutput")
        dbg["y1"] = nc.dram_tensor("dbg_y1", [128, KC * T], F32, kind="ExternalOutput")
        dbg["h"] = nc.dram_tensor("dbg_h", [128, 32 * T], F32, kind="ExternalOutput")
        dbg["r2"] = nc.dram_tensor("dbg_r2", [128, KC * T], F32, kind="ExternalOutput")

    with contextlib.ExitStack() as ctx:
        S_sb = ctx.enter_context(nc.sbuf_tensor("S_sb", [128, KC * T], F32))
        ab = ctx.enter_context(nc.sbuf_tensor("ab", [128, 65536], BF16))
        sq_sb = ctx.enter_context(nc.sbuf_tensor("sq_sb", [128, 1024], F32))
        st_sb = ctx.enter_context(nc.sbuf_tensor("st_sb", [1, 1024], F32))
        scl_sb = ctx.enter_context(nc.sbuf_tensor("scl_sb", [128, 144], F32))
        d01_sb = ctx.enter_context(nc.sbuf_tensor("d01_sb", [128, 128], BF16))
        onc_sb = ctx.enter_context(nc.sbuf_tensor("onc_sb", [128, 1], F32))
        onr_sb = ctx.enter_context(nc.sbuf_tensor("onr_sb", [1, 128], F32))
        sums_sb = ctx.enter_context(nc.sbuf_tensor("sums_sb", [128, 6], F32))
        Dm_sb = ctx.enter_context(nc.sbuf_tensor("Dm_sb", [128, 384], BF16))
        nr_sb = ctx.enter_context(nc.sbuf_tensor("nr_sb", [1, 128], F32))
        oncb_sb = ctx.enter_context(nc.sbuf_tensor("oncb_sb", [128, 1], BF16))
        eps_sb = ctx.enter_context(nc.sbuf_tensor("eps_sb", [1, 512], F32))

        def abv(off, n):
            return ab[:, off:off + n]

        x_bf_o, q_bf_o, k_bf_o, vT_o, wvres_o, wbuf_o = 0, 24576, 40960, 47104, 53248, 61440
        attn_o, probs_o, pT_o, mask_o = 0, 16384, 18304, 53248
        h_o, y1_o, wgu_o, wd_o = 0, 32768, 49152, 57344

        sg = ctx.enter_context(nc.semaphore("sg"))
        sq = ctx.enter_context(nc.semaphore("sq"))
        tm = ctx.enter_context(nc.semaphore("tm"))
        vv = ctx.enter_context(nc.semaphore("vv"))
        ss = ctx.enter_context(nc.semaphore("ss"))
        so = ctx.enter_context(nc.semaphore("so"))
        we = ctx.enter_context(nc.semaphore("we"))
        wo = ctx.enter_context(nc.semaphore("wo"))
        sx = ctx.enter_context(nc.semaphore("sx"))
        sh = ctx.enter_context(nc.semaphore("sh"))
        sems = {"sg": sg, "sq": sq, "tm": tm, "vv": vv, "ss": ss, "so": so,
                "we": we, "wo": wo, "sx": sx, "sh": sh}
        cnt = {k: 0 for k in sems}

        def inc(instr, name, step=1):
            cnt[name] += step
            instr.then_inc(sems[name], step)
            return cnt[name]

        def wdma(g, pn, dst, src):
            g.wait_ge(sems[pn], cnt[pn])
            return inc(g.dma_start(dst, src), pn, 16)

        block = ctx.enter_context(nc.Block())

        _bar_prev = [{k: 0 for k in sems}]

        def barrier():
            snap = {k: cnt[k] for k in sems}
            prev = _bar_prev[0]
            own_of = {"tm": block.tensor, "vv": block.vector, "ss": block.scalar}
            for own, engfn in (("tm", block.tensor), ("vv", block.vector),
                               ("ss", block.scalar), (None, block.gpsimd),
                               (None, block.sync)):
                def _f(e, snap=snap, own=own, prev=dict(prev)):
                    for k, v in snap.items():
                        if v > 0 and k != own and v > prev[k]:
                            e.wait_ge(sems[k], v)
                engfn(_f)
            _bar_prev[0] = snap


        def finish_prefix():
            @block.sync
            def _(sy):
                sy.wait_ge(so, cnt["so"])
        @block.sync
        def _(sy):
            inc(sy.dma_start(S_sb[:, :], x_own[:, :]), "sq", 16)
            inc(sy.dma_start(scl_sb[:, :], scales[:, :]), "sq", 16)
            inc(sy.dma_start(d01_sb[:, :], d01_h[:, :]), "sq", 16)
            inc(sy.dma_start(onc_sb[:, :], ones_c[:, :]), "sq", 16)
            inc(sy.dma_start(onr_sb[:1, :], ones_r[:1, :]), "sq", 16)
            inc(sy.dma_start(nr_sb[:1, :], negrow[:1, :]), "sq", 16)
            inc(sy.dma_start(oncb_sb[:, :], ones_cb[:, :]), "sq", 16)
            inc(sy.dma_start(eps_sb[:1, :], eps_row[:1, :]), "sq", 16)
        p0_sq = cnt["sq"]

        @block.gpsimd
        def _(g):
            for k in range(KC):
                inc(g.dma_start(abv(wvres_o + 512 * k, 512), wv_ks[k, :, :]), "sg", 16)
            for k in range(KC):
                inc(g.dma_start(abv(x_bf_o + k * TK + HALO, T),
                                x_all[:, k * TK + HALO:(k + 1) * TK]), "sx", 16)
        own_ld = cnt["sx"]
        wv_only = cnt["sg"]

        @block.gpsimd
        def _(g):
            for k in range(KC):
                inc(g.dma_start(abv(x_bf_o + k * TK, HALO),
                                x_all[:, k * TK:k * TK + HALO]), "sh", 16)
        halo_ld = cnt["sh"]

        with nc.psum_tensor("psA", [128, 512], F32) as psA, \
             nc.psum_tensor("psB", [128, 512], F32) as psB, \
             nc.psum_tensor("psQ0", [128, 1024], F32) as psQ0, \
             nc.psum_tensor("psQ1", [128, 1024], F32) as psQ1:
            psV = [psA, psB]
            psQ = [psQ0, psQ1]

            vt_mm = [None] * NKT
            vt_ev = [None] * NKT
            vt_order = list(range(4, NKT)) + list(range(4))
            for jj, j in enumerate(vt_order):
                @block.tensor
                def _(t, j=j, jj=jj):
                    if jj == 0:
                        t.wait_ge(sg, wv_only)
                        t.wait_ge(sx, own_ld)
                    if j < 4 and jj == len(vt_order) - 4:
                        t.wait_ge(sh, halo_ld)
                    if jj >= 2:
                        t.wait_ge(vv, vt_ev[vt_order[jj - 2]])
                    ps = psV[jj % 2]
                    last = None
                    for k in range(KC):
                        last = t.matmul(ps[:, :], abv(x_bf_o + k * TK + j * 128, 128),
                                        abv(wvres_o + 512 * k, 512),
                                        start=(k == 0), stop=(k == KC - 1))
                    vt_mm[j] = inc(last, "tm")

                @block.vector
                def _(v, j=j, jj=jj):
                    v.wait_ge(tm, vt_mm[j])
                    vt_ev[j] = inc(v.tensor_copy(abv(vT_o + 512 * j, 512), psV[jj % 2][:, :]), "vv")

            def proj(noc, wsrc, rhs_off, rhs_stride, rhs_base, ntt, out_kind, out_off):
                mm = [None] * noc
                ev = [None] * noc
                tm0 = cnt["tm"]
                for o in range(noc):
                    @block.gpsimd
                    def _(g, o=o, tm0=tm0):
                        if o >= 2:
                            g.wait_ge(tm, mm[o - 2])
                        else:
                            g.wait_ge(tm, tm0)
                        wdma(g, "we" if o % 2 == 0 else "wo",
                             abv(wbuf_o + 2048 * (o % 2), 2048), wsrc[o, :, :])
                    ld = cnt["we" if o % 2 == 0 else "wo"]

                    @block.tensor
                    def _(t, o=o, ld=ld):
                        t.wait_ge(we if o % 2 == 0 else wo, ld)
                        if o >= 2:
                            t.wait_ge(vv, ev[o - 2])
                        last = None
                        if ntt == 2:
                            ps = psQ[o % 2]
                            for tt in range(2):
                                for k in range(KC):
                                    last = t.matmul(ps[:, 512 * tt:512 * tt + 512],
                                                    abv(wbuf_o + 2048 * (o % 2) + 128 * k, 128),
                                                    abv(rhs_base + k * rhs_stride + rhs_off + 512 * tt, 512),
                                                    start=(k == 0), stop=(k == KC - 1))
                        else:
                            for tt in range(3):
                                ps = psQ[o % 2] if tt < 2 else psV[o % 2]
                                sl = ps[:, 512 * tt:512 * tt + 512] if tt < 2 else ps[:, :]
                                for k in range(KC):
                                    last = t.matmul(sl,
                                                    abv(wbuf_o + 2048 * (o % 2) + 128 * k, 128),
                                                    abv(rhs_base + k * rhs_stride + rhs_off + 512 * tt, 512),
                                                    start=(k == 0), stop=(k == KC - 1))
                        mm[o] = inc(last, "tm")

                    @block.vector
                    def _(v, o=o):
                        v.wait_ge(tm, mm[o])
                        if out_kind == "copy":
                            if ntt == 2:
                                ev[o] = inc(v.tensor_copy(abv(out_off + 1024 * o, 1024),
                                                          psQ[o % 2][:, :]), "vv")
                            else:
                                v.tensor_copy(abv(out_off + 1536 * o, 1024), psQ[o % 2][:, :])
                                ev[o] = inc(v.tensor_copy(abv(out_off + 1536 * o + 1024, 512),
                                                          psV[o % 2][:, :]), "vv")
                        else:
                            ev[o] = inc(v.tensor_tensor(S_sb[:, 1024 * o:1024 * o + 1024],
                                                        S_sb[:, 1024 * o:1024 * o + 1024],
                                                        psQ[o % 2][:, :], ALU.add), "vv")
                return ev

            proj(OC, wq_os, HALO, TK, x_bf_o, 2, "copy", q_bf_o)
            vt_guard = cnt["vv"]

            @block.tensor
            def _(t):
                t.wait_ge(vv, vt_guard)
            proj(HKV, wk_os, 0, TK, x_bf_o, 3, "copy", k_bf_o)

            if debug and "qkv" in dumps:
                @block.gpsimd
                def _(g):
                    g.wait_ge(vv, cnt["vv"])
                    inc(g.dma_start(dbg["q"][:, :], abv(q_bf_o, HQ * 1024)), "so", 16)
                    inc(g.dma_start(dbg["k"][:, :], abv(k_bf_o, HKV * 1536)), "so", 16)
                    inc(g.dma_start(dbg["vT"][:, :], abv(vT_o, NKT * 512)), "so", 16)

            barrier()

        if upto <= 1:
            finish_prefix()
            return nc

        @block.sync
        def _(sy):
            inc(sy.dma_start(abv(mask_o, 8 * 640), masks[:, :]), "sq", 16)
        mask_ld = cnt["sq"]

        with nc.psum_tensor("sc0", [128, 640], F32) as sc0, \
             nc.psum_tensor("sc1", [128, 640], F32) as sc1, \
             nc.psum_tensor("pT", [128, 640], F32) as pTp, \
             nc.psum_tensor("pv0", [128, 128], F32) as pv0, \
             nc.psum_tensor("pv1", [128, 128], F32) as pv1:
            scs = [sc0, sc1]
            pvs = [pv0, pv1]
            items = [(h, qt) for h in range(HQ) for qt in range(NQT)]
            NIT = len(items)
            ss0_attn = cnt["ss"]
            tm0_attn = cnt["tm"]
            vv0_attn = cnt["vv"]
            c_mm = [None] * NIT
            c_tD = [None] * NIT
            c_pv = [None] * NIT
            c_msk = [None] * NIT
            c_ae = [None] * NIT
            c_rec = [None] * NIT
            c_D = [None] * NIT
            c_exp = [None] * NIT
            c_pTe = [None] * NIT
            _t, _v, _s = tm0_attn, vv0_attn, ss0_attn
            for n in range(NIT + 3):
                if n < NIT:
                    _t += 1; c_mm[n] = _t
                if 0 <= n - 2 < NIT:
                    _t += 1; c_tD[n - 2] = _t
                if 0 <= n - 3 < NIT:
                    _t += 1; c_pv[n - 3] = _t
                if n < NIT:
                    _v += 1; c_msk[n] = _v
                if 0 <= n - 3 < NIT:
                    _v += 1; c_ae[n - 3] = _v
                if 0 <= n - 1 < NIT:
                    _v += 1; c_rec[n - 1] = _v
                    _v += 1; c_D[n - 1] = _v
            for n in range(NIT + 2):
                if n < NIT:
                    _s += 1; c_exp[n] = _s
                if 0 <= n - 2 < NIT:
                    _s += 1; c_pTe[n - 2] = _s

            @block.tensor
            def _(t):
                for n in range(NIT + 3):
                    if n < NIT:
                        h, qt = items[n]
                        if n >= 2:
                            t.wait_ge(ss, c_exp[n - 2])
                        qs = q_bf_o + 1024 * h + 128 * qt
                        ks = k_bf_o + 1536 * (h // 4) + 128 * qt
                        sc = scs[n % 2]
                        t.matmul(sc[:, 0:512], abv(qs, 128), abv(ks, 512), start=True, stop=True)
                        assert inc(t.matmul(sc[:, 512:640], abv(qs, 128), abv(ks + 512, 128),
                                            start=True, stop=True), "tm") == c_mm[n]
                    if 0 <= n - 2 < NIT:
                        m = n - 2
                        t.wait_ge(vv, c_D[m])
                        if m >= 1:
                            t.wait_ge(ss, c_pTe[m - 1])
                        last = None
                        for b in range(5):
                            last = t.matmul(pTp[:, 128 * b:128 * b + 128],
                                            abv(probs_o + 640 * (m % 3) + 128 * b, 128),
                                            Dm_sb[:, 128 * (m % 3):128 * (m % 3) + 128],
                                            start=True, stop=True)
                        assert inc(last, "tm") == c_tD[m]
                    if 0 <= n - 3 < NIT:
                        m = n - 3
                        t.wait_ge(ss, c_pTe[m])
                        if m >= 2:
                            t.wait_ge(vv, c_ae[m - 2])
                        mh, mqt = items[m]
                        last = None
                        for b in range(5):
                            last = t.matmul(pvs[m % 2][:, :],
                                            abv(vT_o + 512 * (mqt + b) + 128 * (mh // 4), 128),
                                            abv(pT_o + 640 * (m % 2) + 128 * b, 128),
                                            start=(b == 0), stop=(b == 4))
                        assert inc(last, "tm") == c_pv[m]

            @block.vector
            def _(v):
                for n in range(NIT + 3):
                    if n < NIT:
                        h, qt = items[n]
                        if n == 0:
                            v.wait_ge(sq, mask_ld)
                        v.wait_ge(tm, c_mm[n])
                        assert inc(v.tensor_tensor(scs[n % 2][:, :], scs[n % 2][:, :],
                                                   abv(mask_o + 640 * qt, 640), ALU.add), "vv") == c_msk[n]
                    if 0 <= n - 3 < NIT:
                        m = n - 3
                        mh, mqt = items[m]
                        v.wait_ge(tm, c_pv[m])
                        assert inc(v.tensor_copy(abv(attn_o + 1024 * mh + 128 * mqt, 128),
                                                 pvs[m % 2][:, :]), "vv") == c_ae[m]
                    if 0 <= n - 1 < NIT:
                        m = n - 1
                        v.wait_ge(ss, c_exp[m])
                        assert inc(v.reciprocal(sums_sb[:, 2 * (m % 3) + 1:2 * (m % 3) + 2],
                                                sums_sb[:, 2 * (m % 3):2 * (m % 3) + 1]), "vv") == c_rec[m]
                        v.wait_ge(vv, c_rec[m])
                        assert inc(v.scalar_tensor_tensor(
                            Dm_sb[:, 128 * (m % 3):128 * (m % 3) + 128], d01_sb[:, :],
                            sums_sb[:, 2 * (m % 3) + 1:2 * (m % 3) + 2], d01_sb[:, :],
                            ALU.mult, ALU.mult), "vv") == c_D[m]

            @block.scalar
            def _(s):
                s.wait_ge(sq, p0_sq)
                for n in range(NIT + 2):
                    if n < NIT:
                        s.wait_ge(vv, c_msk[n])
                        if n >= 3:
                            s.wait_ge(tm, c_tD[n - 3])
                        got = inc(
                            s.activation(abv(probs_o + 640 * (n % 3), 640), scs[n % 2][:, :], AF.Exp,
                                         bias=0.0, scale=1.0 / 128.0,
                                         accum_out=sums_sb[:, 2 * (n % 3):2 * (n % 3) + 1]), "ss")
                        assert got == c_exp[n]
                    if 0 <= n - 2 < NIT:
                        m = n - 2
                        s.wait_ge(tm, c_tD[m])
                        got = inc(s.copy(abv(pT_o + 640 * (m % 2), 640), pTp[:, :]), "ss")
                        assert got == c_pTe[m]

            pre_wo = {}

            @block.gpsimd
            def _(g):
                g.wait_ge(tm, tm0_attn)
                for o in range(2):
                    pn = "we" if o % 2 == 0 else "wo"
                    pre_wo[o] = wdma(g, pn, abv(wbuf_o + 2048 * (o % 2), 2048), wo_os[o, :, :])

            if debug and "attn" in dumps:
                @block.gpsimd
                def _(g):
                    g.wait_ge(vv, cnt["vv"])
                    inc(g.dma_start(dbg["attn"][:, :], abv(attn_o, KC * T)), "so", 16)

            barrier()

        if upto <= 2:
            finish_prefix()
            return nc

        with nc.psum_tensor("psW0", [128, 1024], F32) as psW0, \
             nc.psum_tensor("psW1", [128, 1024], F32) as psW1:
            psQ = [psW0, psW1]

            def proj_wo():
                mm = [None] * OC
                ev = [None] * OC
                for o in range(OC):
                    if o >= 2:
                        @block.gpsimd
                        def _(g, o=o):
                            g.wait_ge(tm, mm[o - 2])
                            wdma(g, "we" if o % 2 == 0 else "wo",
                                 abv(wbuf_o + 2048 * (o % 2), 2048), wo_os[o, :, :])
                        ld = cnt["we" if o % 2 == 0 else "wo"]
                    else:
                        ld = pre_wo[o]

                    @block.tensor
                    def _(t, o=o, ld=ld):
                        t.wait_ge(we if o % 2 == 0 else wo, ld)
                        if o >= 2:
                            t.wait_ge(vv, ev[o - 2])
                        last = None
                        for tt in range(2):
                            for k in range(KC):
                                last = t.matmul(psQ[o % 2][:, 512 * tt:512 * tt + 512],
                                                abv(wbuf_o + 2048 * (o % 2) + 128 * k, 128),
                                                abv(attn_o + 1024 * k + 512 * tt, 512),
                                                start=(k == 0), stop=(k == KC - 1))
                        mm[o] = inc(last, "tm")

                    @block.vector
                    def _(v, o=o):
                        v.wait_ge(tm, mm[o])
                        ev[o] = inc(v.tensor_tensor(S_sb[:, 1024 * o:1024 * o + 1024],
                                                    S_sb[:, 1024 * o:1024 * o + 1024],
                                                    psQ[o % 2][:, :], ALU.add), "vv")
            proj_wo()
            barrier()

        if upto <= 3:
            finish_prefix()
            return nc

        _ln_id = [0]

        def emit_ln(write_bf16_to=None):
            _ln_id[0] += 1
            li = _ln_id[0]
            with nc.psum_tensor(f"psS1_{li}", [1, 1024], F32) as psS1, \
                 nc.psum_tensor(f"psS2_{li}", [1, 1024], F32) as psS2, \
                 nc.psum_tensor(f"psB1_{li}", [128, 1024], F32) as psB1, \
                 nc.psum_tensor(f"psB2_{li}", [128, 1024], F32) as psB2:
                @block.tensor
                def _(t):
                    t.wait_ge(sq, p0_sq)
                    last = None
                    for tt in range(2):
                        for k in range(KC):
                            last = t.matmul(psS1[:1, 512 * tt:512 * tt + 512], onc_sb[:, :1],
                                            S_sb[:, 1024 * k + 512 * tt:1024 * k + 512 * tt + 512],
                                            start=(k == 0), stop=(k == KC - 1))
                    inc(last, "tm")
                tm0_sq = cnt["tm"]
                c_smm = [tm0_sq + j + 1 for j in range(2 * KC)]
                ss0_sq = cnt["ss"]
                c_sq = [ss0_sq + j + 1 for j in range(2 * KC)]

                @block.scalar
                def _(s):
                    for j in range(2 * KC):
                        tt, k = j // KC, j % KC
                        if j >= 2:
                            s.wait_ge(tm, c_smm[j - 2])
                        got = inc(
                            s.activation(sq_sb[:, 512 * (j % 2):512 * (j % 2) + 512],
                                         S_sb[:, 1024 * k + 512 * tt:1024 * k + 512 * tt + 512],
                                         AF.Square), "ss")
                        assert got == c_sq[j]

                @block.tensor
                def _(t):
                    for j in range(2 * KC):
                        tt, k = j // KC, j % KC
                        t.wait_ge(ss, c_sq[j])
                        got = t.matmul(psS2[:1, 512 * tt:512 * tt + 512], onc_sb[:, :1],
                                       sq_sb[:, 512 * (j % 2):512 * (j % 2) + 512],
                                       start=(k == 0), stop=False)
                        assert inc(got, "tm") == c_smm[j]
                    for tt in range(2):
                        inc(t.matmul(psS2[:1, 512 * tt:512 * tt + 512], onr_sb[:1, :1],
                                     eps_sb[:1, :], start=False, stop=True), "tm")
                tm_sums = cnt["tm"]

                psM = sq_sb[0:1, 0:1024]
                psQ2 = st_sb[:1, 0:1024]
                @block.vector
                def _(v):
                    v.wait_ge(tm, tm_sums)
                    inc(v.tensor_scalar(psM, psS1[:1, :], 1.0 / D, None, ALU.mult), "vv")
                c1 = cnt["vv"]

                @block.scalar
                def _(s):
                    s.wait_ge(vv, c1)
                    inc(s.activation(psQ2, psM, AF.Square), "ss")
                c2 = cnt["ss"]

                @block.vector
                def _(v):
                    v.wait_ge(ss, c2)
                    inc(v.scalar_tensor_tensor(psS2[:1, :], psS2[:1, :], 1.0 / D,
                                               psQ2, ALU.mult, ALU.subtract), "vv")
                    v.wait_ge(vv, cnt["vv"])
                    inc(v.reciprocal(psS2[:1, :], psS2[:1, :]), "vv")
                c3 = cnt["vv"]

                @block.scalar
                def _(s):
                    s.wait_ge(vv, c3)
                    inc(s.activation(psS2[:1, :], psS2[:1, :], AF.Sqrt), "ss")
                c4 = cnt["ss"]

                @block.vector
                def _(v):
                    v.wait_ge(ss, c4)
                    inc(v.tensor_tensor(psQ2, psM, psS2[:1, :], ALU.mult), "vv")
                    v.wait_ge(vv, cnt["vv"])
                    inc(v.tensor_copy(psM, psS2[:1, :]), "vv")
                c5 = cnt["vv"]

                @block.tensor
                def _(t):
                    t.wait_ge(vv, c5)
                    t.matmul(psB1[:, 0:512], onr_sb[:1, :], sq_sb[0:1, 0:512], start=True, stop=True)
                    t.matmul(psB1[:, 512:1024], onr_sb[:1, :], sq_sb[0:1, 512:1024], start=True, stop=True)
                    t.matmul(psB2[:, 0:512], onr_sb[:1, :], st_sb[:1, 0:512], start=True, stop=True)
                    inc(t.matmul(psB2[:, 512:1024], onr_sb[:1, :], st_sb[:1, 512:1024],
                                 start=True, stop=True), "tm")
                tm_bc = cnt["tm"]

                ln_done = [None] * KC

                @block.vector
                def _(v, ln_done=ln_done):
                    v.wait_ge(tm, tm_bc)
                    for k in range(KC):
                        sl = S_sb[:, 1024 * k:1024 * k + 1024]
                        inc(v.tensor_tensor(sl, sl, psB1[:, :], ALU.mult), "vv")
                    mid = cnt["vv"]
                    v.wait_ge(vv, mid)
                    for k in range(KC):
                        sl = S_sb[:, 1024 * k:1024 * k + 1024]
                        ln_done[k] = inc(v.tensor_tensor(sl, sl, psB2[:, :], ALU.subtract), "vv")

                if write_bf16_to is not None:
                    @block.scalar
                    def _(s, ln_done=ln_done):
                        for k in range(KC):
                            s.wait_ge(vv, ln_done[k])
                            inc(s.copy(abv(write_bf16_to + 1024 * k, 1024),
                                       S_sb[:, 1024 * k:1024 * k + 1024]), "ss")

                barrier()
                return ln_done

        emit_ln(write_bf16_to=y1_o)

        if debug and "y1" in dumps:
            @block.sync
            def _(sy):
                sy.wait_ge(ss, cnt["ss"])
                inc(sy.dma_start(dbg["y1"][:, :], S_sb[:, :]), "so", 16)
            barrier()

        if upto <= 4:
            finish_prefix()
            return nc

        pre_gu = {}
        pre_wd = {}

        @block.gpsimd
        def _(g):
            for i in range(2):
                e = wdma(g, "we", abv(wgu_o + 4096 * (i % 2), 2048), wg_is[i, :, :])
                o_ = wdma(g, "wo", abv(wgu_o + 4096 * (i % 2) + 2048, 2048), wu_is[i, :, :])
                pre_gu[(0, i)] = (e, o_)

        for half in range(2):
            with nc.psum_tensor(f"psG0_{half}", [128, 1024], F32) as psG0, \
                 nc.psum_tensor(f"psU0_{half}", [128, 1024], F32) as psU0, \
                 nc.psum_tensor(f"psG1_{half}", [128, 1024], F32) as psG1, \
                 nc.psum_tensor(f"psU1_{half}", [128, 1024], F32) as psU1:
                psG = [psG0, psG1]
                psU = [psU0, psU1]
                c_ms = [None] * 32
                c_g = [[None, None] for _ in range(32)]
                c_u = [None] * 32
                ss0_ffn = cnt["ss"]
                c_sil = [[ss0_ffn + 2 * i + tt + 1 for tt in range(2)] for i in range(32)]
                c_h = [[None, None] for _ in range(32)]

                @block.gpsimd
                def _(g, half=half):
                    for o in range(2):
                        pn = "we" if o % 2 == 0 else "wo"
                        pre_wd[(half, o)] = wdma(g, pn, abv(wd_o + 4096 * (o % 2), 4096),
                                                 wd_os[half, o, :, :])

                for i in range(32):
                    ii = 32 * half + i

                    if i >= 2:
                        @block.gpsimd
                        def _(g, i=i, ii=ii):
                            g.wait_ge(tm, c_u[i - 2])
                            wdma(g, "we", abv(wgu_o + 4096 * (i % 2), 2048), wg_is[ii, :, :])
                            wdma(g, "wo", abv(wgu_o + 4096 * (i % 2) + 2048, 2048), wu_is[ii, :, :])
                        ld = (cnt["we"], cnt["wo"])
                    else:
                        ld = pre_gu[(half, i)]

                    @block.tensor
                    def _(t, i=i, ld=ld):
                        t.wait_ge(we, ld[0])
                        t.wait_ge(wo, ld[1])
                        if i >= 2:
                            t.wait_ge(ss, c_sil[i - 2][1])
                            t.wait_ge(vv, c_h[i - 2][1])
                        for tt in range(2):
                            last = None
                            for k in range(KC):
                                last = t.matmul(psG[i % 2][:, 512 * tt:512 * tt + 512],
                                                abv(wgu_o + 4096 * (i % 2) + 128 * k, 128),
                                                abv(y1_o + 1024 * k + 512 * tt, 512),
                                                start=(k == 0), stop=(k == KC - 1))
                            c_g[i][tt] = inc(last, "tm")
                        last = None
                        for tt in range(2):
                            for k in range(KC):
                                last = t.matmul(psU[i % 2][:, 512 * tt:512 * tt + 512],
                                                abv(wgu_o + 4096 * (i % 2) + 2048 + 128 * k, 128),
                                                abv(y1_o + 1024 * k + 512 * tt, 512),
                                                start=(k == 0), stop=(k == KC - 1))
                        c_u[i] = inc(last, "tm")

                    for tt in range(2):
                        @block.vector
                        def _(v, i=i, ii=ii, tt=tt):
                            v.wait_ge(ss, c_sil[i][tt])
                            v.wait_ge(tm, c_u[i])
                            c_h[i][tt] = inc(
                                v.scalar_tensor_tensor(abv(h_o + 1024 * i + 512 * tt, 512),
                                                       psU[i % 2][:, 512 * tt:512 * tt + 512],
                                                       scl_sb[:, 64 + ii:64 + ii + 1],
                                                       sq_sb[:, 512 * tt:512 * tt + 512],
                                                       ALU.mult, ALU.mult), "vv")

                @block.scalar
                def _(s, half=half):
                    for i in range(32):
                        ii = 32 * half + i
                        for tt in range(2):
                            s.wait_ge(tm, c_g[i][tt])
                            if i >= 1:
                                s.wait_ge(vv, c_h[i - 1][tt])
                            got = inc(
                                s.activation(sq_sb[:, 512 * tt:512 * tt + 512],
                                             psG[i % 2][:, 512 * tt:512 * tt + 512], AF.Silu,
                                             bias=0.0, scale=scl_sb[:, ii:ii + 1]), "ss")
                            assert got == c_sil[i][tt]

                barrier()

            if debug and "h" in dumps and half == 0:
                @block.gpsimd
                def _(g):
                    g.wait_ge(vv, cnt["vv"])
                    inc(g.dma_start(dbg["h"][:, :], abv(h_o, 32 * 1024)), "so", 16)
                barrier()

            if upto <= 5 and half == 0:
                break

            with nc.psum_tensor(f"psD0_{half}", [128, 1024], F32) as psD0, \
                 nc.psum_tensor(f"psD1_{half}", [128, 1024], F32) as psD1, \
                 nc.psum_tensor(f"psH_{half}", [1, 1024], F32) as psH:
                psD = [psD0, psD1]
                c_wms = [None] * OC
                d_mm = [None] * OC
                d_ev = [None] * OC

                @block.tensor
                def _(t, half=half):
                    last = None
                    for i in range(32):
                        t.wait_ge(vv, c_h[i][1])
                        for tt in range(2):
                            last = t.matmul(psH[:1, 512 * tt:512 * tt + 512], oncb_sb[:, :1],
                                            abv(h_o + 1024 * i + 512 * tt, 512),
                                            start=(i == 0), stop=(i == 31))
                    inc(last, "tm")
                tm_hs = cnt["tm"]

                @block.vector
                def _(v):
                    v.wait_ge(tm, tm_hs)
                    inc(v.tensor_copy(st_sb[:1, 0:1024], psH[:1, :]), "vv")
                vv_hs = cnt["vv"]

                if half == 0:
                    @block.gpsimd
                    def _(g):
                        for i in range(2):
                            e = wdma(g, "we", abv(wgu_o + 4096 * (i % 2), 2048), wg_is[32 + i, :, :])
                            o_ = wdma(g, "wo", abv(wgu_o + 4096 * (i % 2) + 2048, 2048), wu_is[32 + i, :, :])
                            pre_gu[(1, i)] = (e, o_)

                for o in range(OC):
                    if o >= 2:
                        @block.gpsimd
                        def _(g, o=o, half=half):
                            g.wait_ge(tm, d_mm[o - 2])
                            wdma(g, "we" if o % 2 == 0 else "wo",
                                 abv(wd_o + 4096 * (o % 2), 4096), wd_os[half, o, :, :])
                        ld = cnt["we" if o % 2 == 0 else "wo"]
                    else:
                        ld = pre_wd[(half, o)]

                    @block.tensor
                    def _(t, o=o, ld=ld):
                        t.wait_ge(we if o % 2 == 0 else wo, ld)
                        if o >= 2:
                            t.wait_ge(vv, d_ev[o - 2])
                        if o == 0:
                            t.wait_ge(vv, vv_hs)
                        last = None
                        for tt in range(2):
                            t.matmul(psD[o % 2][:, 512 * tt:512 * tt + 512],
                                     nr_sb[:1, :], st_sb[:1, 512 * tt:512 * tt + 512],
                                     start=True, stop=False)
                            for i in range(32):
                                last = t.matmul(psD[o % 2][:, 512 * tt:512 * tt + 512],
                                                abv(wd_o + 4096 * (o % 2) + 128 * i, 128),
                                                abv(h_o + 1024 * i + 512 * tt, 512),
                                                start=False, stop=(i == 31))
                        d_mm[o] = inc(last, "tm")

                    @block.vector
                    def _(v, o=o):
                        v.wait_ge(tm, d_mm[o])
                        d_ev[o] = inc(
                            v.scalar_tensor_tensor(S_sb[:, 1024 * o:1024 * o + 1024],
                                                   psD[o % 2][:, :],
                                                   scl_sb[:, 128 + o:128 + o + 1],
                                                   S_sb[:, 1024 * o:1024 * o + 1024],
                                                   ALU.mult, ALU.add), "vv")

                barrier()
            if upto <= 6 and half == 0:
                break

        if upto <= 6:
            finish_prefix()
            return nc

        if debug and "r2" in dumps:
            @block.sync
            def _(sy):
                sy.wait_ge(vv, cnt["vv"])
                inc(sy.dma_start(dbg["r2"][:, :], S_sb[:, :]), "so", 16)
            barrier()

        ln2_done = emit_ln(write_bf16_to=None)

        @block.sync
        def _(sy):
            for k in range(KC):
                sy.wait_ge(vv, ln2_done[k])
                inc(sy.dma_start(y_out[:, 1024 * k:1024 * k + 1024],
                                 S_sb[:, 1024 * k:1024 * k + 1024]), "so", 16)
            sy.wait_ge(so, cnt["so"])

    return nc


def _prep_shared(inp):
    Wq = np.asarray(inp["L0_attn_Wq"])
    Wk = np.asarray(inp["L0_attn_Wk"])
    Wv = np.asarray(inp["L0_attn_Wv"])
    Wo = np.asarray(inp["L0_attn_Wo"])
    Wg = np.asarray(inp["L0_ffn_gate_w"])
    Wu = np.asarray(inp["L0_ffn_up_w"])
    Wd = np.asarray(inp["L0_ffn_down_w"])
    gs = np.asarray(inp["L0_ffn_gate_scale"])
    us = np.asarray(inp["L0_ffn_up_scale"])
    ds = np.asarray(inp["L0_ffn_down_scale"])

    def oslab(W, oc):
        return np.ascontiguousarray(
            W.reshape(KC, 128, oc, 128).transpose(2, 1, 0, 3).reshape(oc, 128, KC * 128))

    wq_os = oslab(Wq, OC).astype(np.float32)
    wk_os = oslab(Wk, HKV).astype(np.float32)
    wo_os = oslab(Wo, OC).astype(np.float32)
    wv_ks = np.ascontiguousarray(Wv.reshape(KC, 128, 512)).astype(np.float32)
    wg_is = np.ascontiguousarray(
        Wg.reshape(IC, 128, KC, 128).transpose(0, 3, 2, 1).reshape(IC, 128, KC * 128)).astype(np.int32)
    wu_is = np.ascontiguousarray(
        Wu.reshape(IC, 128, KC, 128).transpose(0, 3, 2, 1).reshape(IC, 128, KC * 128)).astype(np.int32)
    wd_os = np.ascontiguousarray(
        Wd.reshape(OC, 128, 2, 32, 128).transpose(2, 0, 4, 3, 1).reshape(2, OC, 128, 32 * 128)).astype(np.int32)
    scales = np.concatenate([gs.reshape(IC, 128).T, us.reshape(IC, 128).T,
                             ds.reshape(OC, 128).T], axis=1).astype(np.float32)
    d01 = np.eye(128).astype(ml_dtypes.bfloat16)
    ones_c = np.ones((128, 1), np.float32)
    ones_r = np.ones((1, 128), np.float32)
    negrow = np.full((1, 128), -128.0, np.float32)
    ones_cb = np.ones((128, 1), ml_dtypes.bfloat16)
    eps_row = np.full((1, 512), float(D) * EPS, np.float32)
    return dict(negrow=negrow, ones_cb=ones_cb, eps_row=eps_row, wq_os=wq_os, wk_os=wk_os, wv_ks=wv_ks, wo_os=wo_os,
                wg_is=wg_is, wu_is=wu_is, wd_os=wd_os, scales=scales,
                d01=d01, ones_c=ones_c, ones_r=ones_r)


def _mk_masks(part):
    r = np.arange(128)[:, None]
    c = np.arange(640)[None, :]
    band = (c > r) & (c <= r + 512)
    m = np.empty((NQT, 128, 640), np.float32)
    for qt in range(NQT):
        valid = band.copy()
        if part == 0:
            valid = valid & (c >= 512 - 128 * qt)
        m[qt] = np.where(valid, 0.0, NEG)
    return np.ascontiguousarray(m.transpose(1, 0, 2).reshape(128, NQT * 640)).astype(ml_dtypes.bfloat16)


def _prep_core(x, c):
    b, part = c // 4, c % 4
    tok0 = T * part
    xt = np.ascontiguousarray(np.asarray(x)[b].T)
    x_own = np.ascontiguousarray(
        xt[:, tok0:tok0 + T].reshape(KC, 128, T).transpose(1, 0, 2).reshape(128, KC * T)).astype(np.float32)
    win = np.zeros((D, TK), np.float32)
    lo = max(0, tok0 - HALO)
    win[:, HALO - (tok0 - lo):HALO] = xt[:, lo:tok0]
    win[:, HALO:] = xt[:, tok0:tok0 + T]
    x_all = np.ascontiguousarray(
        win.reshape(KC, 128, TK).transpose(1, 0, 2).reshape(128, KC * TK)).astype(np.float32)
    return dict(x_own=x_own, x_all=x_all, masks=_mk_masks(part))


def kernel(**inputs):
    shared = _prep_shared(inputs)
    x = inputs["L0_attn_hidden_states"]
    in_maps = []
    for c in range(NCORES):
        m = dict(shared)
        m.update(_prep_core(x, c))
        in_maps.append(m)
    nc = build_nc(debug=False)
    res = run_bass_kernel_spmd(nc, in_maps, core_ids=list(range(NCORES)))
    out = np.empty((B, S, D), np.float32)
    for c in range(NCORES):
        b, part = c // 4, c % 4
        y = np.asarray(res.results[c]["y"]).reshape(128, KC, T).transpose(1, 0, 2).reshape(D, T)
        out[b, T * part:T * part + T, :] = y.T
    return out


# version 10
# speedup vs baseline: 1.9278x; 1.0159x over previous
import contextlib
import numpy as np
import ml_dtypes
import concourse.bass as bass
import concourse.mybir as mybir
from concourse.bass_utils import run_bass_kernel_spmd

F32 = mybir.dt.float32
BF16 = mybir.dt.bfloat16
I32 = mybir.dt.int32
AF = mybir.ActivationFunctionType
ALU = mybir.AluOpType

B, S, D = 2, 4096, 2048
HQ, HKV, HD, WIN = 16, 4, 128, 512
NCORES = 8
T = 1024
HALO = 512
TK = T + HALO
KC = D // 128
OC = D // 128
IC = 8192 // 128
NQT = T // 128
NKT = TK // 128
EPS = 1e-5
NEG = -1.0e30


def build_nc(debug=False, upto=99, dumps=None):
    if dumps is None:
        dumps = {"qkv", "attn", "y1", "h", "r2"} if debug else set()
    nc = bass.Bass("TRN2", target_bir_lowering=False)

    x_own = nc.dram_tensor("x_own", [128, KC * T], F32, kind="ExternalInput")
    x_all = nc.dram_tensor("x_all", [128, KC * TK], F32, kind="ExternalInput")
    masks = nc.dram_tensor("masks", [128, NQT * 640], BF16, kind="ExternalInput")
    wq_os = nc.dram_tensor("wq_os", [OC, 128, 2048], F32, kind="ExternalInput")
    wk_os = nc.dram_tensor("wk_os", [HKV, 128, 2048], F32, kind="ExternalInput")
    wv_ks = nc.dram_tensor("wv_ks", [KC, 128, 512], F32, kind="ExternalInput")
    wo_os = nc.dram_tensor("wo_os", [OC, 128, 2048], F32, kind="ExternalInput")
    wg_is = nc.dram_tensor("wg_is", [IC, 128, 2048], I32, kind="ExternalInput")
    wu_is = nc.dram_tensor("wu_is", [IC, 128, 2048], I32, kind="ExternalInput")
    wd_os = nc.dram_tensor("wd_os", [2, OC, 128, 4096], I32, kind="ExternalInput")
    scales = nc.dram_tensor("scales", [128, 144], F32, kind="ExternalInput")
    d01_h = nc.dram_tensor("d01", [128, 128], BF16, kind="ExternalInput")
    ones_c = nc.dram_tensor("ones_c", [128, 1], F32, kind="ExternalInput")
    ones_r = nc.dram_tensor("ones_r", [1, 128], F32, kind="ExternalInput")
    negrow = nc.dram_tensor("negrow", [1, 128], F32, kind="ExternalInput")
    ones_cb = nc.dram_tensor("ones_cb", [128, 1], BF16, kind="ExternalInput")
    eps_row = nc.dram_tensor("eps_row", [1, 512], F32, kind="ExternalInput")
    y_out = nc.dram_tensor("y", [128, KC * T], F32, kind="ExternalOutput")

    dbg = {}
    if debug:
        dbg["q"] = nc.dram_tensor("dbg_q", [128, HQ * T], F32, kind="ExternalOutput")
        dbg["k"] = nc.dram_tensor("dbg_k", [128, HKV * TK], F32, kind="ExternalOutput")
        dbg["vT"] = nc.dram_tensor("dbg_vT", [128, NKT * 512], F32, kind="ExternalOutput")
        dbg["attn"] = nc.dram_tensor("dbg_attn", [128, KC * T], F32, kind="ExternalOutput")
        dbg["y1"] = nc.dram_tensor("dbg_y1", [128, KC * T], F32, kind="ExternalOutput")
        dbg["h"] = nc.dram_tensor("dbg_h", [128, 32 * T], F32, kind="ExternalOutput")
        dbg["r2"] = nc.dram_tensor("dbg_r2", [128, KC * T], F32, kind="ExternalOutput")

    with contextlib.ExitStack() as ctx:
        S_sb = ctx.enter_context(nc.sbuf_tensor("S_sb", [128, KC * T], F32))
        ab = ctx.enter_context(nc.sbuf_tensor("ab", [128, 65536], BF16))
        sq_sb = ctx.enter_context(nc.sbuf_tensor("sq_sb", [128, 1024], F32))
        st_sb = ctx.enter_context(nc.sbuf_tensor("st_sb", [1, 1024], F32))
        scl_sb = ctx.enter_context(nc.sbuf_tensor("scl_sb", [128, 144], F32))
        d01_sb = ctx.enter_context(nc.sbuf_tensor("d01_sb", [128, 128], BF16))
        onc_sb = ctx.enter_context(nc.sbuf_tensor("onc_sb", [128, 1], F32))
        onr_sb = ctx.enter_context(nc.sbuf_tensor("onr_sb", [1, 128], F32))
        sums_sb = ctx.enter_context(nc.sbuf_tensor("sums_sb", [128, 6], F32))
        Dm_sb = ctx.enter_context(nc.sbuf_tensor("Dm_sb", [128, 384], BF16))
        nr_sb = ctx.enter_context(nc.sbuf_tensor("nr_sb", [1, 128], F32))
        oncb_sb = ctx.enter_context(nc.sbuf_tensor("oncb_sb", [128, 1], BF16))
        eps_sb = ctx.enter_context(nc.sbuf_tensor("eps_sb", [1, 512], F32))

        def abv(off, n):
            return ab[:, off:off + n]

        x_bf_o, q_bf_o, k_bf_o, vT_o, wvres_o, wbuf_o = 0, 24576, 40960, 47104, 53248, 61440
        attn_o, probs_o, pT_o, mask_o = 0, 16384, 18304, 53248
        h_o, y1_o, wgu_o, wd_o = 0, 32768, 49152, 57344

        sg = ctx.enter_context(nc.semaphore("sg"))
        sq = ctx.enter_context(nc.semaphore("sq"))
        tm = ctx.enter_context(nc.semaphore("tm"))
        vv = ctx.enter_context(nc.semaphore("vv"))
        ss = ctx.enter_context(nc.semaphore("ss"))
        so = ctx.enter_context(nc.semaphore("so"))
        we = ctx.enter_context(nc.semaphore("we"))
        wo = ctx.enter_context(nc.semaphore("wo"))
        sx = ctx.enter_context(nc.semaphore("sx"))
        sh = ctx.enter_context(nc.semaphore("sh"))
        sems = {"sg": sg, "sq": sq, "tm": tm, "vv": vv, "ss": ss, "so": so,
                "we": we, "wo": wo, "sx": sx, "sh": sh}
        cnt = {k: 0 for k in sems}

        def inc(instr, name, step=1):
            cnt[name] += step
            instr.then_inc(sems[name], step)
            return cnt[name]

        def wdma(g, pn, dst, src):
            g.wait_ge(sems[pn], cnt[pn])
            return inc(g.dma_start(dst, src), pn, 16)

        block = ctx.enter_context(nc.Block())

        _bar_prev = [{k: 0 for k in sems}]

        def barrier():
            snap = {k: cnt[k] for k in sems}
            prev = _bar_prev[0]
            own_of = {"tm": block.tensor, "vv": block.vector, "ss": block.scalar}
            for own, engfn in (("tm", block.tensor), ("vv", block.vector),
                               ("ss", block.scalar), (None, block.gpsimd),
                               (None, block.sync)):
                def _f(e, snap=snap, own=own, prev=dict(prev)):
                    for k, v in snap.items():
                        if v > 0 and k != own and v > prev[k]:
                            e.wait_ge(sems[k], v)
                engfn(_f)
            _bar_prev[0] = snap


        def finish_prefix():
            @block.sync
            def _(sy):
                sy.wait_ge(so, cnt["so"])
        @block.sync
        def _(sy):
            inc(sy.dma_start(S_sb[:, :], x_own[:, :]), "sq", 16)
            inc(sy.dma_start(scl_sb[:, :], scales[:, :]), "sq", 16)
            inc(sy.dma_start(d01_sb[:, :], d01_h[:, :]), "sq", 16)
            inc(sy.dma_start(onc_sb[:, :], ones_c[:, :]), "sq", 16)
            inc(sy.dma_start(onr_sb[:1, :], ones_r[:1, :]), "sq", 16)
            inc(sy.dma_start(nr_sb[:1, :], negrow[:1, :]), "sq", 16)
            inc(sy.dma_start(oncb_sb[:, :], ones_cb[:, :]), "sq", 16)
            inc(sy.dma_start(eps_sb[:1, :], eps_row[:1, :]), "sq", 16)
        p0_sq = cnt["sq"]

        @block.gpsimd
        def _(g):
            for k in range(KC):
                inc(g.dma_start(abv(wvres_o + 512 * k, 512), wv_ks[k, :, :]), "sg", 16)
            for k in range(KC):
                inc(g.dma_start(abv(x_bf_o + k * TK + HALO, T),
                                x_all[:, k * TK + HALO:(k + 1) * TK]), "sx", 16)
        own_ld = cnt["sx"]
        wv_only = cnt["sg"]

        @block.gpsimd
        def _(g):
            for k in range(KC):
                inc(g.dma_start(abv(x_bf_o + k * TK, HALO),
                                x_all[:, k * TK:k * TK + HALO]), "sh", 16)
        halo_ld = cnt["sh"]

        with nc.psum_tensor("psA", [128, 512], F32) as psA, \
             nc.psum_tensor("psB", [128, 512], F32) as psB, \
             nc.psum_tensor("psQ0", [128, 1024], F32) as psQ0, \
             nc.psum_tensor("psQ1", [128, 1024], F32) as psQ1:
            psV = [psA, psB]
            psQ = [psQ0, psQ1]

            vt_mm = [None] * NKT
            vt_ev = [None] * NKT
            vt_order = list(range(4, NKT)) + list(range(4))
            for jj, j in enumerate(vt_order):
                @block.tensor
                def _(t, j=j, jj=jj):
                    if jj == 0:
                        t.wait_ge(sg, wv_only)
                        t.wait_ge(sx, own_ld)
                    if j < 4 and jj == len(vt_order) - 4:
                        t.wait_ge(sh, halo_ld)
                    if jj >= 2:
                        t.wait_ge(vv, vt_ev[vt_order[jj - 2]])
                    ps = psV[jj % 2]
                    last = None
                    for k in range(KC):
                        last = t.matmul(ps[:, :], abv(x_bf_o + k * TK + j * 128, 128),
                                        abv(wvres_o + 512 * k, 512),
                                        start=(k == 0), stop=(k == KC - 1))
                    vt_mm[j] = inc(last, "tm")

                @block.vector
                def _(v, j=j, jj=jj):
                    v.wait_ge(tm, vt_mm[j])
                    vt_ev[j] = inc(v.tensor_copy(abv(vT_o + 512 * j, 512), psV[jj % 2][:, :]), "vv")

            def proj(noc, wsrc, rhs_off, rhs_stride, rhs_base, ntt, out_kind, out_off):
                mm = [None] * noc
                ev = [None] * noc
                tm0 = cnt["tm"]
                for o in range(noc):
                    @block.gpsimd
                    def _(g, o=o, tm0=tm0):
                        if o >= 2:
                            g.wait_ge(tm, mm[o - 2])
                        else:
                            g.wait_ge(tm, tm0)
                        wdma(g, "we" if o % 2 == 0 else "wo",
                             abv(wbuf_o + 2048 * (o % 2), 2048), wsrc[o, :, :])
                    ld = cnt["we" if o % 2 == 0 else "wo"]

                    @block.tensor
                    def _(t, o=o, ld=ld):
                        t.wait_ge(we if o % 2 == 0 else wo, ld)
                        if o >= 2:
                            t.wait_ge(vv, ev[o - 2])
                        last = None
                        if ntt == 2:
                            ps = psQ[o % 2]
                            for tt in range(2):
                                for k in range(KC):
                                    last = t.matmul(ps[:, 512 * tt:512 * tt + 512],
                                                    abv(wbuf_o + 2048 * (o % 2) + 128 * k, 128),
                                                    abv(rhs_base + k * rhs_stride + rhs_off + 512 * tt, 512),
                                                    start=(k == 0), stop=(k == KC - 1))
                        else:
                            for tt in range(3):
                                ps = psQ[o % 2] if tt < 2 else psV[o % 2]
                                sl = ps[:, 512 * tt:512 * tt + 512] if tt < 2 else ps[:, :]
                                for k in range(KC):
                                    last = t.matmul(sl,
                                                    abv(wbuf_o + 2048 * (o % 2) + 128 * k, 128),
                                                    abv(rhs_base + k * rhs_stride + rhs_off + 512 * tt, 512),
                                                    start=(k == 0), stop=(k == KC - 1))
                        mm[o] = inc(last, "tm")

                    @block.vector
                    def _(v, o=o):
                        v.wait_ge(tm, mm[o])
                        if out_kind == "copy":
                            if ntt == 2:
                                ev[o] = inc(v.tensor_copy(abv(out_off + 1024 * o, 1024),
                                                          psQ[o % 2][:, :]), "vv")
                            else:
                                v.tensor_copy(abv(out_off + 1536 * o, 1024), psQ[o % 2][:, :])
                                ev[o] = inc(v.tensor_copy(abv(out_off + 1536 * o + 1024, 512),
                                                          psV[o % 2][:, :]), "vv")
                        else:
                            ev[o] = inc(v.tensor_tensor(S_sb[:, 1024 * o:1024 * o + 1024],
                                                        S_sb[:, 1024 * o:1024 * o + 1024],
                                                        psQ[o % 2][:, :], ALU.add), "vv")
                return ev

            proj(OC, wq_os, HALO, TK, x_bf_o, 2, "copy", q_bf_o)
            vt_guard = cnt["vv"]

            @block.tensor
            def _(t):
                t.wait_ge(vv, vt_guard)
            proj(HKV, wk_os, 0, TK, x_bf_o, 3, "copy", k_bf_o)

            if debug and "qkv" in dumps:
                @block.gpsimd
                def _(g):
                    g.wait_ge(vv, cnt["vv"])
                    inc(g.dma_start(dbg["q"][:, :], abv(q_bf_o, HQ * 1024)), "so", 16)
                    inc(g.dma_start(dbg["k"][:, :], abv(k_bf_o, HKV * 1536)), "so", 16)
                    inc(g.dma_start(dbg["vT"][:, :], abv(vT_o, NKT * 512)), "so", 16)
                barrier()

            p1_vv = cnt["vv"]
            p1_tm = cnt["tm"]
            vt_last = max(vt_mm)

        if upto <= 1:
            finish_prefix()
            return nc

        @block.sync
        def _(sy):
            sy.wait_ge(sq, p0_sq)
            sy.wait_ge(tm, vt_last)
            inc(sy.dma_start(abv(mask_o, 8 * 640), masks[:, :]), "sq", 16)
        mask_ld = cnt["sq"]

        with nc.psum_tensor("sc0", [128, 640], F32) as sc0, \
             nc.psum_tensor("sc1", [128, 640], F32) as sc1, \
             nc.psum_tensor("pT", [128, 640], F32) as pTp, \
             nc.psum_tensor("pv0", [128, 128], F32) as pv0, \
             nc.psum_tensor("pv1", [128, 128], F32) as pv1:
            scs = [sc0, sc1]
            pvs = [pv0, pv1]
            items = [(h, qt) for h in range(HQ) for qt in range(NQT)]
            NIT = len(items)
            ss0_attn = cnt["ss"]
            tm0_attn = cnt["tm"]
            vv0_attn = cnt["vv"]
            c_mm = [None] * NIT
            c_tD = [None] * NIT
            c_pv = [None] * NIT
            c_msk = [None] * NIT
            c_ae = [None] * NIT
            c_rec = [None] * NIT
            c_D = [None] * NIT
            c_exp = [None] * NIT
            c_pTe = [None] * NIT
            _t, _v, _s = tm0_attn, vv0_attn, ss0_attn
            for n in range(NIT + 3):
                if n < NIT:
                    _t += 1; c_mm[n] = _t
                if 0 <= n - 2 < NIT:
                    _t += 1; c_tD[n - 2] = _t
                if 0 <= n - 3 < NIT:
                    _t += 1; c_pv[n - 3] = _t
                if n < NIT:
                    _v += 1; c_msk[n] = _v
                if 0 <= n - 3 < NIT:
                    _v += 1; c_ae[n - 3] = _v
                if 0 <= n - 1 < NIT:
                    _v += 1; c_rec[n - 1] = _v
                    _v += 1; c_D[n - 1] = _v
            for n in range(NIT + 2):
                if n < NIT:
                    _s += 1; c_exp[n] = _s
                if 0 <= n - 2 < NIT:
                    _s += 1; c_pTe[n - 2] = _s

            @block.tensor
            def _(t):
                t.wait_ge(vv, p1_vv)
                for n in range(NIT + 3):
                    if n < NIT:
                        h, qt = items[n]
                        if n >= 2:
                            t.wait_ge(ss, c_exp[n - 2])
                        qs = q_bf_o + 1024 * h + 128 * qt
                        ks = k_bf_o + 1536 * (h // 4) + 128 * qt
                        sc = scs[n % 2]
                        t.matmul(sc[:, 0:512], abv(qs, 128), abv(ks, 512), start=True, stop=True)
                        assert inc(t.matmul(sc[:, 512:640], abv(qs, 128), abv(ks + 512, 128),
                                            start=True, stop=True), "tm") == c_mm[n]
                    if 0 <= n - 2 < NIT:
                        m = n - 2
                        t.wait_ge(vv, c_D[m])
                        if m >= 1:
                            t.wait_ge(ss, c_pTe[m - 1])
                        last = None
                        for b in range(5):
                            last = t.matmul(pTp[:, 128 * b:128 * b + 128],
                                            abv(probs_o + 640 * (m % 3) + 128 * b, 128),
                                            Dm_sb[:, 128 * (m % 3):128 * (m % 3) + 128],
                                            start=True, stop=True)
                        assert inc(last, "tm") == c_tD[m]
                    if 0 <= n - 3 < NIT:
                        m = n - 3
                        t.wait_ge(ss, c_pTe[m])
                        if m >= 2:
                            t.wait_ge(vv, c_ae[m - 2])
                        mh, mqt = items[m]
                        last = None
                        for b in range(5):
                            last = t.matmul(pvs[m % 2][:, :],
                                            abv(vT_o + 512 * (mqt + b) + 128 * (mh // 4), 128),
                                            abv(pT_o + 640 * (m % 2) + 128 * b, 128),
                                            start=(b == 0), stop=(b == 4))
                        assert inc(last, "tm") == c_pv[m]

            @block.vector
            def _(v):
                for n in range(NIT + 3):
                    if n < NIT:
                        h, qt = items[n]
                        if n == 0:
                            v.wait_ge(sq, mask_ld)
                        v.wait_ge(tm, c_mm[n])
                        assert inc(v.tensor_tensor(scs[n % 2][:, :], scs[n % 2][:, :],
                                                   abv(mask_o + 640 * qt, 640), ALU.add), "vv") == c_msk[n]
                    if 0 <= n - 3 < NIT:
                        m = n - 3
                        mh, mqt = items[m]
                        v.wait_ge(tm, c_pv[m])
                        assert inc(v.tensor_copy(abv(attn_o + 1024 * mh + 128 * mqt, 128),
                                                 pvs[m % 2][:, :]), "vv") == c_ae[m]
                    if 0 <= n - 1 < NIT:
                        m = n - 1
                        v.wait_ge(ss, c_exp[m])
                        assert inc(v.reciprocal(sums_sb[:, 2 * (m % 3) + 1:2 * (m % 3) + 2],
                                                sums_sb[:, 2 * (m % 3):2 * (m % 3) + 1]), "vv") == c_rec[m]
                        v.wait_ge(vv, c_rec[m])
                        assert inc(v.scalar_tensor_tensor(
                            Dm_sb[:, 128 * (m % 3):128 * (m % 3) + 128], d01_sb[:, :],
                            sums_sb[:, 2 * (m % 3) + 1:2 * (m % 3) + 2], d01_sb[:, :],
                            ALU.mult, ALU.mult), "vv") == c_D[m]

            @block.scalar
            def _(s):
                s.wait_ge(sq, p0_sq)
                for n in range(NIT + 2):
                    if n < NIT:
                        s.wait_ge(vv, c_msk[n])
                        if n >= 3:
                            s.wait_ge(tm, c_tD[n - 3])
                        got = inc(
                            s.activation(abv(probs_o + 640 * (n % 3), 640), scs[n % 2][:, :], AF.Exp,
                                         bias=0.0, scale=1.0 / 128.0,
                                         accum_out=sums_sb[:, 2 * (n % 3):2 * (n % 3) + 1]), "ss")
                        assert got == c_exp[n]
                    if 0 <= n - 2 < NIT:
                        m = n - 2
                        s.wait_ge(tm, c_tD[m])
                        got = inc(s.copy(abv(pT_o + 640 * (m % 2), 640), pTp[:, :]), "ss")
                        assert got == c_pTe[m]

            pre_wo = {}

            @block.gpsimd
            def _(g):
                g.wait_ge(tm, tm0_attn)
                for o in range(2):
                    pn = "we" if o % 2 == 0 else "wo"
                    pre_wo[o] = wdma(g, pn, abv(wbuf_o + 2048 * (o % 2), 2048), wo_os[o, :, :])

            if debug and "attn" in dumps:
                @block.gpsimd
                def _(g):
                    g.wait_ge(vv, cnt["vv"])
                    inc(g.dma_start(dbg["attn"][:, :], abv(attn_o, KC * T)), "so", 16)
                barrier()

            p2_vv = cnt["vv"]
            p2_ss = cnt["ss"]

        if upto <= 2:
            finish_prefix()
            return nc

        with nc.psum_tensor("psW0", [128, 1024], F32) as psW0, \
             nc.psum_tensor("psW1", [128, 1024], F32) as psW1:
            psQ = [psW0, psW1]

            def proj_wo():
                mm = [None] * OC
                ev = [None] * OC

                @block.tensor
                def _(t):
                    t.wait_ge(vv, p2_vv)
                    t.wait_ge(ss, p2_ss)

                for o in range(OC):
                    if o >= 2:
                        @block.gpsimd
                        def _(g, o=o):
                            g.wait_ge(tm, mm[o - 2])
                            wdma(g, "we" if o % 2 == 0 else "wo",
                                 abv(wbuf_o + 2048 * (o % 2), 2048), wo_os[o, :, :])
                        ld = cnt["we" if o % 2 == 0 else "wo"]
                    else:
                        ld = pre_wo[o]

                    @block.tensor
                    def _(t, o=o, ld=ld):
                        t.wait_ge(we if o % 2 == 0 else wo, ld)
                        if o >= 2:
                            t.wait_ge(vv, ev[o - 2])
                        last = None
                        for tt in range(2):
                            for k in range(KC):
                                last = t.matmul(psQ[o % 2][:, 512 * tt:512 * tt + 512],
                                                abv(wbuf_o + 2048 * (o % 2) + 128 * k, 128),
                                                abv(attn_o + 1024 * k + 512 * tt, 512),
                                                start=(k == 0), stop=(k == KC - 1))
                        mm[o] = inc(last, "tm")

                    @block.vector
                    def _(v, o=o):
                        v.wait_ge(tm, mm[o])
                        ev[o] = inc(v.tensor_tensor(S_sb[:, 1024 * o:1024 * o + 1024],
                                                    S_sb[:, 1024 * o:1024 * o + 1024],
                                                    psQ[o % 2][:, :], ALU.add), "vv")
            proj_wo()
            barrier()

        if upto <= 3:
            finish_prefix()
            return nc

        _ln_id = [0]

        def emit_ln(write_bf16_to=None, rbf_o=None):
            _ln_id[0] += 1
            li = _ln_id[0]
            with nc.psum_tensor(f"psS1_{li}", [1, 1024], F32) as psS1, \
                 nc.psum_tensor(f"psS2_{li}", [1, 1024], F32) as psS2, \
                 nc.psum_tensor(f"psB1_{li}", [128, 1024], F32) as psB1, \
                 nc.psum_tensor(f"psB2_{li}", [128, 1024], F32) as psB2:
                ss0_rc = cnt["ss"]
                c_rc = [ss0_rc + k + 1 for k in range(KC)]

                @block.scalar
                def _(s):
                    for k in range(KC):
                        got = inc(s.copy(abv(rbf_o + 1024 * k, 1024),
                                         S_sb[:, 1024 * k:1024 * k + 1024]), "ss")
                        assert got == c_rc[k]

                @block.tensor
                def _(t):
                    t.wait_ge(sq, p0_sq)
                    last = None
                    for tt in range(2):
                        for k in range(KC):
                            if tt == 0:
                                t.wait_ge(ss, c_rc[k])
                            last = t.matmul(psS1[:1, 512 * tt:512 * tt + 512], oncb_sb[:, :1],
                                            abv(rbf_o + 1024 * k + 512 * tt, 512),
                                            start=(k == 0), stop=(k == KC - 1))
                    inc(last, "tm")
                tm0_sq = cnt["tm"]
                c_smm = [tm0_sq + j + 1 for j in range(2 * KC)]
                ss0_sq = cnt["ss"]
                c_sq = [ss0_sq + j + 1 for j in range(2 * KC)]

                @block.scalar
                def _(s):
                    for j in range(2 * KC):
                        tt, k = j // KC, j % KC
                        if j >= 2:
                            s.wait_ge(tm, c_smm[j - 2])
                        got = inc(
                            s.activation(abv(rbf_o + 16384 + 512 * (j % 2), 512),
                                         S_sb[:, 1024 * k + 512 * tt:1024 * k + 512 * tt + 512],
                                         AF.Square), "ss")
                        assert got == c_sq[j]

                @block.tensor
                def _(t):
                    for j in range(2 * KC):
                        tt, k = j // KC, j % KC
                        t.wait_ge(ss, c_sq[j])
                        got = t.matmul(psS2[:1, 512 * tt:512 * tt + 512], oncb_sb[:, :1],
                                       abv(rbf_o + 16384 + 512 * (j % 2), 512),
                                       start=(k == 0), stop=False)
                        assert inc(got, "tm") == c_smm[j]
                    for tt in range(2):
                        inc(t.matmul(psS2[:1, 512 * tt:512 * tt + 512], onr_sb[:1, :1],
                                     eps_sb[:1, :], start=False, stop=True), "tm")
                tm_sums = cnt["tm"]

                psM = sq_sb[0:1, 0:1024]
                psQ2 = st_sb[:1, 0:1024]
                @block.vector
                def _(v):
                    v.wait_ge(tm, tm_sums)
                    inc(v.tensor_scalar(psM, psS1[:1, :], 1.0 / D, None, ALU.mult), "vv")
                c1 = cnt["vv"]

                @block.scalar
                def _(s):
                    s.wait_ge(vv, c1)
                    inc(s.activation(psQ2, psM, AF.Square), "ss")
                c2 = cnt["ss"]

                @block.vector
                def _(v):
                    v.wait_ge(ss, c2)
                    inc(v.scalar_tensor_tensor(psS2[:1, :], psS2[:1, :], 1.0 / D,
                                               psQ2, ALU.mult, ALU.subtract), "vv")
                    v.wait_ge(vv, cnt["vv"])
                    inc(v.reciprocal(psS2[:1, :], psS2[:1, :]), "vv")
                c3 = cnt["vv"]

                @block.scalar
                def _(s):
                    s.wait_ge(vv, c3)
                    inc(s.activation(psS2[:1, :], psS2[:1, :], AF.Sqrt), "ss")
                c4 = cnt["ss"]

                @block.vector
                def _(v):
                    v.wait_ge(ss, c4)
                    inc(v.tensor_tensor(psQ2, psM, psS2[:1, :], ALU.mult), "vv")
                    v.wait_ge(vv, cnt["vv"])
                    inc(v.tensor_copy(psM, psS2[:1, :]), "vv")
                c5 = cnt["vv"]

                @block.tensor
                def _(t):
                    t.wait_ge(vv, c5)
                    t.matmul(psB1[:, 0:512], onr_sb[:1, :], sq_sb[0:1, 0:512], start=True, stop=True)
                    t.matmul(psB1[:, 512:1024], onr_sb[:1, :], sq_sb[0:1, 512:1024], start=True, stop=True)
                    t.matmul(psB2[:, 0:512], onr_sb[:1, :], st_sb[:1, 0:512], start=True, stop=True)
                    inc(t.matmul(psB2[:, 512:1024], onr_sb[:1, :], st_sb[:1, 512:1024],
                                 start=True, stop=True), "tm")
                tm_bc = cnt["tm"]

                ln_done = [None] * KC

                @block.vector
                def _(v, ln_done=ln_done):
                    v.wait_ge(tm, tm_bc)
                    for k in range(KC):
                        sl = S_sb[:, 1024 * k:1024 * k + 1024]
                        inc(v.tensor_tensor(sl, sl, psB1[:, :], ALU.mult), "vv")
                    mid = cnt["vv"]
                    v.wait_ge(vv, mid)
                    for k in range(KC):
                        sl = S_sb[:, 1024 * k:1024 * k + 1024]
                        ln_done[k] = inc(v.tensor_tensor(sl, sl, psB2[:, :], ALU.subtract), "vv")

                if write_bf16_to is not None:
                    @block.scalar
                    def _(s, ln_done=ln_done):
                        for k in range(KC):
                            s.wait_ge(vv, ln_done[k])
                            inc(s.copy(abv(write_bf16_to + 1024 * k, 1024),
                                       S_sb[:, 1024 * k:1024 * k + 1024]), "ss")

                barrier()
                return ln_done

        emit_ln(write_bf16_to=y1_o, rbf_o=y1_o)

        if debug and "y1" in dumps:
            @block.sync
            def _(sy):
                sy.wait_ge(ss, cnt["ss"])
                inc(sy.dma_start(dbg["y1"][:, :], S_sb[:, :]), "so", 16)
            barrier()

        if upto <= 4:
            finish_prefix()
            return nc

        pre_gu = {}
        pre_wd = {}

        @block.gpsimd
        def _(g):
            for i in range(2):
                e = wdma(g, "we", abv(wgu_o + 4096 * (i % 2), 2048), wg_is[i, :, :])
                o_ = wdma(g, "wo", abv(wgu_o + 4096 * (i % 2) + 2048, 2048), wu_is[i, :, :])
                pre_gu[(0, i)] = (e, o_)

        for half in range(2):
            with nc.psum_tensor(f"psG0_{half}", [128, 1024], F32) as psG0, \
                 nc.psum_tensor(f"psU0_{half}", [128, 1024], F32) as psU0, \
                 nc.psum_tensor(f"psG1_{half}", [128, 1024], F32) as psG1, \
                 nc.psum_tensor(f"psU1_{half}", [128, 1024], F32) as psU1:
                psG = [psG0, psG1]
                psU = [psU0, psU1]
                c_ms = [None] * 32
                c_g = [[None, None] for _ in range(32)]
                c_u = [None] * 32
                ss0_ffn = cnt["ss"]
                c_sil = [[ss0_ffn + 2 * i + tt + 1 for tt in range(2)] for i in range(32)]
                c_h = [[None, None] for _ in range(32)]

                @block.gpsimd
                def _(g, half=half):
                    for o in range(2):
                        pn = "we" if o % 2 == 0 else "wo"
                        pre_wd[(half, o)] = wdma(g, pn, abv(wd_o + 4096 * (o % 2), 4096),
                                                 wd_os[half, o, :, :])

                for i in range(32):
                    ii = 32 * half + i

                    if i >= 2:
                        @block.gpsimd
                        def _(g, i=i, ii=ii):
                            g.wait_ge(tm, c_u[i - 2])
                            wdma(g, "we", abv(wgu_o + 4096 * (i % 2), 2048), wg_is[ii, :, :])
                            wdma(g, "wo", abv(wgu_o + 4096 * (i % 2) + 2048, 2048), wu_is[ii, :, :])
                        ld = (cnt["we"], cnt["wo"])
                    else:
                        ld = pre_gu[(half, i)]

                    @block.tensor
                    def _(t, i=i, ld=ld):
                        t.wait_ge(we, ld[0])
                        t.wait_ge(wo, ld[1])
                        if i >= 2:
                            t.wait_ge(ss, c_sil[i - 2][1])
                            t.wait_ge(vv, c_h[i - 2][1])
                        for tt in range(2):
                            last = None
                            for k in range(KC):
                                last = t.matmul(psG[i % 2][:, 512 * tt:512 * tt + 512],
                                                abv(wgu_o + 4096 * (i % 2) + 128 * k, 128),
                                                abv(y1_o + 1024 * k + 512 * tt, 512),
                                                start=(k == 0), stop=(k == KC - 1))
                            c_g[i][tt] = inc(last, "tm")
                        last = None
                        for tt in range(2):
                            for k in range(KC):
                                last = t.matmul(psU[i % 2][:, 512 * tt:512 * tt + 512],
                                                abv(wgu_o + 4096 * (i % 2) + 2048 + 128 * k, 128),
                                                abv(y1_o + 1024 * k + 512 * tt, 512),
                                                start=(k == 0), stop=(k == KC - 1))
                        c_u[i] = inc(last, "tm")

                    for tt in range(2):
                        @block.vector
                        def _(v, i=i, ii=ii, tt=tt):
                            v.wait_ge(ss, c_sil[i][tt])
                            v.wait_ge(tm, c_u[i])
                            c_h[i][tt] = inc(
                                v.scalar_tensor_tensor(abv(h_o + 1024 * i + 512 * tt, 512),
                                                       psU[i % 2][:, 512 * tt:512 * tt + 512],
                                                       scl_sb[:, 64 + ii:64 + ii + 1],
                                                       sq_sb[:, 512 * tt:512 * tt + 512],
                                                       ALU.mult, ALU.mult), "vv")

                @block.scalar
                def _(s, half=half):
                    for i in range(32):
                        ii = 32 * half + i
                        for tt in range(2):
                            s.wait_ge(tm, c_g[i][tt])
                            if i >= 1:
                                s.wait_ge(vv, c_h[i - 1][tt])
                            got = inc(
                                s.activation(sq_sb[:, 512 * tt:512 * tt + 512],
                                             psG[i % 2][:, 512 * tt:512 * tt + 512], AF.Silu,
                                             bias=0.0, scale=scl_sb[:, ii:ii + 1]), "ss")
                            assert got == c_sil[i][tt]

                barrier()

            if debug and "h" in dumps and half == 0:
                @block.gpsimd
                def _(g):
                    g.wait_ge(vv, cnt["vv"])
                    inc(g.dma_start(dbg["h"][:, :], abv(h_o, 32 * 1024)), "so", 16)
                barrier()

            if upto <= 5 and half == 0:
                break

            with nc.psum_tensor(f"psD0_{half}", [128, 1024], F32) as psD0, \
                 nc.psum_tensor(f"psD1_{half}", [128, 1024], F32) as psD1, \
                 nc.psum_tensor(f"psH_{half}", [1, 1024], F32) as psH:
                psD = [psD0, psD1]
                c_wms = [None] * OC
                d_mm = [None] * OC
                d_ev = [None] * OC

                @block.tensor
                def _(t, half=half):
                    last = None
                    for i in range(32):
                        t.wait_ge(vv, c_h[i][1])
                        for tt in range(2):
                            last = t.matmul(psH[:1, 512 * tt:512 * tt + 512], oncb_sb[:, :1],
                                            abv(h_o + 1024 * i + 512 * tt, 512),
                                            start=(i == 0), stop=(i == 31))
                    inc(last, "tm")
                tm_hs = cnt["tm"]

                @block.vector
                def _(v):
                    v.wait_ge(tm, tm_hs)
                    inc(v.tensor_copy(st_sb[:1, 0:1024], psH[:1, :]), "vv")
                vv_hs = cnt["vv"]

                if half == 0:
                    @block.gpsimd
                    def _(g):
                        for i in range(2):
                            e = wdma(g, "we", abv(wgu_o + 4096 * (i % 2), 2048), wg_is[32 + i, :, :])
                            o_ = wdma(g, "wo", abv(wgu_o + 4096 * (i % 2) + 2048, 2048), wu_is[32 + i, :, :])
                            pre_gu[(1, i)] = (e, o_)

                for o in range(OC):
                    if o >= 2:
                        @block.gpsimd
                        def _(g, o=o, half=half):
                            g.wait_ge(tm, d_mm[o - 2])
                            wdma(g, "we" if o % 2 == 0 else "wo",
                                 abv(wd_o + 4096 * (o % 2), 4096), wd_os[half, o, :, :])
                        ld = cnt["we" if o % 2 == 0 else "wo"]
                    else:
                        ld = pre_wd[(half, o)]

                    @block.tensor
                    def _(t, o=o, ld=ld):
                        t.wait_ge(we if o % 2 == 0 else wo, ld)
                        if o >= 2:
                            t.wait_ge(vv, d_ev[o - 2])
                        if o == 0:
                            t.wait_ge(vv, vv_hs)
                        last = None
                        for tt in range(2):
                            t.matmul(psD[o % 2][:, 512 * tt:512 * tt + 512],
                                     nr_sb[:1, :], st_sb[:1, 512 * tt:512 * tt + 512],
                                     start=True, stop=False)
                            for i in range(32):
                                last = t.matmul(psD[o % 2][:, 512 * tt:512 * tt + 512],
                                                abv(wd_o + 4096 * (o % 2) + 128 * i, 128),
                                                abv(h_o + 1024 * i + 512 * tt, 512),
                                                start=False, stop=(i == 31))
                        d_mm[o] = inc(last, "tm")

                    @block.vector
                    def _(v, o=o):
                        v.wait_ge(tm, d_mm[o])
                        d_ev[o] = inc(
                            v.scalar_tensor_tensor(S_sb[:, 1024 * o:1024 * o + 1024],
                                                   psD[o % 2][:, :],
                                                   scl_sb[:, 128 + o:128 + o + 1],
                                                   S_sb[:, 1024 * o:1024 * o + 1024],
                                                   ALU.mult, ALU.add), "vv")

                barrier()
            if upto <= 6 and half == 0:
                break

        if upto <= 6:
            finish_prefix()
            return nc

        if debug and "r2" in dumps:
            @block.sync
            def _(sy):
                sy.wait_ge(vv, cnt["vv"])
                inc(sy.dma_start(dbg["r2"][:, :], S_sb[:, :]), "so", 16)
            barrier()

        ln2_done = emit_ln(write_bf16_to=None, rbf_o=h_o)

        @block.sync
        def _(sy):
            for k in range(KC):
                sy.wait_ge(vv, ln2_done[k])
                inc(sy.dma_start(y_out[:, 1024 * k:1024 * k + 1024],
                                 S_sb[:, 1024 * k:1024 * k + 1024]), "so", 16)
            sy.wait_ge(so, cnt["so"])

    return nc


def _prep_shared(inp):
    Wq = np.asarray(inp["L0_attn_Wq"])
    Wk = np.asarray(inp["L0_attn_Wk"])
    Wv = np.asarray(inp["L0_attn_Wv"])
    Wo = np.asarray(inp["L0_attn_Wo"])
    Wg = np.asarray(inp["L0_ffn_gate_w"])
    Wu = np.asarray(inp["L0_ffn_up_w"])
    Wd = np.asarray(inp["L0_ffn_down_w"])
    gs = np.asarray(inp["L0_ffn_gate_scale"])
    us = np.asarray(inp["L0_ffn_up_scale"])
    ds = np.asarray(inp["L0_ffn_down_scale"])

    def oslab(W, oc):
        return np.ascontiguousarray(
            W.reshape(KC, 128, oc, 128).transpose(2, 1, 0, 3).reshape(oc, 128, KC * 128))

    wq_os = oslab(Wq, OC).astype(np.float32)
    wk_os = oslab(Wk, HKV).astype(np.float32)
    wo_os = oslab(Wo, OC).astype(np.float32)
    wv_ks = np.ascontiguousarray(Wv.reshape(KC, 128, 512)).astype(np.float32)
    wg_is = np.ascontiguousarray(
        Wg.reshape(IC, 128, KC, 128).transpose(0, 3, 2, 1).reshape(IC, 128, KC * 128)).astype(np.int32)
    wu_is = np.ascontiguousarray(
        Wu.reshape(IC, 128, KC, 128).transpose(0, 3, 2, 1).reshape(IC, 128, KC * 128)).astype(np.int32)
    wd_os = np.ascontiguousarray(
        Wd.reshape(OC, 128, 2, 32, 128).transpose(2, 0, 4, 3, 1).reshape(2, OC, 128, 32 * 128)).astype(np.int32)
    scales = np.concatenate([gs.reshape(IC, 128).T, us.reshape(IC, 128).T,
                             ds.reshape(OC, 128).T], axis=1).astype(np.float32)
    d01 = np.eye(128).astype(ml_dtypes.bfloat16)
    ones_c = np.ones((128, 1), np.float32)
    ones_r = np.ones((1, 128), np.float32)
    negrow = np.full((1, 128), -128.0, np.float32)
    ones_cb = np.ones((128, 1), ml_dtypes.bfloat16)
    eps_row = np.full((1, 512), float(D) * EPS, np.float32)
    return dict(negrow=negrow, ones_cb=ones_cb, eps_row=eps_row, wq_os=wq_os, wk_os=wk_os, wv_ks=wv_ks, wo_os=wo_os,
                wg_is=wg_is, wu_is=wu_is, wd_os=wd_os, scales=scales,
                d01=d01, ones_c=ones_c, ones_r=ones_r)


def _mk_masks(part):
    r = np.arange(128)[:, None]
    c = np.arange(640)[None, :]
    band = (c > r) & (c <= r + 512)
    m = np.empty((NQT, 128, 640), np.float32)
    for qt in range(NQT):
        valid = band.copy()
        if part == 0:
            valid = valid & (c >= 512 - 128 * qt)
        m[qt] = np.where(valid, 0.0, NEG)
    return np.ascontiguousarray(m.transpose(1, 0, 2).reshape(128, NQT * 640)).astype(ml_dtypes.bfloat16)


def _prep_core(x, c):
    b, part = c // 4, c % 4
    tok0 = T * part
    xt = np.ascontiguousarray(np.asarray(x)[b].T)
    x_own = np.ascontiguousarray(
        xt[:, tok0:tok0 + T].reshape(KC, 128, T).transpose(1, 0, 2).reshape(128, KC * T)).astype(np.float32)
    win = np.zeros((D, TK), np.float32)
    lo = max(0, tok0 - HALO)
    win[:, HALO - (tok0 - lo):HALO] = xt[:, lo:tok0]
    win[:, HALO:] = xt[:, tok0:tok0 + T]
    x_all = np.ascontiguousarray(
        win.reshape(KC, 128, TK).transpose(1, 0, 2).reshape(128, KC * TK)).astype(np.float32)
    return dict(x_own=x_own, x_all=x_all, masks=_mk_masks(part))


def kernel(**inputs):
    shared = _prep_shared(inputs)
    x = inputs["L0_attn_hidden_states"]
    in_maps = []
    for c in range(NCORES):
        m = dict(shared)
        m.update(_prep_core(x, c))
        in_maps.append(m)
    nc = build_nc(debug=False)
    res = run_bass_kernel_spmd(nc, in_maps, core_ids=list(range(NCORES)))
    out = np.empty((B, S, D), np.float32)
    for c in range(NCORES):
        b, part = c // 4, c % 4
        y = np.asarray(res.results[c]["y"]).reshape(128, KC, T).transpose(1, 0, 2).reshape(D, T)
        out[b, T * part:T * part + T, :] = y.T
    return out
